# version 25
# baseline (speedup 1.0000x reference)
import math
import contextlib
import numpy as np
import ml_dtypes

import concourse.bass as bass
import concourse.mybir as mybir
from concourse.bass_utils import run_bass_kernel_spmd

F32 = mybir.dt.float32
BF16 = mybir.dt.bfloat16
AF = mybir.ActivationFunctionType
AO = mybir.AluOpType
AX = mybir.AxisListType

S = 2048
D = 1024
H = 16
HD = 64
NSEQ = 4
NT = S // 128
NG = 4
I_FFN = 2816
N_CORES = 8

ENGINES = ("sync", "vector", "scalar", "tensor", "gpsimd")


def host_consts():
    pos = np.arange(S, dtype=np.float32)[:, None]
    di = np.arange(0, HD, 2, dtype=np.float32) / HD
    theta = pos * np.exp(-di * math.log(10000.0))
    cos = np.repeat(np.cos(theta), 2, axis=1).astype(np.float32)
    sin = np.repeat(np.sin(theta), 2, axis=1).astype(np.float32)
    sinp = sin.copy()
    sinp[:, :32] *= -1.0
    dmask = np.zeros((4, 128, 512), dtype=np.float32)
    for j in range(4):
        p = np.arange(128)[:, None]
        f = np.arange(512)[None, :]
        dmask[j] = ((128 * j + p) <= f).astype(np.float32)
    return {
        "c_cos": cos, "c_sinp": sinp,
        "c_ident": np.eye(128, dtype=np.float32),
        "c_dmask": dmask.astype(ml_dtypes.bfloat16),
    }


def host_inputs_layout(inp):
    out = {}
    rep = lambda a: np.ascontiguousarray(
        np.broadcast_to(np.asarray(a, np.float32).reshape(1, -1), (128, np.asarray(a).size)))
    perm = np.empty(HD, dtype=np.int64)
    perm[:32] = 2 * np.arange(32) + 1
    perm[32:] = 2 * np.arange(32)
    for nm, g, b in (("q", inp["L0_attn_q_gamma"], inp["L0_attn_q_beta"]),
                     ("k", inp["L0_attn_k_gamma"], inp["L0_attn_k_beta"])):
        g = np.asarray(g, np.float32); b = np.asarray(b, np.float32)
        out[f"g{nm}"] = rep(g)
        out[f"g{nm}p"] = rep(g[perm])
        out[f"b{nm}"] = rep(b)
        out[f"b{nm}p"] = rep(b[perm])
    out["bo_rep"] = rep(inp["L0_attn_bo"])
    return out


class Sem:
    def __init__(self, stack, nc, name):
        self.h = stack.enter_context(nc.semaphore(name))
        self.n = 0


class Prog:
    def __init__(self, nc, stack):
        self.nc = nc
        self.stack = stack
        self.ops = {e: [] for e in ENGINES}

    def sem(self, name):
        return Sem(self.stack, self.nc, name)

    def emit(self, eng, fn, inc=None, inc_by=1, wait=()):
        waits = [(w[0], w[1]) if isinstance(w, tuple) else (w, w.n) for w in wait]
        if inc is not None:
            inc.n += inc_by
        post = inc.n if inc is not None else None

        def run(e, fn=fn, waits=waits, inc=inc, inc_by=inc_by):
            for s_, v_ in waits:
                if v_ > 0:
                    e.wait_ge(s_.h, v_)
            r = fn(e)
            if inc is not None:
                r.then_inc(inc.h, inc_by)
        self.ops[eng].append(run)
        return post


def build_program(taps=(), stop_after=None, PHASEB_SPLIT=False):
    nc = bass.Bass("TRN2", target_bir_lowering=False)
    stack = contextlib.ExitStack()

    def din(name, shape, dtype=F32):
        return nc.dram_tensor(name, list(shape), dtype, kind="ExternalInput").ap()

    xs = din("xs", (NSEQ, S, D))
    Wq = din("Wq", (D, D)); Wk = din("Wk", (D, D)); Wv = din("Wv", (D, D)); Wo = din("Wo", (D, D))
    wg = din("wg", (I_FFN, D)); wu = din("wu", (I_FFN, D)); wd = din("wd", (D, I_FFN))
    c_cos = din("c_cos", (S, HD)); c_sinp = din("c_sinp", (S, HD))
    c_ident = din("c_ident", (128, 128))
    c_dmask = din("c_dmask", (4, 128, 512), BF16)
    gq = din("gq", (128, HD)); gqp = din("gqp", (128, HD)); bq = din("bq", (128, HD)); bqp = din("bqp", (128, HD))
    gk = din("gk", (128, HD)); gkp = din("gkp", (128, HD)); bk = din("bk", (128, HD)); bkp = din("bkp", (128, HD))
    bo_rep = din("bo_rep", (128, D))

    out = nc.dram_tensor("out", [NSEQ, S, D], F32, kind="ExternalOutput").ap()
    xmid = nc.dram_tensor("xmid", [NSEQ, S, D], F32, kind="Internal").ap()
    rs_scr = nc.dram_tensor("rs_scr", [H, 512], F32, kind="Internal").ap()
    nm_scr = nc.dram_tensor("nm_scr", [H], F32, kind="Internal").ap()
    wq_scr = nc.dram_tensor("wq_scr", [2 * 128 * 8 * I_FFN + 128 * 22 * D], BF16, kind="Internal").ap()
    WGT_OFF = 0
    WUT_OFF = 128 * 8 * I_FFN
    WDT_OFF = 2 * 128 * 8 * I_FFN

    taps = set(taps)
    tap_t = {}
    for nm, shp in (("h", [NSEQ, S, D]), ("qr", [NSEQ, S, D]), ("kr", [NSEQ, S, D]),
                    ("attnT", [NSEQ, NG, 1024, 512]), ("x1", [NSEQ, S, D]),
                    ("h2q", [NSEQ, S, D]), ("h1", [NSEQ, S, I_FFN]), ("xb", [NSEQ, S, D])):
        if nm in taps:
            tap_t[nm] = nc.dram_tensor("tap_" + nm, shp, F32, kind="ExternalOutput").ap()

    P = Prog(nc, stack)
    ld = P.sem("ld")
    st = P.sem("st")
    sv = P.sem("sv")
    sa = P.sem("sa")
    sq = P.sem("sq")
    so = P.sem("so")
    sp = P.sem("sp")

    sb = lambda name, shape, dtype: stack.enter_context(nc.sbuf_tensor(name, list(shape), dtype))
    ident_f = sb("ident_f", (128, 128), F32)
    ident_b = sb("ident_b", (128, 128), BF16)
    dmask_b = sb("dmask_b", (128, 4 * 512), BF16)
    gq_s = sb("gq_s", (128, HD), F32); gqp_s = sb("gqp_s", (128, HD), F32)
    bq_s = sb("bq_s", (128, HD), F32); bqp_s = sb("bqp_s", (128, HD), F32)
    gk_s = sb("gk_s", (128, HD), F32); gkp_s = sb("gkp_s", (128, HD), F32)
    bk_s = sb("bk_s", (128, HD), F32); bkp_s = sb("bkp_s", (128, HD), F32)
    bo_s = sb("bo_s", (128, D), F32)
    negm_bc = sb("negm_bc", (128, H), F32)
    negmp = sb("negmp", (128, H // 2), F32)

    def ldc(dst, src):
        P.emit("sync", lambda e, d=dst, s=src: e.dma_start(out=d, in_=s), inc=ld, inc_by=16)
    ldc(ident_f[:], c_ident[:, :])
    ldc(dmask_b[:].rearrange("p (j n) -> p j n", n=512), c_dmask[:, :, :].transpose([1, 0, 2]))
    for d_, s_ in ((gq_s, gq), (gqp_s, gqp), (bq_s, bq), (bqp_s, bqp),
                   (gk_s, gk), (gkp_s, gkp), (bk_s, bk), (bkp_s, bkp), (bo_s, bo_rep)):
        ldc(d_[:], s_[:, :])
    ld_consts = ld.n
    v_identb = P.emit("vector", lambda e: e.tensor_copy(out=ident_b[:], in_=ident_f[:]),
                      inc=sv, wait=[(ld, ld_consts)])

    watt = contextlib.ExitStack()
    WqT = watt.enter_context(nc.sbuf_tensor("WqT", [128, 8, D], BF16))
    WkT = watt.enter_context(nc.sbuf_tensor("WkT", [128, 8, D], BF16))
    WvT = watt.enter_context(nc.sbuf_tensor("WvT", [128, 8, D], BF16))
    WoT = watt.enter_context(nc.sbuf_tensor("WoT", [128, 8, D], BF16))

    p0 = contextlib.ExitStack()
    wstage = p0.enter_context(nc.sbuf_tensor("wstage", [128, 8, D], F32))
    wcast = p0.enter_context(nc.sbuf_tensor("wcast", [128, 8, D], BF16))
    ptw = p0.enter_context(nc.psum_tensor("ptw", [128, 512], BF16))

    for wsrc, wdstT in ((Wq, WqT), (Wk, WkT), (Wv, WvT), (Wo, WoT)):
        l0 = P.emit("sync", lambda e, w=wsrc: e.dma_start(
            out=wstage[:], in_=w[:, :].rearrange("(r p) n -> p r n", p=128)),
            inc=ld, inc_by=16, wait=[(sv, sv.n)])
        cv = P.emit("vector", lambda e: e.tensor_copy(out=wcast[:], in_=wstage[:]),
                    inc=sv, wait=[(ld, l0), (sv, v_identb), (sq, sq.n)])
        for r in range(8):
            for cq in range(2):
                w_first = [(sv, cv), (sv, sv.n)]
                for k in range(4):
                    c = 4 * cq + k
                    P.emit("tensor", lambda e, r=r, c=c, k=k: e.transpose(
                        ptw[:, 128 * k:128 * (k + 1)],
                        wcast[:, r, 128 * c:128 * (c + 1)], ident_b[:]),
                        inc=(sq if k == 3 else None),
                        wait=(w_first if k == 0 else []))
                tq = sq.n
                P.emit("vector", lambda e, w=wdstT, r=r, cq=cq: e.tensor_copy(
                    out=w[:, 4 * cq:4 * cq + 4, 128 * r:128 * (r + 1)],
                    in_=ptw[:].rearrange("p (k n) -> p k n", n=128)),
                    inc=sv, wait=[(sq, tq)])

    wbar = p0.enter_context(nc.sbuf_tensor("wbar", [128, 8, H], F32))
    for wT in (WqT, WkT):
        vr = P.emit("vector", lambda e, wT=wT: e.tensor_reduce(
            out=wbar[:], in_=wT[:].rearrange("p c (h j) -> p c h j", j=64),
            axis=AX.X, op=AO.add), inc=sv, wait=[(sq, sq.n), (sv, sv.n)])
        P.emit("vector", lambda e: e.tensor_scalar_mul(out=wbar[:], in0=wbar[:], scalar1=1.0 / 64),
               inc=sv, wait=[(sv, vr)])
        P.emit("vector", lambda e, wT=wT: e.tensor_tensor(
            out=wT[:].rearrange("p c (h j) -> p c h j", j=64),
            in0=wT[:].rearrange("p c (h j) -> p c h j", j=64),
            in1=wbar[:].unsqueeze(3).broadcast_to([128, 8, H, 64]), op=AO.subtract),
            inc=sv, wait=[(sv, sv.n)])

    p0.close()
    fqs = contextlib.ExitStack()
    f_ins = [fqs.enter_context(nc.sbuf_tensor(f"f_in{pi}", [128, I_FFN], F32)) for pi in range(2)]
    f_qs = [fqs.enter_context(nc.sbuf_tensor(f"f_q{pi}", [128, I_FFN], BF16)) for pi in range(2)]
    f_ax2 = fqs.enter_context(nc.sbuf_tensor("f_ax2", [128, I_FFN], F32))
    f_t2 = fqs.enter_context(nc.sbuf_tensor("f_t2", [128, I_FFN], F32))
    f_y = fqs.enter_context(nc.sbuf_tensor("f_y", [128, I_FFN], F32))
    f_y2 = fqs.enter_context(nc.sbuf_tensor("f_y2", [128, I_FFN], F32))
    f_sg = fqs.enter_context(nc.sbuf_tensor("f_sg", [128, I_FFN], BF16))
    f_m = fqs.enter_context(nc.sbuf_tensor("f_m", [128, I_FFN], BF16))
    f_bm = fqs.enter_context(nc.sbuf_tensor("f_bm", [128, I_FFN // 16], F32))
    f_sc = fqs.enter_context(nc.sbuf_tensor("f_sc", [128, I_FFN // 16], F32))
    f_rs = fqs.enter_context(nc.sbuf_tensor("f_rs", [128, I_FFN // 16], F32))
    f_s2 = fqs.enter_context(nc.sbuf_tensor("f_s2", [128, I_FFN // 16], BF16))
    f_scr = (f_ax2, f_t2, f_y, f_y2, f_sg, f_m, f_bm, f_sc, f_rs, f_s2)
    f_ev = fqs.enter_context(nc.sbuf_tensor("f_ev", [128, 512], BF16))
    ptf = fqs.enter_context(nc.psum_tensor("ptf", [128, 512], BF16))

    def emit_staircase(eng, sem, t2_ap, y_ap, y2_ap, n, prot, wait_first=()):
        i1 = y2_ap.bitcast(mybir.dt.int32)
        pw = (lambda: [(sem, sem.n)]) if prot else (lambda: [])
        P.emit(eng, lambda e: e.tensor_copy(out=i1, in_=t2_ap), inc=sem, wait=pw() + list(wait_first))
        P.emit(eng, lambda e: e.tensor_scalar(out=y_ap, in0=i1, scalar1=4.0, scalar2=0.5,
                                              op0=AO.min, op1=AO.mult), inc=sem, wait=pw() or [(sem, sem.n)])
        P.emit(eng, lambda e: e.tensor_scalar(out=i1, in0=t2_ap, scalar1=0.5, scalar2=-2.0,
                                              op0=AO.mult, op1=AO.add), inc=sem, wait=pw() or [(sem, sem.n)])
        P.emit(eng, lambda e: e.tensor_scalar(out=y2_ap, in0=i1, scalar1=0.0, scalar2=2.0,
                                              op0=AO.max, op1=AO.min), inc=sem, wait=pw() or [(sem, sem.n)])
        P.emit(eng, lambda e: e.tensor_tensor(out=y_ap, in0=y_ap, in1=y2_ap, op=AO.add),
               inc=sem, wait=pw() or [(sem, sem.n)])
        P.emit(eng, lambda e: e.tensor_scalar(out=i1, in0=t2_ap, scalar1=0.5, scalar2=-4.5,
                                              op0=AO.mult, op1=AO.add), inc=sem, wait=pw() or [(sem, sem.n)])
        P.emit(eng, lambda e: e.tensor_scalar(out=y2_ap, in0=i1, scalar1=0.0, scalar2=1.0,
                                              op0=AO.max, op1=AO.min), inc=sem, wait=pw() or [(sem, sem.n)])
        P.emit(eng, lambda e: e.scalar_tensor_tensor(out=y_ap, in0=y2_ap, scalar=2.0, in1=y_ap,
                                                     op0=AO.mult, op1=AO.add), inc=sem, wait=pw() or [(sem, sem.n)])

    def emit_quant(scr, src_ap, n, dst_flat_bf, sign_src, wv, wa, w_final=(), split=True, scratch_guard=None, rscale_act=True):
        ax2, t2_, y_, y2_, sg_, m_, bm_, sc_, rs_, s2_ = scr
        nb = n // 16
        peng, psem = ("gpsimd", sp) if split else ("vector", sv)
        gw = ([scratch_guard] if scratch_guard else [(sp, sp.n)]) if split else ([scratch_guard] if scratch_guard else [])
        a0 = P.emit("scalar", lambda e: e.activation(out=ax2[:, :n], in_=src_ap, func=AF.Abs, scale=2.0),
                    inc=sa, wait=list(wa) + gw)
        P.emit("vector", lambda e: e.tensor_reduce(
            out=bm_[:, :nb], in_=src_ap.rearrange("p (b k) -> p b k", k=16),
            axis=AX.X, op=AO.max, apply_absolute_value=True), inc=sv, wait=list(wv) + gw)
        P.emit("vector", lambda e: e.tensor_scalar(out=sc_[:, :nb], in0=bm_[:, :nb],
                                                   scalar1=1.0 / 6.0, scalar2=1e-12,
                                                   op0=AO.mult, op1=AO.max), inc=sv, wait=[(sv, sv.n)] + gw)
        sc_done = sv.n
        if rscale_act:
            aL = P.emit("scalar", lambda e: e.activation(out=rs_[:, :nb], in_=sc_[:, :nb], func=AF.Ln),
                        inc=sa, wait=[(sv, sc_done)] + gw)
            aE = P.emit("scalar", lambda e: e.activation(out=rs_[:, :nb], in_=rs_[:, :nb], func=AF.Exp, scale=-1.0),
                        inc=sa, wait=[(sa, aL)])
            rs_ready = (sa, aE)
        else:
            vre = P.emit("vector", lambda e: e.reciprocal(out=rs_[:, :nb], in_=sc_[:, :nb]),
                         inc=sv, wait=[(sv, sc_done)] + gw)
            rs_ready = (sv, vre)
            aE = sa.n
        P.emit(peng, lambda e: e.tensor_scalar_mul(out=s2_[:, :nb], in0=sc_[:, :nb], scalar1=2.0),
               inc=psem, wait=[(sv, sc_done)] + ([(psem, psem.n)] if True else []))
        t2p = P.emit(peng, lambda e: e.tensor_tensor(
            out=t2_.rearrange("p (b k) -> p b k", k=16)[:, 0:nb, :],
            in0=ax2.rearrange("p (b k) -> p b k", k=16)[:, 0:nb, :],
            in1=rs_[:, 0:nb].unsqueeze(2).broadcast_to([128, nb, 16]), op=AO.mult),
            inc=psem, wait=[rs_ready] + ([(psem, psem.n)] if split else []))
        P.emit(peng, lambda e: e.tensor_scalar(out=sg_[:, :n], in0=sign_src[:, :n], scalar1=0.0, scalar2=0.5,
                                               op0=AO.is_gt, op1=AO.subtract),
               inc=psem, wait=(list(wa) if split else []))
        sgp = psem.n
        emit_staircase("vector", sv, t2_[:, :n], y_[:, :n], y2_[:, :n], n,
                       prot=(n < 512), wait_first=([(psem, t2p)] if split else []) + gw)
        y_done = sv.n
        P.emit(peng, lambda e: e.tensor_tensor(out=m_[:, :n], in0=y_[:, :n], in1=sg_[:, :n], op=AO.mult),
               inc=psem, wait=([(sv, y_done), (psem, sgp)] if split else ([(sv, sv.n)] if n < 512 else [])))
        vqp = P.emit(peng, lambda e: e.tensor_tensor(
            out=dst_flat_bf.rearrange("p (b k) -> p b k", k=16)[:, 0:nb, :],
            in0=m_.rearrange("p (b k) -> p b k", k=16)[:, 0:nb, :],
            in1=s2_[:, 0:nb].unsqueeze(2).broadcast_to([128, nb, 16]), op=AO.mult),
            inc=psem, wait=list(w_final) + ([(psem, psem.n)] if split else ([(sv, sv.n)] if n < 512 else [])))
        return (None, (psem, vqp))

    f_scr = (f_ax2, f_t2, f_y, f_y2, f_sg, f_m, f_bm, f_sc, f_rs, f_s2)

    F_GUARD = [(sq, sq.n), (sq, sq.n)]
    ABS_GUARD = [(sv, sv.n)]
    SCR_GUARD = [(sa, sa.n)]

    def quant_weight(wsrc, n_rows, n_cols, dst_off, tchunks, chain_idx):
        for r in range(n_rows // 128):
            pi = chain_idx[0] % 2
            chain_idx[0] += 1
            f_in = f_ins[pi]
            f_q = f_qs[pi]
            gwait = [F_GUARD[pi], (sv, sv.n if chain_idx[0] <= 2 else 0)] if chain_idx[0] <= 2 else [F_GUARD[pi]]
            l0 = P.emit("sync", lambda e, r=r, f_in=f_in: e.dma_start(
                out=f_in[:, :n_cols], in_=wsrc[128 * r:128 * (r + 1), :]),
                inc=ld, inc_by=16, wait=gwait)
            _, vqp = emit_quant(f_scr, f_in[:, :n_cols], n_cols, f_q[:, :n_cols], f_in,
                                wv=[(ld, l0)], wa=([(ld, l0)] + [ABS_GUARD[0]] if ABS_GUARD[0] else [(ld, l0)]),
                                split=False, scratch_guard=SCR_GUARD[0])
            ABS_GUARD[0] = (sv, sv.n)
            SCR_GUARD[0] = (sa, sa.n)
            ncol_ch = n_cols // 128
            for cq in range((ncol_ch + 3) // 4):
                ks = [k for k in range(4) if 4 * cq + k < ncol_ch]
                for k in ks:
                    c = 4 * cq + k
                    P.emit("tensor", lambda e, c=c, k=k, f_q=f_q: e.transpose(
                        ptf[:, 128 * k:128 * (k + 1)],
                        f_q[:, 128 * c:128 * (c + 1)], ident_b[:]),
                        inc=(sq if k == ks[-1] else None),
                        wait=(([(sv, sv.n), vqp]) if k == 0 else []))
                tq = sq.n
                ev = P.emit("vector", lambda e, nk=len(ks): e.tensor_copy(
                    out=f_ev[:, :128 * nk], in_=ptf[:, :128 * nk]),
                    inc=sv, wait=[(sq, tq), (st, st.n)])

                def store(e, cq=cq, nk=len(ks), r=r, dst_off=dst_off, tchunks=tchunks, nrows=n_rows):
                    dst = wq_scr[dst_off:dst_off + 128 * tchunks * nrows].rearrange(
                        "(p c n) -> p c n", p=128, c=tchunks)
                    return e.dma_start(out=dst[:, 4 * cq:4 * cq + nk, 128 * r:128 * (r + 1)],
                                       in_=f_ev[:, :128 * nk].rearrange("p (k n) -> p k n", n=128))
                P.emit("gpsimd", store, inc=st, inc_by=16, wait=[(sv, ev)])
            F_GUARD[pi] = (sq, sq.n)

    _ci = [0]
    quant_weight(wg, I_FFN, D, WGT_OFF, 8, _ci)
    quant_weight(wu, I_FFN, D, WUT_OFF, 8, _ci)
    quant_weight(wd, D, I_FFN, WDT_OFF, 22, _ci)
    fqs.close()
    barrier0 = [(sv, sv.n), (sq, sq.n), (st, st.n), (ld, ld.n), (sa, sa.n)]

    pa = contextlib.ExitStack()
    psb = lambda name, shape, dtype: pa.enter_context(nc.sbuf_tensor(name, list(shape), dtype))
    kT_all = psb("kT_all", (128, 8, S), BF16)
    Vp = psb("Vp", (128, NT, H * 65), BF16)
    qT_g = psb("qT_g", (128, 8, 512), BF16)
    hT_g = psb("hT_g", (128, 8, 512), BF16)
    attnT_g = psb("attnT_g", (128, 8, 512), BF16)
    x_t = psb("x_t", (128, D), F32)
    h_bf = psb("h_bf", (128, D), BF16)
    h_bf4 = [h_bf] + [psb(f"h_bf{i}", (128, D), BF16) for i in (1, 2)] + [h_bf]
    qkp_bf = psb("qkp_bf", (128, D), BF16)
    z2 = psb("z2", (128, D), F32)
    zu = psb("zu", (128, D), F32)
    zv = psb("zv", (128, D), F32)
    zqc0 = psb("zqc0", (128, D), BF16); zkc0 = psb("zkc0", (128, D), BF16)
    zqc1 = psb("zqc1", (128, D), BF16); zkc1 = psb("zkc1", (128, D), BF16)
    cos_t = psb("cos_t", (128, HD), F32)
    sinp_t = psb("sinp_t", (128, HD), F32)
    A_t = psb("A_t", (128, HD), F32); B_t = psb("B_t", (128, HD), F32)
    D_t = psb("D_t", (128, HD), F32); e1_t = psb("e1_t", (128, HD), F32)
    PP0 = psb("PP0", (128, 1024), BF16)
    PP1 = psb("PP1", (128, 1024), BF16)
    rb_s = psb("rb_s", (128, 512), F32)
    rs2_s = psb("rs2_s", (1, 1024), F32)
    rsA = rs2_s[:, 0:512]
    rsB = rs2_s[:, 512:1024]
    st16 = psb("st16", (128, 16), F32)
    st16b = psb("st16b", (128, 16), F32)
    mu_h = psb("mu_h", (128, 16), F32)
    rstd_h = psb("rstd_h", (128, 16), F32)
    qm2_a = psb("qm2_a", (128, 16), F32); qm2_b = psb("qm2_b", (128, 16), F32)
    km2_a = psb("km2_a", (128, 16), F32); km2_b = psb("km2_b", (128, 16), F32)
    s1_c = psb("s1_c", (128, 1), F32)
    s2_c = psb("s2_c", (128, 1), F32)
    mu_c = psb("mu_c", (128, 1), F32)
    rstd_c = psb("rstd_c", (128, 1), F32)
    nm16 = psb("nm16", (16, 1), F32)
    nm16b = psb("nm16b", (16, 1), F32)

    pA_q = pa.enter_context(nc.psum_tensor("pA_q", [128, 1024], F32))
    pA_k = pa.enter_context(nc.psum_tensor("pA_k", [128, 1024], F32))
    pA_v = pa.enter_context(nc.psum_tensor("pA_v", [128, 1024], F32))
    pA_t = pa.enter_context(nc.psum_tensor("pA_t", [128, 1024], BF16))
    pA_o = pa.enter_context(nc.psum_tensor("pA_o", [65, 512], F32))
    pA_s = pA_t[:].bitcast(F32)[0:16, 0:128]
    pA_oB = pA_t[:].bitcast(F32)[0:65, 0:512]

    for eng in ENGINES:
        P.emit(eng, lambda e: e.nop(), wait=barrier0)

    v_ones = P.emit("vector", lambda e: e.memset(
        Vp[:].rearrange("p t (h j) -> p t h j", j=65)[:, :, :, 64:65], 1.0), inc=sv)

    NM_WAIT = {}

    def emit_ln_stats(src_f32, n, l_wait):
        P.emit("vector", lambda e: e.reduce_sum(out=s1_c[:], in_=src_f32, axis=AX.X),
               inc=sv, wait=[(ld, l_wait)])
        a0 = P.emit("scalar", lambda e, n=n: e.activation(out=zv[:, :n], in_=src_f32, func=AF.Square),
                    inc=sa, wait=[(ld, l_wait), (sv, sv.n)])
        P.emit("vector", lambda e, n=n: e.reduce_sum(out=s2_c[:], in_=zv[:, :n], axis=AX.X),
               inc=sv, wait=[(sa, a0)])
        P.emit("vector", lambda e, n=n: e.tensor_scalar_mul(out=mu_c[:], in0=s1_c[:], scalar1=1.0 / n),
               inc=sv, wait=[(sv, sv.n)])
        P.emit("vector", lambda e, n=n: e.tensor_scalar_mul(out=s2_c[:], in0=s2_c[:], scalar1=1.0 / n),
               inc=sv, wait=[(sv, sv.n)])
        P.emit("vector", lambda e: e.scalar_tensor_tensor(
            out=s1_c[:], in0=mu_c[:], scalar=mu_c[:, 0:1], in1=s2_c[:],
            op0=AO.mult, op1=AO.subtract), inc=sv, wait=[(sv, sv.n)])
        P.emit("vector", lambda e: e.tensor_scalar(out=s1_c[:], in0=s1_c[:], scalar1=-1.0, scalar2=1e-5,
                                                   op0=AO.mult, op1=AO.add), inc=sv, wait=[(sv, sv.n)])
        vpe = sv.n
        aln = P.emit("scalar", lambda e: e.activation(out=s2_c[:], in_=s1_c[:], func=AF.Ln),
                     inc=sa, wait=[(sv, vpe)])
        va = P.emit("scalar", lambda e: e.activation(out=rstd_c[:], in_=s2_c[:], func=AF.Exp, scale=-0.5),
                    inc=sa, wait=[(sa, aln)])
        return va, sv.n

    def phase_a_seq(s):
        km_cnt = 0
        Z_GUARD = [None, None]
        RB_GUARD = [(sv, 0)]
        FD_DONE = set()

        def front_dve(t, slot):
            FD_DONE.add(t)
            xbuf = x_t if slot % 2 == 0 else zu
            l_x = P.emit("sync", lambda e, t=t, xbuf=xbuf: e.dma_start(
                out=xbuf[:], in_=xs[s, 128 * t:128 * (t + 1), :]),
                inc=ld, inc_by=16, wait=[(sa, sa.n), (sv, sv.n), (st, st.n)])
            va, vb = emit_ln_stats(xbuf[:], D, l_x)
            ah = P.emit("vector", lambda e, slot=slot, xbuf=xbuf: e.tensor_scalar(
                out=h_bf4[slot][:], in0=xbuf[:], scalar1=mu_c[:, 0:1], scalar2=rstd_c[:, 0:1],
                op0=AO.subtract, op1=AO.mult),
                inc=sv, wait=[(sa, va), (sv, sv.n), (sq, sq.n), (st, st.n)])
            if "h" in taps:
                P.emit("gpsimd", lambda e, s=s, t=t, slot=slot: e.dma_start(
                    out=tap_t["h"][s, 128 * t:128 * (t + 1), :], in_=h_bf4[slot][:]),
                    inc=st, inc_by=16, wait=[(sv, ah)])

        for g in range(NG):
            qm_cnt = 0

            def frontend(t, tt):
                if t not in FD_DONE:
                    front_dve(t, tt)
                for cq in range(2):
                    for k in range(4):
                        c = 4 * cq + k
                        P.emit("tensor", lambda e, c=c, k=k, tt=tt: e.transpose(
                            pA_t[:, 128 * k:128 * (k + 1)],
                            h_bf4[tt][:, 128 * c:128 * (c + 1)], ident_b[:]),
                            inc=(sq if k == 3 else None),
                            wait=([(sv, sv.n)] if k == 0 else []))
                    tq = sq.n
                    P.emit("vector", lambda e, cq=cq, tt=tt: e.tensor_copy(
                        out=hT_g[:, 4 * cq:4 * cq + 4, 128 * tt:128 * (tt + 1)],
                        in_=pA_t[:, 0:512].rearrange("p (k n) -> p k n", n=128)),
                        inc=sv, wait=[(sq, tq)])
                v_hT = sv.n
                first_w = [(sv, v_hT), (sa, sa.n), (sq, sq.n)]
                for (pdst, wT) in ((pA_q, WqT), (pA_k, WkT), (pA_v, WvT)):
                    for o2 in range(2):
                        for c in range(8):
                            P.emit("tensor", lambda e, pdst=pdst, wT=wT, o2=o2, c=c, tt=tt: e.matmul(
                                pdst[:, 512 * o2:512 * (o2 + 1)],
                                hT_g[:, c, 128 * tt:128 * (tt + 1)],
                                wT[:, c, 512 * o2:512 * (o2 + 1)],
                                start=(c == 0), stop=(c == 7)),
                                inc=(sq if c == 7 else None),
                                wait=(first_w if (c == 0 and first_w) else []))
                            first_w = []
                q_all = sq.n
                P.emit("vector", lambda e, t=t: e.tensor_copy(
                    out=Vp[:, t, :].rearrange("p (h j) -> p h j", j=65)[:, :, 0:64],
                    in_=pA_v[:].rearrange("p (h j) -> p h j", j=64)),
                    inc=sv, wait=[(sq, q_all), (sv, v_ones)])
                return q_all

            def emit_evacs(q_all, par, guard):
                zq_ = (zqc0, zqc1)[par]; zk_ = (zkc0, zkc1)[par]
                w0 = [(sq, q_all)] + ([guard] if guard else [])
                eq = P.emit("scalar", lambda e, zq_=zq_: e.activation(out=zq_[:], in_=pA_q[:, :], func=AF.Copy),
                            inc=sa, wait=w0)
                ek = P.emit("scalar", lambda e, zk_=zk_: e.activation(out=zk_[:], in_=pA_k[:, :], func=AF.Copy),
                            inc=sa)
                return ek

            def backend(t, tt, ek, par):
                nonlocal qm_cnt, km_cnt
                zqc = (zqc0, zqc1)[par]; zkc = (zkc0, zkc1)[par]
                P.emit("sync", lambda e, t=t: e.dma_start(
                    out=cos_t[:], in_=c_cos[128 * t:128 * (t + 1), :]), inc=ld, inc_by=16,
                    wait=[(sv, sv.n)])
                l_cs = P.emit("sync", lambda e, t=t: e.dma_start(
                    out=sinp_t[:], in_=c_sinp[128 * t:128 * (t + 1), :]), inc=ld, inc_by=16)
                for which, pcs, g_s_, gp_s_, b_s_, bp_s_, dst_bf in (
                        ("q", zqc, gq_s, gqp_s, bq_s, bqp_s, qkp_bf),
                        ("k", zkc, gk_s, gkp_s, bk_s, bkp_s, qkp_bf)):
                    pcv = pcs[:].rearrange("p (h j) -> p h j", j=64)
                    a1 = P.emit("scalar", lambda e, pcs=pcs: e.activation(
                        out=zu[:], in_=pcs[:], func=AF.Square), inc=sa, wait=[(sa, ek), (sv, sv.n), (st, st.n)])
                    P.emit("vector", lambda e: e.reduce_sum(
                        out=st16[:], in_=zu[:].rearrange("p (h j) -> p h j", j=64), axis=AX.X),
                        inc=sv, wait=[(sa, a1), (sv, sv.n)])
                    P.emit("vector", lambda e: e.tensor_scalar(out=st16[:], in0=st16[:], scalar1=1.0 / 64, scalar2=1e-5,
                                                               op0=AO.mult, op1=AO.add),
                           inc=sv, wait=[(sv, sv.n)])
                    aln2 = P.emit("scalar", lambda e: e.activation(out=st16b[:], in_=st16[:], func=AF.Ln),
                                  inc=sa, wait=[(sv, sv.n)])
                    a2 = P.emit("scalar", lambda e: e.activation(out=rstd_h[:], in_=st16b[:], func=AF.Exp, scale=-0.5),
                                inc=sa, wait=[(sa, aln2)])
                    P.emit("vector", lambda e, gg=g_s_: e.tensor_tensor(out=A_t[:], in0=cos_t[:], in1=gg[:], op=AO.mult),
                           inc=sv, wait=[(ld, l_cs)])
                    P.emit("vector", lambda e, gp=gp_s_: e.tensor_tensor(out=B_t[:], in0=sinp_t[:], in1=gp[:], op=AO.mult), inc=sv)
                    P.emit("vector", lambda e, bb=b_s_: e.tensor_tensor(out=e1_t[:], in0=cos_t[:], in1=bb[:], op=AO.mult), inc=sv)
                    P.emit("vector", lambda e, bp=bp_s_: e.tensor_tensor(out=D_t[:], in0=sinp_t[:], in1=bp[:], op=AO.mult), inc=sv)
                    P.emit("vector", lambda e: e.tensor_tensor(out=D_t[:], in0=D_t[:], in1=e1_t[:], op=AO.add),
                           inc=sv, wait=[(sv, sv.n)])
                    hb = lambda a: a.unsqueeze(2).broadcast_to([128, 16, 64])
                    jb = lambda a: a.unsqueeze(1).broadcast_to([128, 16, 64])
                    jb32 = lambda a: a.unsqueeze(1).broadcast_to([128, 16, 32])
                    z2v = z2[:].rearrange("p (h j) -> p h j", j=64)
                    zuv = zu[:].rearrange("p (h j) -> p h j", j=64)
                    zvv = zv[:].rearrange("p (h j) -> p h j", j=64)
                    P.emit("vector", lambda e, pcv=pcv, z2v=z2v: e.tensor_tensor(
                        out=z2v, in0=pcv, in1=hb(rstd_h[:]), op=AO.mult), inc=sv, wait=[(sa, a2), (sv, sv.n)])
                    P.emit("vector", lambda e, z2v=z2v, zuv=zuv: e.tensor_tensor(
                        out=zuv, in0=z2v, in1=jb(A_t[:]), op=AO.mult), inc=sv)
                    P.emit("vector", lambda e, z2v=z2v, zvv=zvv: e.tensor_tensor(
                        out=zvv[:, :, 0:32], in0=z2v[:, :, 1::2], in1=jb32(B_t[:, 0:32]), op=AO.mult), inc=sv)
                    P.emit("vector", lambda e, z2v=z2v, zvv=zvv: e.tensor_tensor(
                        out=zvv[:, :, 32:64], in0=z2v[:, :, 0::2], in1=jb32(B_t[:, 32:64]), op=AO.mult), inc=sv)
                    P.emit("vector", lambda e: e.tensor_tensor(out=zu[:], in0=zu[:], in1=zv[:], op=AO.add), inc=sv)
                    vq = P.emit("vector", lambda e, dst=dst_bf, zuv=zuv: e.tensor_tensor(
                        out=dst[:].rearrange("p (h j) -> p h j", j=64), in0=zuv, in1=jb(D_t[:]), op=AO.add),
                        inc=sv, wait=[(sq, sq.n), (st, st.n), (sa, sa.n)])
                    a3 = P.emit("scalar", lambda e, dst=dst_bf: e.activation(
                        out=zu[:], in_=dst[:], func=AF.Square), inc=sa, wait=[(sv, vq)])
                    P.emit("vector", lambda e: e.reduce_sum(
                        out=st16[:], in_=zu[:].rearrange("p (h j) -> p h j", j=64), axis=AX.X),
                        inc=sv, wait=[(sa, a3), (sv, sv.n)])
                    if which == "q":
                        srcb, dstb = (qm2_a, qm2_b) if qm_cnt % 2 else (qm2_b, qm2_a)
                        if qm_cnt == 0:
                            P.emit("vector", lambda e, d=dstb: e.tensor_copy(out=d[:], in_=st16[:]),
                                   inc=sv, wait=[(sv, sv.n)])
                        else:
                            P.emit("vector", lambda e, s_=srcb, d=dstb: e.tensor_tensor(
                                out=d[:], in0=s_[:], in1=st16[:], op=AO.max), inc=sv, wait=[(sv, sv.n)])
                        qm_cnt += 1
                    else:
                        srcb, dstb = (km2_a, km2_b) if km_cnt % 2 else (km2_b, km2_a)
                        if km_cnt == 0:
                            P.emit("vector", lambda e, d=dstb: e.tensor_copy(out=d[:], in_=st16[:]),
                                   inc=sv, wait=[(sv, sv.n)])
                        else:
                            P.emit("vector", lambda e, s_=srcb, d=dstb: e.tensor_tensor(
                                out=d[:], in0=s_[:], in1=st16[:], op=AO.max), inc=sv, wait=[(sv, sv.n)])
                        km_cnt += 1
                    if (which == "q" and "qr" in taps) or (which == "k" and "kr" in taps):
                        nm = "qr" if which == "q" else "kr"
                        P.emit("gpsimd", lambda e, s=s, t=t, dst=dst_bf, nm=nm: e.dma_start(
                            out=tap_t[nm][s, 128 * t:128 * (t + 1), :], in_=dst[:]),
                            inc=st, inc_by=16, wait=[(sv, vq)])
                    tdst = qT_g if which == "q" else kT_all
                    toff = 128 * tt if which == "q" else 128 * t
                    for cq in range(2):
                        for k in range(4):
                            c = 4 * cq + k
                            P.emit("tensor", lambda e, c=c, k=k, dst=dst_bf: e.transpose(
                                pA_t[:, 128 * k:128 * (k + 1)],
                                dst[:, 128 * c:128 * (c + 1)], ident_b[:]),
                                inc=(sq if k == 3 else None),
                                wait=([(sv, sv.n)] if k == 0 else []))
                        tq2 = sq.n
                        P.emit("vector", lambda e, cq=cq, td=tdst, to=toff: e.tensor_copy(
                            out=td[:, 4 * cq:4 * cq + 4, to:to + 128],
                            in_=pA_t[:, 0:512].rearrange("p (k n) -> p k n", n=128)),
                            inc=sv, wait=[(sq, tq2)])

            pend = None
            for tt in range(4):
                t = 4 * g + tt
                q_all_t = frontend(t, tt)
                if pend is not None:
                    backend(pend[0], pend[1], pend[2], pend[3])
                    Z_GUARD[pend[3]] = (sv, sv.n)
                ek_t = emit_evacs(q_all_t, tt % 2, Z_GUARD[tt % 2])
                pend = (t, tt, ek_t, tt % 2)
            backend(pend[0], pend[1], pend[2], pend[3])
            Z_GUARD[pend[3]] = (sv, sv.n)

            qmb = qm2_b if ((qm_cnt - 1) % 2 == 1) else qm2_a
            kmb = km2_b if ((km_cnt - 1) % 2 == 1) else km2_a
            tqs = P.emit("tensor", lambda e, qmb=qmb: e.transpose(
                pA_s, qmb[:], ident_f[:]), inc=sq, wait=[(sv, sv.n)])
            P.emit("vector", lambda e: e.reduce_max(out=nm16[:], in_=pA_s, axis=AX.X),
                   inc=sv, wait=[(sq, tqs), (sv, sv.n)])
            tks = P.emit("tensor", lambda e, kmb=kmb: e.transpose(
                pA_s, kmb[:], ident_f[:]), inc=sq, wait=[(sv, sv.n)])
            P.emit("vector", lambda e: e.reduce_max(out=nm16b[:], in_=pA_s, axis=AX.X),
                   inc=sv, wait=[(sq, tks), (sv, sv.n)])
            vmm = P.emit("vector", lambda e: e.tensor_tensor(out=nm16[:], in0=nm16[:], in1=nm16b[:], op=AO.mult),
                         inc=sv, wait=[(sv, sv.n)])
            aM0 = P.emit("scalar", lambda e: e.activation(out=nm16b[:], in_=nm16[:], func=AF.Ln),
                         inc=sa, wait=[(sv, vmm)])
            aM = P.emit("scalar", lambda e: e.activation(out=nm16b[:], in_=nm16b[:], func=AF.Exp, scale=0.5),
                        inc=sa, wait=[(sa, aM0)])
            vM = P.emit("vector", lambda e: e.tensor_scalar_mul(out=nm16[:], in0=nm16b[:], scalar1=-0.125),
                        inc=sv, wait=[(sa, aM)])
            sM = P.emit("gpsimd", lambda e: e.dma_start(
                out=nm_scr[:].rearrange("(h o) -> h o", o=1), in_=nm16[:]),
                inc=st, inc_by=16, wait=[(sv, vM)])
            lM = P.emit("sync", lambda e: e.dma_start(
                out=negm_bc[:], in_=nm_scr[:].unsqueeze(0).broadcast_to([128, H])),
                inc=ld, inc_by=16, wait=[(st, sM), (sa, sa.n)])
            vnp = P.emit("vector", lambda e: e.tensor_tensor(
                out=negmp[:], in0=negm_bc[:].rearrange("p (h two) -> p h two", two=2)[:, :, 0],
                in1=negm_bc[:].rearrange("p (h two) -> p h two", two=2)[:, :, 1], op=AO.min),
                inc=sv, wait=[(ld, lM), (sv, sv.n)])
            NM_WAIT[(s, g)] = (lM, vnp)

            nkc = 4 * (g + 1)
            last_exp_parity = [0, 0]
            last_norm_pv = [sv.n, sv.n]
            chunk_list = [(hp, kc) for hp in range(8) for kc in range(nkc)]

            def emit_scores(idx, extra_wait=()):
                hp, kc = chunk_list[idx]
                buf = kc % 2
                sPair = (pA_q, pA_k)[buf]
                w0 = [(sa, last_exp_parity[buf])] + list(extra_wait)
                P.emit("tensor", lambda e, sPair=sPair, hp=hp, kc=kc: e.matmul(
                    sPair[:, 0:512], kT_all[0:64, hp, 128 * kc:128 * (kc + 1)],
                    qT_g[0:64, hp, :], start=True, stop=True, tile_position=(0, 0)),
                    inc=sq, wait=w0)
                P.emit("tensor", lambda e, sPair=sPair, hp=hp, kc=kc: e.matmul(
                    sPair[:, 512:1024], kT_all[64:128, hp, 128 * kc:128 * (kc + 1)],
                    qT_g[64:128, hp, :], start=True, stop=True, tile_position=(64, 0)),
                    inc=sq)
                return sq.n

            PEND_TAIL = []

            def emit_pair_tail(hp, pvset, pvtA, pvtB, pv_q):
                hA = 2 * hp
                if pvset == 0:
                    aR1 = P.emit("scalar", lambda e: e.activation(out=rs2_s[:], in_=pA_v[64:65, 0:1024], func=AF.Ln),
                                 inc=sa, wait=[(sq, pv_q), (st, st.n)])
                    aR4 = P.emit("scalar", lambda e: e.activation(out=rs2_s[:], in_=rs2_s[:], func=AF.Exp, scale=-1.0),
                                 inc=sa, wait=[(sa, aR1)])
                    s2_ = P.emit("gpsimd", lambda e, hA=hA: e.dma_start(
                        out=rs_scr[hA:hA + 2, :].rearrange("a n -> (a n)").unsqueeze(0), in_=rs2_s[:]),
                        inc=st, inc_by=16, wait=[(sa, aR4)])
                else:
                    aR1 = P.emit("scalar", lambda e, pvtA=pvtA: e.activation(out=rsA, in_=pvtA[64:65, :], func=AF.Ln),
                                 inc=sa, wait=[(sq, pv_q), (st, st.n)])
                    aR2 = P.emit("scalar", lambda e: e.activation(out=rsA, in_=rsA, func=AF.Exp, scale=-1.0),
                                 inc=sa, wait=[(sa, aR1)])
                    aR3 = P.emit("scalar", lambda e, pvtB=pvtB: e.activation(out=rsB, in_=pvtB[64:65, :], func=AF.Ln),
                                 inc=sa, wait=[(sa, aR2)])
                    aR4 = P.emit("scalar", lambda e: e.activation(out=rsB, in_=rsB, func=AF.Exp, scale=-1.0),
                                 inc=sa, wait=[(sa, aR3)])
                    P.emit("gpsimd", lambda e, hA=hA: e.dma_start(out=rs_scr[hA:hA + 1, :], in_=rsA),
                           inc=st, inc_by=16, wait=[(sa, aR4)])
                    s2_ = P.emit("gpsimd", lambda e, hA=hA: e.dma_start(out=rs_scr[hA + 1:hA + 2, :], in_=rsB),
                                 inc=st, inc_by=16)
                lrb = P.emit("sync", lambda e, hA=hA: e.dma_start(
                    out=rb_s[:], in_=rs_scr[hA:hA + 2, :].unsqueeze(1).broadcast_to([2, 64, 512])),
                    inc=ld, inc_by=16, wait=[(st, s2_), RB_GUARD[0]])
                P.emit("vector", lambda e, hp=hp, pvtA=pvtA: e.tensor_tensor(
                    out=attnT_g[0:64, hp, :], in0=pvtA[0:64, :], in1=rb_s[0:64, :], op=AO.mult),
                    inc=sv, wait=[(ld, lrb), (sq, sq.n)])
                vn = P.emit("vector", lambda e, hp=hp, pvtB=pvtB: e.tensor_tensor(
                    out=attnT_g[64:128, hp, :], in0=pvtB[0:64, :], in1=rb_s[64:128, :], op=AO.mult),
                    inc=sv)
                RB_GUARD[0] = (sv, vn)
                last_norm_pv[pvset] = vn
                if "attnT" in taps:
                    vcp = P.emit("vector", lambda e, hp=hp: e.tensor_copy(out=zu[:, 0:512], in_=attnT_g[:, hp, :]),
                                 inc=sv, wait=[(sv, vn)])
                    P.emit("gpsimd", lambda e, s=s, g=g, hp=hp: e.dma_start(
                        out=tap_t["attnT"][s, g, 128 * hp:128 * (hp + 1), :], in_=zu[:, 0:512]),
                        inc=st, inc_by=16, wait=[(sv, vcp)])

            FD_HOOKS = []
            if g + 1 < NG:
                L = len(chunk_list)
                for i in range(3):
                    FD_HOOKS.append((max(1, (L * (i + 1)) // 5), 4 * (g + 1) + i, i))

            mq_pend = emit_scores(0, extra_wait=[(sv, sv.n), (sq, sq.n), (sa, sa.n)])
            for idx, (hp, kc) in enumerate(chunk_list):
                while FD_HOOKS and idx == FD_HOOKS[0][0]:
                    _, t2, sl = FD_HOOKS.pop(0)
                    front_dve(t2, sl)
                hA, hB = 2 * hp, 2 * hp + 1
                pvset = hp % 2
                pvtA = pA_v[0:65, 0:512] if pvset == 0 else pA_o[0:65, 0:512]
                pvtB = pA_v[0:65, 512:1024] if pvset == 0 else pA_oB
                buf = kc % 2
                sPair = (pA_q, pA_k)[buf]
                PPd = (PP0, PP1)[buf]
                mq = mq_pend
                diag = (kc >= 4 * g)
                wexp = [(sq, mq), (ld, NM_WAIT[(s, g)][0]), (sv, NM_WAIT[(s, g)][1])]
                aB = P.emit("scalar", lambda e, sPair=sPair, hp=hp, dst=PPd: e.activation(
                    out=dst[:], in_=sPair[:, 0:1024], func=AF.Exp, scale=0.125, bias=negmp[:, hp:hp + 1]),
                    inc=sa, wait=wexp)
                aA = aB
                last_exp_parity[buf] = aB
                if idx + 1 < len(chunk_list):
                    mq_pend = emit_scores(idx + 1)
                pw = [(sa, aB)]
                if diag:
                    j = kc - 4 * g
                    vmB = P.emit("vector", lambda e, j=j, PPd=PPd: e.tensor_tensor(
                        out=PPd[:].rearrange("p (two n) -> p two n", n=512),
                        in0=PPd[:].rearrange("p (two n) -> p two n", n=512),
                        in1=dmask_b[:, 512 * j:512 * (j + 1)].unsqueeze(1).transpose([0, 2, 1]).broadcast_to([128, 2, 512])
                        if False else dmask_b[:, 512 * j:512 * (j + 1)].unsqueeze(1).broadcast_to([128, 2, 512]),
                        op=AO.mult),
                        inc=sv, wait=[(sa, aB)])
                    pw = [(sv, vmB)]
                if kc == 0:
                    pw = pw + [(sv, last_norm_pv[pvset])]
                P.emit("tensor", lambda e, kc=kc, hA=hA, PPd=PPd, pvtA=pvtA,
                       st_=(kc == 0), sp_=(kc == nkc - 1): e.matmul(
                    pvtA, Vp[:, kc, 65 * hA:65 * hA + 65], PPd[:, 0:512],
                    start=st_, stop=sp_),
                    inc=(sq if kc == nkc - 1 else None), wait=pw)
                P.emit("tensor", lambda e, kc=kc, hB=hB, PPd=PPd, pvtB=pvtB,
                       st_=(kc == 0), sp_=(kc == nkc - 1): e.matmul(
                    pvtB, Vp[:, kc, 65 * hB:65 * hB + 65], PPd[:, 512:1024],
                    start=st_, stop=sp_),
                    inc=(sq if kc == nkc - 1 else None))
                if kc == nkc - 1:
                    PEND_TAIL.append((hp, pvset, pvtA, pvtB, sq.n))
                if PEND_TAIL and (kc == min(2, nkc - 1) or idx == len(chunk_list) - 1):
                    if PEND_TAIL[0][0] != hp or idx == len(chunk_list) - 1:
                        thp, tset, tA, tB, tpv = PEND_TAIL.pop(0)
                        emit_pair_tail(thp, tset, tA, tB, tpv)
            while PEND_TAIL:
                thp, tset, tA, tB, tpv = PEND_TAIL.pop(0)
                emit_pair_tail(thp, tset, tA, tB, tpv)
            v_attn = sv.n

            for tt in range(4):
                t = 4 * g + tt
                lxr = P.emit("sync", lambda e, s=s, t=t: e.dma_start(
                    out=x_t[:], in_=xs[s, 128 * t:128 * (t + 1), :]),
                    inc=ld, inc_by=16, wait=[(sv, sv.n), (sa, sa.n)])
                for o2 in range(2):
                    for hp in range(8):
                        P.emit("tensor", lambda e, o2=o2, hp=hp, tt=tt: e.matmul(
                            pA_q[:, 512 * o2:512 * (o2 + 1)],
                            attnT_g[:, hp, 128 * tt:128 * (tt + 1)],
                            WoT[:, hp, 512 * o2:512 * (o2 + 1)],
                            start=(hp == 0), stop=(hp == 7)),
                            inc=(sq if hp == 7 else None),
                            wait=([(sv, v_attn), (sa, sa.n), (sv, sv.n)] if (hp == 0 and o2 == 0) else []))
                mo = sq.n
                P.emit("vector", lambda e: e.tensor_tensor(out=zu[:], in0=pA_q[:, :], in1=bo_s[:], op=AO.add),
                       inc=sv, wait=[(sq, mo), (st, st.n)])
                vx1 = P.emit("vector", lambda e: e.tensor_tensor(out=zu[:], in0=zu[:], in1=x_t[:], op=AO.add),
                             inc=sv, wait=[(ld, lxr)])
                P.emit("gpsimd", lambda e, s=s, t=t: e.dma_start(
                    out=xmid[s, 128 * t:128 * (t + 1), :], in_=zu[:]),
                    inc=st, inc_by=16, wait=[(sv, vx1)])
                if "x1" in taps:
                    P.emit("gpsimd", lambda e, s=s, t=t: e.dma_start(
                        out=tap_t["x1"][s, 128 * t:128 * (t + 1), :], in_=zu[:]),
                        inc=st, inc_by=16, wait=[(sv, vx1)])

    for s in range(NSEQ):
        phase_a_seq(s)
    pa.close()
    watt.close()
    barrierA = [(sv, sv.n), (sq, sq.n), (st, st.n), (ld, ld.n), (sa, sa.n)]

    if stop_after == "phaseA":
        P.emit("gpsimd", lambda e: e.nop(), wait=barrierA)
        finalize(nc, P)
        return nc, stack

    pb = contextlib.ExitStack()
    qsb = lambda name, shape, dtype: pb.enter_context(nc.sbuf_tensor(name, list(shape), dtype))
    wgqT = qsb("wgqT", (128, 8, I_FFN), BF16)
    wuqT = qsb("wuqT", (128, 8, I_FFN), BF16)
    wdqT = qsb("wdqT", (128, 22, D), BF16)
    xb_t = qsb("xb_t", (128, D), F32)
    xb_t2 = qsb("xb_t2", (128, D), F32)
    h2_f = qsb("h2_f", (128, D), F32)
    mu_all = qsb("mu_all", (128, NSEQ * NT), F32)
    rstd_all = qsb("rstd_all", (128, NSEQ * NT), F32)
    h2qT = qsb("h2qT", (128, 8, 128), BF16)
    h1qT = qsb("h1qT", (128, 8, 128), BF16)
    gsil = qsb("gsil", (128, 1024), F32)
    h1_s = qsb("h1_s", (128, 1024), F32)
    ob_t = qsb("ob_t", (128, D), F32)
    b_sq = qsb("b_sq", (128, 2 * D), F32)
    b_s1 = qsb("b_s1", (128, 2), F32); b_s2 = qsb("b_s2", (128, 2), F32)
    b_mu = qsb("b_mu", (128, 1), F32); b_rstd = qsb("b_rstd", (128, 1), F32)
    q_out = qsb("q_out", (128, D), BF16)
    q_ax2 = qsb("q_ax2", (128, D), F32)
    q_t2 = qsb("q_t2", (128, D), F32)
    q_y = qsb("q_y", (128, D), F32)
    q_y2 = qsb("q_y2", (128, D), F32)
    q_sg = qsb("q_sg", (128, D), BF16)
    q_m = qsb("q_m", (128, D), BF16)
    q_bm = qsb("q_bm", (128, D // 16), F32)
    q_sc = qsb("q_sc", (128, D // 16), F32)
    q_rs = qsb("q_rs", (128, D // 16), F32)
    q_s2 = qsb("q_s2", (128, D // 16), BF16)
    b_scr = (q_ax2, q_t2, q_y, q_y2, q_sg, q_m, q_bm, q_sc, q_rs, q_s2)

    pB_g = pb.enter_context(nc.psum_tensor("pB_g", [128, 1024], F32))
    pB_u = pb.enter_context(nc.psum_tensor("pB_u", [128, 1024], F32))
    pB_t = pb.enter_context(nc.psum_tensor("pB_t", [128, 512], BF16))
    pB_d = pb.enter_context(nc.psum_tensor("pB_d", [128, 1024], F32))

    for eng in ENGINES:
        P.emit(eng, lambda e: e.nop(), wait=barrierA)
    for dst, off, ch, nrows in ((wgqT, WGT_OFF, 8, I_FFN), (wuqT, WUT_OFF, 8, I_FFN), (wdqT, WDT_OFF, 22, D)):
        P.emit("sync", lambda e, dst=dst, off=off, ch=ch, nr=nrows: e.dma_start(
            out=dst[:], in_=wq_scr[off:off + 128 * ch * nr].rearrange("(p c n) -> p c n", p=128, c=ch)),
            inc=ld, inc_by=16)
    ld_wq = ld.n

    scr_guard = [None]
    OD_SLICES = [(0, 1024), (1024, 1024), (2048, 768)]

    def phase_b_prologue():
        NBT = 2
        for it in range(NSEQ * NT // NBT):
            ti0 = it * NBT
            s0, t0_ = ti0 // NT, ti0 % NT
            lx = P.emit("sync", lambda e, s0=s0, t0_=t0_: e.dma_start(
                out=b_sq[:].rearrange("p (b n) -> p b n", n=D),
                in_=xmid[s0, 128 * t0_:128 * (t0_ + NBT), :].rearrange("(b q) n -> q b n", b=NBT)),
                inc=ld, inc_by=16, wait=([(ld, ld_wq)] if it == 0 else []) + [(sv, sv.n), (sa, sa.n)])
            vmu = P.emit("vector", lambda e, ti0=ti0: e.reduce_sum(
                out=mu_all[:, ti0:ti0 + NBT], in_=b_sq[:].rearrange("p (b n) -> p b n", n=D), axis=AX.X),
                inc=sv, wait=[(ld, lx)])
            a0 = P.emit("scalar", lambda e: e.activation(out=b_sq[:], in_=b_sq[:], func=AF.Square),
                        inc=sa, wait=[(ld, lx), (sv, vmu)])
            P.emit("vector", lambda e: e.reduce_sum(
                out=b_s2[:, 0:NBT], in_=b_sq[:].rearrange("p (b n) -> p b n", n=D), axis=AX.X),
                inc=sv, wait=[(sa, a0), (sv, sv.n)])
            P.emit("vector", lambda e, ti0=ti0: e.tensor_scalar_mul(
                out=mu_all[:, ti0:ti0 + NBT], in0=mu_all[:, ti0:ti0 + NBT], scalar1=1.0 / D),
                inc=sv, wait=[(sv, sv.n)])
            P.emit("vector", lambda e: e.tensor_scalar_mul(out=b_s2[:, 0:NBT], in0=b_s2[:, 0:NBT], scalar1=1.0 / D),
                   inc=sv, wait=[(sv, sv.n)])
            P.emit("vector", lambda e, ti0=ti0: e.tensor_tensor(
                out=b_s1[:, 0:NBT], in0=mu_all[:, ti0:ti0 + NBT], in1=mu_all[:, ti0:ti0 + NBT], op=AO.mult),
                inc=sv, wait=[(sv, sv.n)])
            P.emit("vector", lambda e: e.tensor_tensor(out=b_s1[:, 0:NBT], in0=b_s1[:, 0:NBT], in1=b_s2[:, 0:NBT], op=AO.subtract),
                   inc=sv, wait=[(sv, sv.n)])
            P.emit("vector", lambda e: e.tensor_scalar(out=b_s1[:, 0:NBT], in0=b_s1[:, 0:NBT], scalar1=-1.0, scalar2=1e-5,
                                                       op0=AO.mult, op1=AO.add), inc=sv, wait=[(sv, sv.n)])
            vr3 = P.emit("vector", lambda e: e.reciprocal(out=b_s2[:, 0:NBT], in_=b_s1[:, 0:NBT]), inc=sv, wait=[(sv, sv.n)])
            P.emit("scalar", lambda e, ti0=ti0: e.activation(out=rstd_all[:, ti0:ti0 + NBT], in_=b_s2[:, 0:NBT], func=AF.Sqrt),
                   inc=sa, wait=[(sv, vr3)])

    VO_H = []

    def phase_b_tile(s, t):
        ti = s * NT + t
        xb = (xb_t, xb_t2)[ti % 2]
        w_lx = [(sv, VO_H[-2])] if len(VO_H) >= 2 else [(sv, sv.n), (sa, sa.n)]
        lx = P.emit("sync", lambda e, s=s, t=t, xb=xb: e.dma_start(
            out=xb[:], in_=xmid[s, 128 * t:128 * (t + 1), :]),
            inc=ld, inc_by=16, wait=w_lx)
        if "xb" in taps:
            P.emit("gpsimd", lambda e, s=s, t=t, xb=xb: e.dma_start(
                out=tap_t["xb"][s, 128 * t:128 * (t + 1), :], in_=xb[:]),
                inc=st, inc_by=16, wait=[(ld, lx)])
        ah = P.emit("vector", lambda e, ti=ti, xb=xb: e.tensor_scalar(
            out=h2_f[:], in0=xb[:], scalar1=mu_all[:, ti:ti + 1], scalar2=rstd_all[:, ti:ti + 1],
            op0=AO.subtract, op1=AO.mult),
            inc=sv, wait=[(ld, lx)])
        _, vq2p = emit_quant(b_scr, h2_f[:], D, q_out[:], h2_f,
                             wv=[(sv, ah)], wa=[(sv, ah)], w_final=[(sq, sq.n)],
                             scratch_guard=scr_guard[0], split=PHASEB_SPLIT, rscale_act=False)
        scr_guard[0] = vq2p
        if "h2q" in taps:
            vcp = P.emit("vector", lambda e: e.tensor_copy(out=q_t2[:], in_=q_out[:]), inc=sv, wait=[vq2p])
            P.emit("gpsimd", lambda e, s=s, t=t: e.dma_start(
                out=tap_t["h2q"][s, 128 * t:128 * (t + 1), :], in_=q_t2[:]),
                inc=st, inc_by=16, wait=[(sv, vcp)])
        for cq in range(2):
            for k in range(4):
                c = 4 * cq + k
                P.emit("tensor", lambda e, c=c, k=k: e.transpose(
                    pB_t[:, 128 * k:128 * (k + 1)], q_out[:, 128 * c:128 * (c + 1)], ident_b[:]),
                    inc=(sq if k == 3 else None),
                    wait=([(sv, sv.n), (sq, sq.n)] if k == 0 else []))
            tq = sq.n
            P.emit("vector", lambda e, cq=cq: e.tensor_copy(
                out=h2qT[:, 4 * cq:4 * cq + 4, :], in_=pB_t[:].rearrange("p (k n) -> p k n", n=128)),
                inc=sv, wait=[(sq, tq)])
        v_h2qT = sv.n

        first_dn = True
        def emit_gu(si, wv_extra):
            o0, ow = OD_SLICES[si]
            mw = [(sv, v_h2qT)] + wv_extra
            for pg, pt in ((pB_g, wgqT), (pB_u, wuqT)):
                for hh in range(0, ow, 512):
                    hw = min(512, ow - hh)
                    for c in range(8):
                        P.emit("tensor", lambda e, c=c, o0=o0, hh=hh, hw=hw, pg=pg, pt=pt: e.matmul(
                            pg[:, hh:hh + hw], h2qT[:, c, :], pt[:, c, o0 + hh:o0 + hh + hw],
                            start=(c == 0), stop=(c == 7)),
                            inc=(sq if c == 7 else None), wait=(mw if c == 0 else []))
                        mw = []
            return sq.n
        mgu = emit_gu(0, [(sa, sa.n), (sv, sv.n)])
        for si, (o0, ow) in enumerate(OD_SLICES):
            ag = P.emit("scalar", lambda e, ow=ow: e.activation(out=gsil[:, 0:ow], in_=pB_g[:, 0:ow], func=AF.Silu),
                        inc=sa, wait=[(sq, mgu)])
            vh1 = P.emit("vector", lambda e, ow=ow: e.tensor_tensor(
                out=h1_s[:, 0:ow], in0=gsil[:, 0:ow], in1=pB_u[:, 0:ow], op=AO.mult),
                inc=sv, wait=[(sa, ag)])
            if si + 1 < len(OD_SLICES):
                mgu = emit_gu(si + 1, [(sa, ag), (sv, vh1)])
            if "h1" in taps:
                P.emit("gpsimd", lambda e, s=s, t=t, o0=o0, ow=ow: e.dma_start(
                    out=tap_t["h1"][s, 128 * t:128 * (t + 1), o0:o0 + ow], in_=h1_s[:, 0:ow]),
                    inc=st, inc_by=16, wait=[(sv, vh1)])
            _, vq1p = emit_quant(b_scr, h1_s[:, 0:ow], ow, q_out[:, 0:ow], h1_s,
                                 wv=[(sv, vh1)], wa=[(sv, vh1)], w_final=[(sq, sq.n)],
                                 scratch_guard=scr_guard[0], split=PHASEB_SPLIT, rscale_act=False)
            scr_guard[0] = vq1p
            nch = ow // 128
            for cq in range((nch + 3) // 4):
                ks = [k for k in range(4) if 4 * cq + k < nch]
                for k in ks:
                    c = 4 * cq + k
                    P.emit("tensor", lambda e, c=c, k=k: e.transpose(
                        pB_t[:, 128 * k:128 * (k + 1)], q_out[:, 128 * c:128 * (c + 1)], ident_b[:]),
                        inc=(sq if k == ks[-1] else None),
                        wait=(([(sv, sv.n), (sq, sq.n)] + ([vq1p] if cq == 0 else [])) if k == 0 else []))
                tq1 = sq.n
                P.emit("vector", lambda e, cq=cq, nk=len(ks): e.tensor_copy(
                    out=h1qT[:, 4 * cq:4 * cq + nk, :], in_=pB_t[:, 0:128 * nk].rearrange("p (k n) -> p k n", n=128)),
                    inc=sv, wait=[(sq, tq1)])
            vev = sv.n
            for k in range(nch):
                cglob = o0 // 128 + k
                for o2 in range(2):
                    P.emit("tensor", lambda e, k=k, cglob=cglob, o2=o2,
                           st_=(first_dn and k == 0),
                           sp_=(si == len(OD_SLICES) - 1 and k == nch - 1): e.matmul(
                        pB_d[:, 512 * o2:512 * (o2 + 1)], h1qT[:, k, :],
                        wdqT[:, cglob, 512 * o2:512 * (o2 + 1)], start=st_, stop=sp_),
                        inc=(sq if (k == nch - 1 and o2 == 1) else None),
                        wait=([(sv, vev)] if (k == 0 and o2 == 0) else []))
            first_dn = False
        m_dn = sq.n
        vo = P.emit("vector", lambda e, xb=(xb_t, xb_t2)[ti % 2]: e.tensor_tensor(
            out=ob_t[:], in0=pB_d[:, :], in1=xb[:], op=AO.add),
            inc=sv, wait=[(sq, m_dn), (so, so.n)])
        VO_H.append(vo)
        P.emit("gpsimd", lambda e, s=s, t=t: e.dma_start(
            out=out[s, 128 * t:128 * (t + 1), :], in_=ob_t[:]),
            inc=so, inc_by=16, wait=[(sv, vo)])

    phase_b_prologue()
    for s in range(NSEQ):
        for t in range(NT):
            phase_b_tile(s, t)
    pb.close()

    P.emit("gpsimd", lambda e: e.nop(), wait=[(so, so.n), (st, st.n)])
    finalize(nc, P)
    return nc, stack


def finalize(nc, P):
    with nc.Block() as block:
        @block.sync
        def _(e):
            for fn in P.ops["sync"]:
                fn(e)

        @block.vector
        def _(e):
            for fn in P.ops["vector"]:
                fn(e)

        @block.scalar
        def _(e):
            for fn in P.ops["scalar"]:
                fn(e)

        @block.tensor
        def _(e):
            for fn in P.ops["tensor"]:
                fn(e)

        @block.gpsimd
        def _(e):
            for fn in P.ops["gpsimd"]:
                fn(e)


def make_in_maps(inputs, consts=None, n_cores=N_CORES):
    if consts is None:
        consts = host_consts()
    lay = host_inputs_layout(inputs)
    x = np.ascontiguousarray(np.asarray(inputs["L0_attn_x"], dtype=np.float32))
    base = {
        "Wq": np.ascontiguousarray(np.asarray(inputs["L0_attn_Wq"], dtype=np.float32)),
        "Wk": np.ascontiguousarray(np.asarray(inputs["L0_attn_Wk"], dtype=np.float32)),
        "Wv": np.ascontiguousarray(np.asarray(inputs["L0_attn_Wv"], dtype=np.float32)),
        "Wo": np.ascontiguousarray(np.asarray(inputs["L0_attn_Wo"], dtype=np.float32)),
        "wg": np.ascontiguousarray(np.asarray(inputs["L0_ffn_w_gate"], dtype=np.float32)),
        "wu": np.ascontiguousarray(np.asarray(inputs["L0_ffn_w_up"], dtype=np.float32)),
        "wd": np.ascontiguousarray(np.asarray(inputs["L0_ffn_w_down"], dtype=np.float32)),
        "c_cos": consts["c_cos"], "c_sinp": consts["c_sinp"],
        "c_ident": consts["c_ident"], "c_dmask": consts["c_dmask"],
        "gq": lay["gq"], "gqp": lay["gqp"], "bq": lay["bq"], "bqp": lay["bqp"],
        "gk": lay["gk"], "gkp": lay["gkp"], "bk": lay["bk"], "bkp": lay["bkp"],
        "bo_rep": lay["bo_rep"],
    }
    in_maps = []
    for c in range(n_cores):
        m = dict(base)
        m["xs"] = np.ascontiguousarray(x[NSEQ * c:NSEQ * (c + 1)])
        in_maps.append(m)
    return in_maps


def kernel(**inputs):
    nc, stack = build_program()
    in_maps = make_in_maps(inputs)
    res = run_bass_kernel_spmd(nc, in_maps, core_ids=list(range(N_CORES)))
    stack.close()
    return np.concatenate([np.asarray(r["out"]) for r in res.results], axis=0).astype(np.float32)


# version 26
# speedup vs baseline: 1.1766x; 1.1766x over previous
import math
import contextlib
import numpy as np
import ml_dtypes

import concourse.bass as bass
import concourse.mybir as mybir
from concourse.bass_utils import run_bass_kernel_spmd

F32 = mybir.dt.float32
BF16 = mybir.dt.bfloat16
AF = mybir.ActivationFunctionType
AO = mybir.AluOpType
AX = mybir.AxisListType

S = 2048
D = 1024
H = 16
HD = 64
NSEQ = 4
NT = S // 128
NG = 4
I_FFN = 2816
N_CORES = 8

ENGINES = ("sync", "vector", "scalar", "tensor", "gpsimd")


def host_consts():
    pos = np.arange(S, dtype=np.float32)[:, None]
    di = np.arange(0, HD, 2, dtype=np.float32) / HD
    theta = pos * np.exp(-di * math.log(10000.0))
    cos = np.repeat(np.cos(theta), 2, axis=1).astype(np.float32)
    sin = np.repeat(np.sin(theta), 2, axis=1).astype(np.float32)
    sinp = sin.copy()
    sinp[:, :32] *= -1.0
    dmask = np.zeros((4, 128, 512), dtype=np.float32)
    for j in range(4):
        p = np.arange(128)[:, None]
        f = np.arange(512)[None, :]
        dmask[j] = ((128 * j + p) <= f).astype(np.float32)
    return {
        "c_cos": cos, "c_sinp": sinp,
        "c_ident": np.eye(128, dtype=np.float32),
        "c_dmask": dmask.astype(ml_dtypes.bfloat16),
    }


def host_inputs_layout(inp):
    out = {}
    rep = lambda a: np.ascontiguousarray(
        np.broadcast_to(np.asarray(a, np.float32).reshape(1, -1), (128, np.asarray(a).size)))
    perm = np.empty(HD, dtype=np.int64)
    perm[:32] = 2 * np.arange(32) + 1
    perm[32:] = 2 * np.arange(32)
    for nm, g, b in (("q", inp["L0_attn_q_gamma"], inp["L0_attn_q_beta"]),
                     ("k", inp["L0_attn_k_gamma"], inp["L0_attn_k_beta"])):
        g = np.asarray(g, np.float32); b = np.asarray(b, np.float32)
        out[f"g{nm}"] = rep(g)
        out[f"g{nm}p"] = rep(g[perm])
        out[f"b{nm}"] = rep(b)
        out[f"b{nm}p"] = rep(b[perm])
    out["bo_rep"] = rep(inp["L0_attn_bo"])
    return out


class Sem:
    def __init__(self, stack, nc, name):
        self.h = stack.enter_context(nc.semaphore(name))
        self.n = 0


class Prog:
    def __init__(self, nc, stack):
        self.nc = nc
        self.stack = stack
        self.ops = {e: [] for e in ENGINES}

    def sem(self, name):
        return Sem(self.stack, self.nc, name)

    def emit(self, eng, fn, inc=None, inc_by=1, wait=()):
        waits = [(w[0], w[1]) if isinstance(w, tuple) else (w, w.n) for w in wait]
        if inc is not None:
            inc.n += inc_by
        post = inc.n if inc is not None else None

        def run(e, fn=fn, waits=waits, inc=inc, inc_by=inc_by):
            for s_, v_ in waits:
                if v_ > 0:
                    e.wait_ge(s_.h, v_)
            r = fn(e)
            if inc is not None:
                r.then_inc(inc.h, inc_by)
        self.ops[eng].append(run)
        return post


def build_program(taps=(), stop_after=None, PHASEB_SPLIT=False):
    nc = bass.Bass("TRN2", target_bir_lowering=False)
    stack = contextlib.ExitStack()

    def din(name, shape, dtype=F32):
        return nc.dram_tensor(name, list(shape), dtype, kind="ExternalInput").ap()

    xs = din("xs", (NSEQ, S, D))
    Wq = din("Wq", (D, D)); Wk = din("Wk", (D, D)); Wv = din("Wv", (D, D)); Wo = din("Wo", (D, D))
    wg = din("wg", (I_FFN, D)); wu = din("wu", (I_FFN, D)); wd = din("wd", (D, I_FFN))
    c_cos = din("c_cos", (S, HD)); c_sinp = din("c_sinp", (S, HD))
    c_ident = din("c_ident", (128, 128))
    c_dmask = din("c_dmask", (4, 128, 512), BF16)
    gq = din("gq", (128, HD)); gqp = din("gqp", (128, HD)); bq = din("bq", (128, HD)); bqp = din("bqp", (128, HD))
    gk = din("gk", (128, HD)); gkp = din("gkp", (128, HD)); bk = din("bk", (128, HD)); bkp = din("bkp", (128, HD))
    bo_rep = din("bo_rep", (128, D))

    out = nc.dram_tensor("out", [NSEQ, S, D], F32, kind="ExternalOutput").ap()
    xmid = nc.dram_tensor("xmid", [NSEQ, S, D], F32, kind="Internal").ap()
    rs_scr = nc.dram_tensor("rs_scr", [H, 512], F32, kind="Internal").ap()
    nm_scr = nc.dram_tensor("nm_scr", [H], F32, kind="Internal").ap()
    wq_scr = nc.dram_tensor("wq_scr", [2 * 128 * 8 * I_FFN + 128 * 22 * D], BF16, kind="Internal").ap()
    WGT_OFF = 0
    WUT_OFF = 128 * 8 * I_FFN
    WDT_OFF = 2 * 128 * 8 * I_FFN

    taps = set(taps)
    tap_t = {}
    for nm, shp in (("h", [NSEQ, S, D]), ("qr", [NSEQ, S, D]), ("kr", [NSEQ, S, D]),
                    ("attnT", [NSEQ, NG, 1024, 512]), ("x1", [NSEQ, S, D]),
                    ("h2q", [NSEQ, S, D]), ("h1", [NSEQ, S, I_FFN]), ("xb", [NSEQ, S, D])):
        if nm in taps:
            tap_t[nm] = nc.dram_tensor("tap_" + nm, shp, F32, kind="ExternalOutput").ap()

    P = Prog(nc, stack)
    ld = P.sem("ld")
    st = P.sem("st")
    sv = P.sem("sv")
    sa = P.sem("sa")
    sq = P.sem("sq")
    so = P.sem("so")
    sp = P.sem("sp")

    sb = lambda name, shape, dtype: stack.enter_context(nc.sbuf_tensor(name, list(shape), dtype))
    ident_f = sb("ident_f", (128, 128), F32)
    ident_b = sb("ident_b", (128, 128), BF16)
    dmask_b = sb("dmask_b", (128, 4 * 512), BF16)
    gq_s = sb("gq_s", (128, HD), F32); gqp_s = sb("gqp_s", (128, HD), F32)
    bq_s = sb("bq_s", (128, HD), F32); bqp_s = sb("bqp_s", (128, HD), F32)
    gk_s = sb("gk_s", (128, HD), F32); gkp_s = sb("gkp_s", (128, HD), F32)
    bk_s = sb("bk_s", (128, HD), F32); bkp_s = sb("bkp_s", (128, HD), F32)
    bo_s = sb("bo_s", (128, D), F32)
    negm_bc = sb("negm_bc", (128, H), F32)
    negmp = sb("negmp", (128, H // 2), F32)

    def ldc(dst, src):
        P.emit("sync", lambda e, d=dst, s=src: e.dma_start(out=d, in_=s), inc=ld, inc_by=16)
    ldc(ident_f[:], c_ident[:, :])
    ldc(dmask_b[:].rearrange("p (j n) -> p j n", n=512), c_dmask[:, :, :].transpose([1, 0, 2]))
    for d_, s_ in ((gq_s, gq), (gqp_s, gqp), (bq_s, bq), (bqp_s, bqp),
                   (gk_s, gk), (gkp_s, gkp), (bk_s, bk), (bkp_s, bkp), (bo_s, bo_rep)):
        ldc(d_[:], s_[:, :])
    ld_consts = ld.n
    v_identb = P.emit("vector", lambda e: e.tensor_copy(out=ident_b[:], in_=ident_f[:]),
                      inc=sv, wait=[(ld, ld_consts)])

    watt = contextlib.ExitStack()
    WqT = watt.enter_context(nc.sbuf_tensor("WqT", [128, 8, D], BF16))
    WkT = watt.enter_context(nc.sbuf_tensor("WkT", [128, 8, D], BF16))
    WvT = watt.enter_context(nc.sbuf_tensor("WvT", [128, 8, D], BF16))
    WoT = watt.enter_context(nc.sbuf_tensor("WoT", [128, 8, D], BF16))

    p0 = contextlib.ExitStack()
    wstage = p0.enter_context(nc.sbuf_tensor("wstage", [128, 8, D], F32))
    wcast = p0.enter_context(nc.sbuf_tensor("wcast", [128, 8, D], BF16))
    ptw = p0.enter_context(nc.psum_tensor("ptw", [128, 512], BF16))

    for wsrc, wdstT in ((Wq, WqT), (Wk, WkT), (Wv, WvT), (Wo, WoT)):
        l0 = P.emit("sync", lambda e, w=wsrc: e.dma_start(
            out=wstage[:], in_=w[:, :].rearrange("(r p) n -> p r n", p=128)),
            inc=ld, inc_by=16, wait=[(sv, sv.n)])
        cv = P.emit("vector", lambda e: e.tensor_copy(out=wcast[:], in_=wstage[:]),
                    inc=sv, wait=[(ld, l0), (sv, v_identb), (sq, sq.n)])
        for r in range(8):
            for cq in range(2):
                w_first = [(sv, cv), (sv, sv.n)]
                for k in range(4):
                    c = 4 * cq + k
                    P.emit("tensor", lambda e, r=r, c=c, k=k: e.transpose(
                        ptw[:, 128 * k:128 * (k + 1)],
                        wcast[:, r, 128 * c:128 * (c + 1)], ident_b[:]),
                        inc=(sq if k == 3 else None),
                        wait=(w_first if k == 0 else []))
                tq = sq.n
                P.emit("vector", lambda e, w=wdstT, r=r, cq=cq: e.tensor_copy(
                    out=w[:, 4 * cq:4 * cq + 4, 128 * r:128 * (r + 1)],
                    in_=ptw[:].rearrange("p (k n) -> p k n", n=128)),
                    inc=sv, wait=[(sq, tq)])

    wbar = p0.enter_context(nc.sbuf_tensor("wbar", [128, 8, H], F32))
    for wT in (WqT, WkT):
        vr = P.emit("vector", lambda e, wT=wT: e.tensor_reduce(
            out=wbar[:], in_=wT[:].rearrange("p c (h j) -> p c h j", j=64),
            axis=AX.X, op=AO.add), inc=sv, wait=[(sq, sq.n), (sv, sv.n)])
        P.emit("vector", lambda e: e.tensor_scalar_mul(out=wbar[:], in0=wbar[:], scalar1=1.0 / 64),
               inc=sv, wait=[(sv, vr)])
        P.emit("vector", lambda e, wT=wT: e.tensor_tensor(
            out=wT[:].rearrange("p c (h j) -> p c h j", j=64),
            in0=wT[:].rearrange("p c (h j) -> p c h j", j=64),
            in1=wbar[:].unsqueeze(3).broadcast_to([128, 8, H, 64]), op=AO.subtract),
            inc=sv, wait=[(sv, sv.n)])

    p0.close()
    fqs = contextlib.ExitStack()
    f_ins = [fqs.enter_context(nc.sbuf_tensor(f"f_in{pi}", [128, I_FFN], F32)) for pi in range(2)]
    f_qs = [fqs.enter_context(nc.sbuf_tensor(f"f_q{pi}", [128, I_FFN], BF16)) for pi in range(2)]
    f_ax2 = fqs.enter_context(nc.sbuf_tensor("f_ax2", [128, I_FFN], F32))
    f_t2 = fqs.enter_context(nc.sbuf_tensor("f_t2", [128, I_FFN], F32))
    f_y = fqs.enter_context(nc.sbuf_tensor("f_y", [128, I_FFN], F32))
    f_y2 = fqs.enter_context(nc.sbuf_tensor("f_y2", [128, I_FFN], F32))
    f_sg = fqs.enter_context(nc.sbuf_tensor("f_sg", [128, I_FFN], BF16))
    f_m = fqs.enter_context(nc.sbuf_tensor("f_m", [128, I_FFN], BF16))
    f_bm = fqs.enter_context(nc.sbuf_tensor("f_bm", [128, I_FFN // 16], F32))
    f_sc = fqs.enter_context(nc.sbuf_tensor("f_sc", [128, I_FFN // 16], F32))
    f_rs = fqs.enter_context(nc.sbuf_tensor("f_rs", [128, I_FFN // 16], F32))
    f_s2 = fqs.enter_context(nc.sbuf_tensor("f_s2", [128, I_FFN // 16], BF16))
    f_scr = (f_ax2, f_t2, f_y, f_y2, f_sg, f_m, f_bm, f_sc, f_rs, f_s2)
    f_ev = fqs.enter_context(nc.sbuf_tensor("f_ev", [128, 512], BF16))
    ptf = fqs.enter_context(nc.psum_tensor("ptf", [128, 512], BF16))

    def emit_staircase(eng, sem, t2_ap, y_ap, y2_ap, n, prot, wait_first=()):
        i1 = y2_ap.bitcast(mybir.dt.int32)
        pw = (lambda: [(sem, sem.n)]) if prot else (lambda: [])
        P.emit(eng, lambda e: e.tensor_copy(out=i1, in_=t2_ap), inc=sem, wait=pw() + list(wait_first))
        P.emit(eng, lambda e: e.tensor_scalar(out=y_ap, in0=i1, scalar1=4.0, scalar2=0.5,
                                              op0=AO.min, op1=AO.mult), inc=sem, wait=pw() or [(sem, sem.n)])
        P.emit(eng, lambda e: e.tensor_scalar(out=i1, in0=t2_ap, scalar1=0.5, scalar2=-2.0,
                                              op0=AO.mult, op1=AO.add), inc=sem, wait=pw() or [(sem, sem.n)])
        P.emit(eng, lambda e: e.tensor_scalar(out=y2_ap, in0=i1, scalar1=0.0, scalar2=2.0,
                                              op0=AO.max, op1=AO.min), inc=sem, wait=pw() or [(sem, sem.n)])
        P.emit(eng, lambda e: e.tensor_tensor(out=y_ap, in0=y_ap, in1=y2_ap, op=AO.add),
               inc=sem, wait=pw() or [(sem, sem.n)])
        P.emit(eng, lambda e: e.tensor_scalar(out=i1, in0=t2_ap, scalar1=0.5, scalar2=-4.5,
                                              op0=AO.mult, op1=AO.add), inc=sem, wait=pw() or [(sem, sem.n)])
        P.emit(eng, lambda e: e.tensor_scalar(out=y2_ap, in0=i1, scalar1=0.0, scalar2=1.0,
                                              op0=AO.max, op1=AO.min), inc=sem, wait=pw() or [(sem, sem.n)])
        P.emit(eng, lambda e: e.scalar_tensor_tensor(out=y_ap, in0=y2_ap, scalar=2.0, in1=y_ap,
                                                     op0=AO.mult, op1=AO.add), inc=sem, wait=pw() or [(sem, sem.n)])

    def emit_quant(scr, src_ap, n, dst_flat_bf, sign_src, wv, wa, w_final=(), split=True, scratch_guard=None, rscale_act=True):
        ax2, t2_, y_, y2_, sg_, m_, bm_, sc_, rs_, s2_ = scr
        nb = n // 16
        peng, psem = ("gpsimd", sp) if split else ("vector", sv)
        gw = ([scratch_guard] if scratch_guard else [(sp, sp.n)]) if split else ([scratch_guard] if scratch_guard else [])
        a0 = P.emit("scalar", lambda e: e.activation(out=ax2[:, :n], in_=src_ap, func=AF.Abs, scale=2.0),
                    inc=sa, wait=list(wa) + gw)
        P.emit("vector", lambda e: e.tensor_reduce(
            out=bm_[:, :nb], in_=src_ap.rearrange("p (b k) -> p b k", k=16),
            axis=AX.X, op=AO.max, apply_absolute_value=True), inc=sv, wait=list(wv) + gw)
        P.emit("vector", lambda e: e.tensor_scalar(out=sc_[:, :nb], in0=bm_[:, :nb],
                                                   scalar1=1.0 / 6.0, scalar2=1e-12,
                                                   op0=AO.mult, op1=AO.max), inc=sv, wait=[(sv, sv.n)] + gw)
        sc_done = sv.n
        if rscale_act:
            aL = P.emit("scalar", lambda e: e.activation(out=rs_[:, :nb], in_=sc_[:, :nb], func=AF.Ln),
                        inc=sa, wait=[(sv, sc_done)] + gw)
            aE = P.emit("scalar", lambda e: e.activation(out=rs_[:, :nb], in_=rs_[:, :nb], func=AF.Exp, scale=-1.0),
                        inc=sa, wait=[(sa, aL)])
            rs_ready = (sa, aE)
        else:
            vre = P.emit("vector", lambda e: e.reciprocal(out=rs_[:, :nb], in_=sc_[:, :nb]),
                         inc=sv, wait=[(sv, sc_done)] + gw)
            rs_ready = (sv, vre)
            aE = sa.n
        P.emit(peng, lambda e: e.tensor_scalar_mul(out=s2_[:, :nb], in0=sc_[:, :nb], scalar1=2.0),
               inc=psem, wait=[(sv, sc_done)] + ([(psem, psem.n)] if True else []))
        t2p = P.emit(peng, lambda e: e.tensor_tensor(
            out=t2_.rearrange("p (b k) -> p b k", k=16)[:, 0:nb, :],
            in0=ax2.rearrange("p (b k) -> p b k", k=16)[:, 0:nb, :],
            in1=rs_[:, 0:nb].unsqueeze(2).broadcast_to([128, nb, 16]), op=AO.mult),
            inc=psem, wait=[rs_ready] + ([(psem, psem.n)] if split else []))
        P.emit(peng, lambda e: e.tensor_scalar(out=sg_[:, :n], in0=sign_src[:, :n], scalar1=0.0, scalar2=0.5,
                                               op0=AO.is_gt, op1=AO.subtract),
               inc=psem, wait=(list(wa) if split else []))
        sgp = psem.n
        emit_staircase("vector", sv, t2_[:, :n], y_[:, :n], y2_[:, :n], n,
                       prot=(n < 512), wait_first=([(psem, t2p)] if split else []) + gw)
        y_done = sv.n
        P.emit(peng, lambda e: e.tensor_tensor(out=m_[:, :n], in0=y_[:, :n], in1=sg_[:, :n], op=AO.mult),
               inc=psem, wait=([(sv, y_done), (psem, sgp)] if split else ([(sv, sv.n)] if n < 512 else [])))
        vqp = P.emit(peng, lambda e: e.tensor_tensor(
            out=dst_flat_bf.rearrange("p (b k) -> p b k", k=16)[:, 0:nb, :],
            in0=m_.rearrange("p (b k) -> p b k", k=16)[:, 0:nb, :],
            in1=s2_[:, 0:nb].unsqueeze(2).broadcast_to([128, nb, 16]), op=AO.mult),
            inc=psem, wait=list(w_final) + ([(psem, psem.n)] if split else ([(sv, sv.n)] if n < 512 else [])))
        return (None, (psem, vqp))

    f_scr = (f_ax2, f_t2, f_y, f_y2, f_sg, f_m, f_bm, f_sc, f_rs, f_s2)

    F_GUARD = [(sq, sq.n), (sq, sq.n)]
    ABS_GUARD = [(sv, sv.n)]
    SCR_GUARD = [(sa, sa.n)]

    def quant_weight(wsrc, n_rows, n_cols, dst_off, tchunks, chain_idx):
        for r in range(n_rows // 128):
            pi = chain_idx[0] % 2
            chain_idx[0] += 1
            f_in = f_ins[pi]
            f_q = f_qs[pi]
            gwait = [F_GUARD[pi], (sv, sv.n if chain_idx[0] <= 2 else 0)] if chain_idx[0] <= 2 else [F_GUARD[pi]]
            l0 = P.emit("sync", lambda e, r=r, f_in=f_in: e.dma_start(
                out=f_in[:, :n_cols], in_=wsrc[128 * r:128 * (r + 1), :]),
                inc=ld, inc_by=16, wait=gwait)
            _, vqp = emit_quant(f_scr, f_in[:, :n_cols], n_cols, f_q[:, :n_cols], f_in,
                                wv=[(ld, l0)], wa=([(ld, l0)] + [ABS_GUARD[0]] if ABS_GUARD[0] else [(ld, l0)]),
                                split=False, scratch_guard=SCR_GUARD[0])
            ABS_GUARD[0] = (sv, sv.n)
            SCR_GUARD[0] = (sa, sa.n)
            ncol_ch = n_cols // 128
            for cq in range((ncol_ch + 3) // 4):
                ks = [k for k in range(4) if 4 * cq + k < ncol_ch]
                for k in ks:
                    c = 4 * cq + k
                    P.emit("tensor", lambda e, c=c, k=k, f_q=f_q: e.transpose(
                        ptf[:, 128 * k:128 * (k + 1)],
                        f_q[:, 128 * c:128 * (c + 1)], ident_b[:]),
                        inc=(sq if k == ks[-1] else None),
                        wait=(([(sv, sv.n), vqp]) if k == 0 else []))
                tq = sq.n
                ev = P.emit("vector", lambda e, nk=len(ks): e.tensor_copy(
                    out=f_ev[:, :128 * nk], in_=ptf[:, :128 * nk]),
                    inc=sv, wait=[(sq, tq), (st, st.n)])

                def store(e, cq=cq, nk=len(ks), r=r, dst_off=dst_off, tchunks=tchunks, nrows=n_rows):
                    dst = wq_scr[dst_off:dst_off + 128 * tchunks * nrows].rearrange(
                        "(p c n) -> p c n", p=128, c=tchunks)
                    return e.dma_start(out=dst[:, 4 * cq:4 * cq + nk, 128 * r:128 * (r + 1)],
                                       in_=f_ev[:, :128 * nk].rearrange("p (k n) -> p k n", n=128))
                P.emit("gpsimd", store, inc=st, inc_by=16, wait=[(sv, ev)])
            F_GUARD[pi] = (sq, sq.n)

    _ci = [0]
    quant_weight(wg, I_FFN, D, WGT_OFF, 8, _ci)
    quant_weight(wu, I_FFN, D, WUT_OFF, 8, _ci)
    quant_weight(wd, D, I_FFN, WDT_OFF, 22, _ci)
    fqs.close()
    barrier0 = [(sv, sv.n), (sq, sq.n), (st, st.n), (ld, ld.n), (sa, sa.n)]

    pa = contextlib.ExitStack()
    psb = lambda name, shape, dtype: pa.enter_context(nc.sbuf_tensor(name, list(shape), dtype))
    kT_all = psb("kT_all", (128, 8, S), BF16)
    Vp = psb("Vp", (128, NT, H * 65), BF16)
    qT_g = psb("qT_g", (128, 8, 512), BF16)
    hT_g = psb("hT_g", (128, 8, 512), BF16)
    attnT_g = psb("attnT_g", (128, 8, 512), BF16)
    x_t = psb("x_t", (128, D), F32)
    h_bf = psb("h_bf", (128, D), BF16)
    h_bf4 = [h_bf] + [psb(f"h_bf{i}", (128, D), BF16) for i in (1, 2)] + [h_bf]
    qkp_bf = psb("qkp_bf", (128, D), BF16)
    z2 = psb("z2", (128, D), F32)
    zu = psb("zu", (128, D), F32)
    zv = psb("zv", (128, D), F32)
    zqc0 = psb("zqc0", (128, D), BF16); zkc0 = psb("zkc0", (128, D), BF16)
    zqc1 = psb("zqc1", (128, D), BF16); zkc1 = psb("zkc1", (128, D), BF16)
    cos_t = psb("cos_t", (128, HD), F32)
    sinp_t = psb("sinp_t", (128, HD), F32)
    A_t = psb("A_t", (128, HD), F32); B_t = psb("B_t", (128, HD), F32)
    D_t = psb("D_t", (128, HD), F32); e1_t = psb("e1_t", (128, HD), F32)
    PP0 = psb("PP0", (128, 1024), BF16)
    PP1 = psb("PP1", (128, 1024), BF16)
    rb_s = psb("rb_s", (128, 512), F32)
    rs2_s = psb("rs2_s", (1, 1024), F32)
    rsA = rs2_s[:, 0:512]
    rsB = rs2_s[:, 512:1024]
    st16 = psb("st16", (128, 16), F32)
    st16b = psb("st16b", (128, 16), F32)
    mu_h = psb("mu_h", (128, 16), F32)
    rstd_h = psb("rstd_h", (128, 16), F32)
    qm2_a = psb("qm2_a", (128, 16), F32); qm2_b = psb("qm2_b", (128, 16), F32)
    km2_a = psb("km2_a", (128, 16), F32); km2_b = psb("km2_b", (128, 16), F32)
    s1_c = psb("s1_c", (128, 1), F32)
    s2_c = psb("s2_c", (128, 1), F32)
    mu_c = psb("mu_c", (128, 1), F32)
    rstd_c = psb("rstd_c", (128, 1), F32)
    nm16 = psb("nm16", (16, 1), F32)
    nm16b = psb("nm16b", (16, 1), F32)

    pA_q = pa.enter_context(nc.psum_tensor("pA_q", [128, 1024], F32))
    pA_k = pa.enter_context(nc.psum_tensor("pA_k", [128, 1024], F32))
    pA_v = pa.enter_context(nc.psum_tensor("pA_v", [128, 1024], F32))
    pA_t = pa.enter_context(nc.psum_tensor("pA_t", [128, 1024], BF16))
    pA_o = pa.enter_context(nc.psum_tensor("pA_o", [65, 512], F32))
    pA_s = pA_t[:].bitcast(F32)[0:16, 0:128]
    pA_oB = pA_t[:].bitcast(F32)[0:65, 0:512]

    for eng in ENGINES:
        P.emit(eng, lambda e: e.nop(), wait=barrier0)

    v_ones = P.emit("vector", lambda e: e.memset(
        Vp[:].rearrange("p t (h j) -> p t h j", j=65)[:, :, :, 64:65], 1.0), inc=sv)

    NM_WAIT = {}

    def emit_ln_stats(src_f32, n, l_wait):
        P.emit("vector", lambda e: e.reduce_sum(out=s1_c[:], in_=src_f32, axis=AX.X),
               inc=sv, wait=[(ld, l_wait)])
        a0 = P.emit("scalar", lambda e, n=n: e.activation(out=zv[:, :n], in_=src_f32, func=AF.Square),
                    inc=sa, wait=[(ld, l_wait), (sv, sv.n)])
        P.emit("vector", lambda e, n=n: e.reduce_sum(out=s2_c[:], in_=zv[:, :n], axis=AX.X),
               inc=sv, wait=[(sa, a0)])
        P.emit("vector", lambda e, n=n: e.tensor_scalar_mul(out=mu_c[:], in0=s1_c[:], scalar1=1.0 / n),
               inc=sv, wait=[(sv, sv.n)])
        P.emit("vector", lambda e, n=n: e.tensor_scalar_mul(out=s2_c[:], in0=s2_c[:], scalar1=1.0 / n),
               inc=sv, wait=[(sv, sv.n)])
        P.emit("vector", lambda e: e.scalar_tensor_tensor(
            out=s1_c[:], in0=mu_c[:], scalar=mu_c[:, 0:1], in1=s2_c[:],
            op0=AO.mult, op1=AO.subtract), inc=sv, wait=[(sv, sv.n)])
        P.emit("vector", lambda e: e.tensor_scalar(out=s1_c[:], in0=s1_c[:], scalar1=-1.0, scalar2=1e-5,
                                                   op0=AO.mult, op1=AO.add), inc=sv, wait=[(sv, sv.n)])
        vpe = sv.n
        aln = P.emit("scalar", lambda e: e.activation(out=s2_c[:], in_=s1_c[:], func=AF.Ln),
                     inc=sa, wait=[(sv, vpe)])
        va = P.emit("scalar", lambda e: e.activation(out=rstd_c[:], in_=s2_c[:], func=AF.Exp, scale=-0.5),
                    inc=sa, wait=[(sa, aln)])
        return va, sv.n

    def phase_a_seq(s):
        km_cnt = 0
        Z_GUARD = [None, None]
        RB_GUARD = [(sv, 0)]
        FD_DONE = set()

        def front_dve(t, slot):
            FD_DONE.add(t)
            xbuf = x_t if slot % 2 == 0 else zu
            l_x = P.emit("sync", lambda e, t=t, xbuf=xbuf: e.dma_start(
                out=xbuf[:], in_=xs[s, 128 * t:128 * (t + 1), :]),
                inc=ld, inc_by=16, wait=[(sa, sa.n), (sv, sv.n), (st, st.n)])
            va, vb = emit_ln_stats(xbuf[:], D, l_x)
            ah = P.emit("vector", lambda e, slot=slot, xbuf=xbuf: e.tensor_scalar(
                out=h_bf4[slot][:], in0=xbuf[:], scalar1=mu_c[:, 0:1], scalar2=rstd_c[:, 0:1],
                op0=AO.subtract, op1=AO.mult),
                inc=sv, wait=[(sa, va), (sv, sv.n), (sq, sq.n), (st, st.n)])
            if "h" in taps:
                P.emit("gpsimd", lambda e, s=s, t=t, slot=slot: e.dma_start(
                    out=tap_t["h"][s, 128 * t:128 * (t + 1), :], in_=h_bf4[slot][:]),
                    inc=st, inc_by=16, wait=[(sv, ah)])

        for g in range(NG):
            qm_cnt = 0

            def frontend(t, tt):
                if t not in FD_DONE:
                    front_dve(t, tt)
                for cq in range(2):
                    for k in range(4):
                        c = 4 * cq + k
                        P.emit("tensor", lambda e, c=c, k=k, tt=tt: e.transpose(
                            pA_t[:, 128 * k:128 * (k + 1)],
                            h_bf4[tt][:, 128 * c:128 * (c + 1)], ident_b[:]),
                            inc=(sq if k == 3 else None),
                            wait=([(sv, sv.n)] if k == 0 else []))
                    tq = sq.n
                    P.emit("vector", lambda e, cq=cq, tt=tt: e.tensor_copy(
                        out=hT_g[:, 4 * cq:4 * cq + 4, 128 * tt:128 * (tt + 1)],
                        in_=pA_t[:, 0:512].rearrange("p (k n) -> p k n", n=128)),
                        inc=sv, wait=[(sq, tq)])
                v_hT = sv.n
                first_w = [(sv, v_hT), (sa, sa.n), (sq, sq.n)]
                for (pdst, wT) in ((pA_q, WqT), (pA_k, WkT), (pA_v, WvT)):
                    for o2 in range(2):
                        for c in range(8):
                            P.emit("tensor", lambda e, pdst=pdst, wT=wT, o2=o2, c=c, tt=tt: e.matmul(
                                pdst[:, 512 * o2:512 * (o2 + 1)],
                                hT_g[:, c, 128 * tt:128 * (tt + 1)],
                                wT[:, c, 512 * o2:512 * (o2 + 1)],
                                start=(c == 0), stop=(c == 7)),
                                inc=(sq if c == 7 else None),
                                wait=(first_w if (c == 0 and first_w) else []))
                            first_w = []
                q_all = sq.n
                P.emit("vector", lambda e, t=t: e.tensor_copy(
                    out=Vp[:, t, :].rearrange("p (h j) -> p h j", j=65)[:, :, 0:64],
                    in_=pA_v[:].rearrange("p (h j) -> p h j", j=64)),
                    inc=sv, wait=[(sq, q_all), (sv, v_ones)])
                return q_all

            def emit_evacs(q_all, par, guard):
                zq_ = (zqc0, zqc1)[par]; zk_ = (zkc0, zkc1)[par]
                w0 = [(sq, q_all)] + ([guard] if guard else [])
                eq = P.emit("scalar", lambda e, zq_=zq_: e.activation(out=zq_[:], in_=pA_q[:, :], func=AF.Copy),
                            inc=sa, wait=w0)
                ek = P.emit("scalar", lambda e, zk_=zk_: e.activation(out=zk_[:], in_=pA_k[:, :], func=AF.Copy),
                            inc=sa)
                return ek

            def backend(t, tt, ek, par):
                nonlocal qm_cnt, km_cnt
                zqc = (zqc0, zqc1)[par]; zkc = (zkc0, zkc1)[par]
                P.emit("sync", lambda e, t=t: e.dma_start(
                    out=cos_t[:], in_=c_cos[128 * t:128 * (t + 1), :]), inc=ld, inc_by=16,
                    wait=[(sv, sv.n)])
                l_cs = P.emit("sync", lambda e, t=t: e.dma_start(
                    out=sinp_t[:], in_=c_sinp[128 * t:128 * (t + 1), :]), inc=ld, inc_by=16)
                for which, pcs, g_s_, gp_s_, b_s_, bp_s_, dst_bf in (
                        ("q", zqc, gq_s, gqp_s, bq_s, bqp_s, qkp_bf),
                        ("k", zkc, gk_s, gkp_s, bk_s, bkp_s, qkp_bf)):
                    pcv = pcs[:].rearrange("p (h j) -> p h j", j=64)
                    a1 = P.emit("scalar", lambda e, pcs=pcs: e.activation(
                        out=zu[:], in_=pcs[:], func=AF.Square), inc=sa, wait=[(sa, ek), (sv, sv.n), (st, st.n)])
                    P.emit("vector", lambda e: e.reduce_sum(
                        out=st16[:], in_=zu[:].rearrange("p (h j) -> p h j", j=64), axis=AX.X),
                        inc=sv, wait=[(sa, a1), (sv, sv.n)])
                    P.emit("vector", lambda e: e.tensor_scalar(out=st16[:], in0=st16[:], scalar1=1.0 / 64, scalar2=1e-5,
                                                               op0=AO.mult, op1=AO.add),
                           inc=sv, wait=[(sv, sv.n)])
                    aln2 = P.emit("scalar", lambda e: e.activation(out=st16b[:], in_=st16[:], func=AF.Ln),
                                  inc=sa, wait=[(sv, sv.n)])
                    a2 = P.emit("scalar", lambda e: e.activation(out=rstd_h[:], in_=st16b[:], func=AF.Exp, scale=-0.5),
                                inc=sa, wait=[(sa, aln2)])
                    P.emit("vector", lambda e, gg=g_s_: e.tensor_tensor(out=A_t[:], in0=cos_t[:], in1=gg[:], op=AO.mult),
                           inc=sv, wait=[(ld, l_cs)])
                    P.emit("vector", lambda e, gp=gp_s_: e.tensor_tensor(out=B_t[:], in0=sinp_t[:], in1=gp[:], op=AO.mult), inc=sv)
                    P.emit("vector", lambda e, bb=b_s_: e.tensor_tensor(out=e1_t[:], in0=cos_t[:], in1=bb[:], op=AO.mult), inc=sv)
                    P.emit("vector", lambda e, bp=bp_s_: e.tensor_tensor(out=D_t[:], in0=sinp_t[:], in1=bp[:], op=AO.mult), inc=sv)
                    P.emit("vector", lambda e: e.tensor_tensor(out=D_t[:], in0=D_t[:], in1=e1_t[:], op=AO.add),
                           inc=sv, wait=[(sv, sv.n)])
                    hb = lambda a: a.unsqueeze(2).broadcast_to([128, 16, 64])
                    jb = lambda a: a.unsqueeze(1).broadcast_to([128, 16, 64])
                    jb32 = lambda a: a.unsqueeze(1).broadcast_to([128, 16, 32])
                    z2v = z2[:].rearrange("p (h j) -> p h j", j=64)
                    zuv = zu[:].rearrange("p (h j) -> p h j", j=64)
                    zvv = zv[:].rearrange("p (h j) -> p h j", j=64)
                    P.emit("vector", lambda e, pcv=pcv, z2v=z2v: e.tensor_tensor(
                        out=z2v, in0=pcv, in1=hb(rstd_h[:]), op=AO.mult), inc=sv, wait=[(sa, a2), (sv, sv.n)])
                    P.emit("vector", lambda e, z2v=z2v, zuv=zuv: e.tensor_tensor(
                        out=zuv, in0=z2v, in1=jb(A_t[:]), op=AO.mult), inc=sv)
                    P.emit("vector", lambda e, z2v=z2v, zvv=zvv: e.tensor_tensor(
                        out=zvv[:, :, 0:32], in0=z2v[:, :, 1::2], in1=jb32(B_t[:, 0:32]), op=AO.mult), inc=sv)
                    P.emit("vector", lambda e, z2v=z2v, zvv=zvv: e.tensor_tensor(
                        out=zvv[:, :, 32:64], in0=z2v[:, :, 0::2], in1=jb32(B_t[:, 32:64]), op=AO.mult), inc=sv)
                    P.emit("vector", lambda e: e.tensor_tensor(out=zu[:], in0=zu[:], in1=zv[:], op=AO.add), inc=sv)
                    vq = P.emit("vector", lambda e, dst=dst_bf, zuv=zuv: e.tensor_tensor(
                        out=dst[:].rearrange("p (h j) -> p h j", j=64), in0=zuv, in1=jb(D_t[:]), op=AO.add),
                        inc=sv, wait=[(sq, sq.n), (st, st.n), (sa, sa.n)])
                    a3 = P.emit("scalar", lambda e, dst=dst_bf: e.activation(
                        out=zu[:], in_=dst[:], func=AF.Square), inc=sa, wait=[(sv, vq)])
                    P.emit("vector", lambda e: e.reduce_sum(
                        out=st16[:], in_=zu[:].rearrange("p (h j) -> p h j", j=64), axis=AX.X),
                        inc=sv, wait=[(sa, a3), (sv, sv.n)])
                    if which == "q":
                        srcb, dstb = (qm2_a, qm2_b) if qm_cnt % 2 else (qm2_b, qm2_a)
                        if qm_cnt == 0:
                            P.emit("vector", lambda e, d=dstb: e.tensor_copy(out=d[:], in_=st16[:]),
                                   inc=sv, wait=[(sv, sv.n)])
                        else:
                            P.emit("vector", lambda e, s_=srcb, d=dstb: e.tensor_tensor(
                                out=d[:], in0=s_[:], in1=st16[:], op=AO.max), inc=sv, wait=[(sv, sv.n)])
                        qm_cnt += 1
                    else:
                        srcb, dstb = (km2_a, km2_b) if km_cnt % 2 else (km2_b, km2_a)
                        if km_cnt == 0:
                            P.emit("vector", lambda e, d=dstb: e.tensor_copy(out=d[:], in_=st16[:]),
                                   inc=sv, wait=[(sv, sv.n)])
                        else:
                            P.emit("vector", lambda e, s_=srcb, d=dstb: e.tensor_tensor(
                                out=d[:], in0=s_[:], in1=st16[:], op=AO.max), inc=sv, wait=[(sv, sv.n)])
                        km_cnt += 1
                    if (which == "q" and "qr" in taps) or (which == "k" and "kr" in taps):
                        nm = "qr" if which == "q" else "kr"
                        P.emit("gpsimd", lambda e, s=s, t=t, dst=dst_bf, nm=nm: e.dma_start(
                            out=tap_t[nm][s, 128 * t:128 * (t + 1), :], in_=dst[:]),
                            inc=st, inc_by=16, wait=[(sv, vq)])
                    tdst = qT_g if which == "q" else kT_all
                    toff = 128 * tt if which == "q" else 128 * t
                    for cq in range(2):
                        for k in range(4):
                            c = 4 * cq + k
                            P.emit("tensor", lambda e, c=c, k=k, dst=dst_bf: e.transpose(
                                pA_t[:, 128 * k:128 * (k + 1)],
                                dst[:, 128 * c:128 * (c + 1)], ident_b[:]),
                                inc=(sq if k == 3 else None),
                                wait=([(sv, sv.n)] if k == 0 else []))
                        tq2 = sq.n
                        P.emit("vector", lambda e, cq=cq, td=tdst, to=toff: e.tensor_copy(
                            out=td[:, 4 * cq:4 * cq + 4, to:to + 128],
                            in_=pA_t[:, 0:512].rearrange("p (k n) -> p k n", n=128)),
                            inc=sv, wait=[(sq, tq2)])

            pend = None
            for tt in range(4):
                t = 4 * g + tt
                q_all_t = frontend(t, tt)
                if pend is not None:
                    backend(pend[0], pend[1], pend[2], pend[3])
                    Z_GUARD[pend[3]] = (sv, sv.n)
                ek_t = emit_evacs(q_all_t, tt % 2, Z_GUARD[tt % 2])
                pend = (t, tt, ek_t, tt % 2)
            backend(pend[0], pend[1], pend[2], pend[3])
            Z_GUARD[pend[3]] = (sv, sv.n)

            qmb = qm2_b if ((qm_cnt - 1) % 2 == 1) else qm2_a
            kmb = km2_b if ((km_cnt - 1) % 2 == 1) else km2_a
            tqs = P.emit("tensor", lambda e, qmb=qmb: e.transpose(
                pA_s, qmb[:], ident_f[:]), inc=sq, wait=[(sv, sv.n)])
            P.emit("vector", lambda e: e.reduce_max(out=nm16[:], in_=pA_s, axis=AX.X),
                   inc=sv, wait=[(sq, tqs), (sv, sv.n)])
            tks = P.emit("tensor", lambda e, kmb=kmb: e.transpose(
                pA_s, kmb[:], ident_f[:]), inc=sq, wait=[(sv, sv.n)])
            P.emit("vector", lambda e: e.reduce_max(out=nm16b[:], in_=pA_s, axis=AX.X),
                   inc=sv, wait=[(sq, tks), (sv, sv.n)])
            vmm = P.emit("vector", lambda e: e.tensor_tensor(out=nm16[:], in0=nm16[:], in1=nm16b[:], op=AO.mult),
                         inc=sv, wait=[(sv, sv.n)])
            aM0 = P.emit("scalar", lambda e: e.activation(out=nm16b[:], in_=nm16[:], func=AF.Ln),
                         inc=sa, wait=[(sv, vmm)])
            aM = P.emit("scalar", lambda e: e.activation(out=nm16b[:], in_=nm16b[:], func=AF.Exp, scale=0.5),
                        inc=sa, wait=[(sa, aM0)])
            vM = P.emit("vector", lambda e: e.tensor_scalar_mul(out=nm16[:], in0=nm16b[:], scalar1=-0.125),
                        inc=sv, wait=[(sa, aM)])
            sM = P.emit("gpsimd", lambda e: e.dma_start(
                out=nm_scr[:].rearrange("(h o) -> h o", o=1), in_=nm16[:]),
                inc=st, inc_by=16, wait=[(sv, vM)])
            lM = P.emit("sync", lambda e: e.dma_start(
                out=negm_bc[:], in_=nm_scr[:].unsqueeze(0).broadcast_to([128, H])),
                inc=ld, inc_by=16, wait=[(st, sM), (sa, sa.n)])
            vnp = P.emit("vector", lambda e: e.tensor_tensor(
                out=negmp[:], in0=negm_bc[:].rearrange("p (h two) -> p h two", two=2)[:, :, 0],
                in1=negm_bc[:].rearrange("p (h two) -> p h two", two=2)[:, :, 1], op=AO.min),
                inc=sv, wait=[(ld, lM), (sv, sv.n)])
            NM_WAIT[(s, g)] = (lM, vnp)

            nkc = 4 * (g + 1)
            last_exp_parity = [0, 0]
            last_norm_pv = [sv.n, sv.n]
            chunk_list = [(hp, kc) for hp in range(8) for kc in range(nkc)]

            def emit_scores(idx, extra_wait=()):
                hp, kc = chunk_list[idx]
                buf = kc % 2
                sPair = (pA_q, pA_k)[buf]
                w0 = [(sa, last_exp_parity[buf])] + list(extra_wait)
                P.emit("tensor", lambda e, sPair=sPair, hp=hp, kc=kc: e.matmul(
                    sPair[:, 0:512], kT_all[0:64, hp, 128 * kc:128 * (kc + 1)],
                    qT_g[0:64, hp, :], start=True, stop=True, tile_position=(0, 0)),
                    inc=sq, wait=w0)
                P.emit("tensor", lambda e, sPair=sPair, hp=hp, kc=kc: e.matmul(
                    sPair[:, 512:1024], kT_all[64:128, hp, 128 * kc:128 * (kc + 1)],
                    qT_g[64:128, hp, :], start=True, stop=True, tile_position=(64, 0)),
                    inc=sq)
                return sq.n

            PEND_TAIL = []

            def emit_pair_tail(hp, pvset, pvtA, pvtB, pv_q):
                hA = 2 * hp
                if pvset == 0:
                    aR1 = P.emit("scalar", lambda e: e.activation(out=rs2_s[:], in_=pA_v[64:65, 0:1024], func=AF.Ln),
                                 inc=sa, wait=[(sq, pv_q), (st, st.n)])
                    aR4 = P.emit("scalar", lambda e: e.activation(out=rs2_s[:], in_=rs2_s[:], func=AF.Exp, scale=-1.0),
                                 inc=sa, wait=[(sa, aR1)])
                    s2_ = P.emit("gpsimd", lambda e, hA=hA: e.dma_start(
                        out=rs_scr[hA:hA + 2, :].rearrange("a n -> (a n)").unsqueeze(0), in_=rs2_s[:]),
                        inc=st, inc_by=16, wait=[(sa, aR4)])
                else:
                    aR1 = P.emit("scalar", lambda e, pvtA=pvtA: e.activation(out=rsA, in_=pvtA[64:65, :], func=AF.Ln),
                                 inc=sa, wait=[(sq, pv_q), (st, st.n)])
                    aR2 = P.emit("scalar", lambda e: e.activation(out=rsA, in_=rsA, func=AF.Exp, scale=-1.0),
                                 inc=sa, wait=[(sa, aR1)])
                    aR3 = P.emit("scalar", lambda e, pvtB=pvtB: e.activation(out=rsB, in_=pvtB[64:65, :], func=AF.Ln),
                                 inc=sa, wait=[(sa, aR2)])
                    aR4 = P.emit("scalar", lambda e: e.activation(out=rsB, in_=rsB, func=AF.Exp, scale=-1.0),
                                 inc=sa, wait=[(sa, aR3)])
                    P.emit("gpsimd", lambda e, hA=hA: e.dma_start(out=rs_scr[hA:hA + 1, :], in_=rsA),
                           inc=st, inc_by=16, wait=[(sa, aR4)])
                    s2_ = P.emit("gpsimd", lambda e, hA=hA: e.dma_start(out=rs_scr[hA + 1:hA + 2, :], in_=rsB),
                                 inc=st, inc_by=16)
                lrb = P.emit("sync", lambda e, hA=hA: e.dma_start(
                    out=rb_s[:], in_=rs_scr[hA:hA + 2, :].unsqueeze(1).broadcast_to([2, 64, 512])),
                    inc=ld, inc_by=16, wait=[(st, s2_), RB_GUARD[0]])
                P.emit("vector", lambda e, hp=hp, pvtA=pvtA: e.tensor_tensor(
                    out=attnT_g[0:64, hp, :], in0=pvtA[0:64, :], in1=rb_s[0:64, :], op=AO.mult),
                    inc=sv, wait=[(ld, lrb), (sq, sq.n)])
                vn = P.emit("vector", lambda e, hp=hp, pvtB=pvtB: e.tensor_tensor(
                    out=attnT_g[64:128, hp, :], in0=pvtB[0:64, :], in1=rb_s[64:128, :], op=AO.mult),
                    inc=sv)
                RB_GUARD[0] = (sv, vn)
                last_norm_pv[pvset] = vn
                if "attnT" in taps:
                    vcp = P.emit("vector", lambda e, hp=hp: e.tensor_copy(out=zu[:, 0:512], in_=attnT_g[:, hp, :]),
                                 inc=sv, wait=[(sv, vn)])
                    P.emit("gpsimd", lambda e, s=s, g=g, hp=hp: e.dma_start(
                        out=tap_t["attnT"][s, g, 128 * hp:128 * (hp + 1), :], in_=zu[:, 0:512]),
                        inc=st, inc_by=16, wait=[(sv, vcp)])

            FD_HOOKS = []
            if g + 1 < NG:
                L = len(chunk_list)
                for i in range(3):
                    FD_HOOKS.append((max(1, (L * (i + 1)) // 5), 4 * (g + 1) + i, i))

            mq_pend = emit_scores(0, extra_wait=[(sv, sv.n), (sq, sq.n), (sa, sa.n)])
            for idx, (hp, kc) in enumerate(chunk_list):
                while FD_HOOKS and idx == FD_HOOKS[0][0]:
                    _, t2, sl = FD_HOOKS.pop(0)
                    front_dve(t2, sl)
                hA, hB = 2 * hp, 2 * hp + 1
                pvset = hp % 2
                pvtA = pA_v[0:65, 0:512] if pvset == 0 else pA_o[0:65, 0:512]
                pvtB = pA_v[0:65, 512:1024] if pvset == 0 else pA_oB
                buf = kc % 2
                sPair = (pA_q, pA_k)[buf]
                PPd = (PP0, PP1)[buf]
                mq = mq_pend
                diag = (kc >= 4 * g)
                wexp = [(sq, mq), (ld, NM_WAIT[(s, g)][0]), (sv, NM_WAIT[(s, g)][1])]
                aB = P.emit("scalar", lambda e, sPair=sPair, hp=hp, dst=PPd: e.activation(
                    out=dst[:], in_=sPair[:, 0:1024], func=AF.Exp, scale=0.125, bias=negmp[:, hp:hp + 1]),
                    inc=sa, wait=wexp)
                aA = aB
                last_exp_parity[buf] = aB
                if idx + 1 < len(chunk_list):
                    mq_pend = emit_scores(idx + 1)
                pw = [(sa, aB)]
                if diag:
                    j = kc - 4 * g
                    vmB = P.emit("vector", lambda e, j=j, PPd=PPd: e.tensor_tensor(
                        out=PPd[:].rearrange("p (two n) -> p two n", n=512),
                        in0=PPd[:].rearrange("p (two n) -> p two n", n=512),
                        in1=dmask_b[:, 512 * j:512 * (j + 1)].unsqueeze(1).transpose([0, 2, 1]).broadcast_to([128, 2, 512])
                        if False else dmask_b[:, 512 * j:512 * (j + 1)].unsqueeze(1).broadcast_to([128, 2, 512]),
                        op=AO.mult),
                        inc=sv, wait=[(sa, aB)])
                    pw = [(sv, vmB)]
                if kc == 0:
                    pw = pw + [(sv, last_norm_pv[pvset])]
                P.emit("tensor", lambda e, kc=kc, hA=hA, PPd=PPd, pvtA=pvtA,
                       st_=(kc == 0), sp_=(kc == nkc - 1): e.matmul(
                    pvtA, Vp[:, kc, 65 * hA:65 * hA + 65], PPd[:, 0:512],
                    start=st_, stop=sp_),
                    inc=(sq if kc == nkc - 1 else None), wait=pw)
                P.emit("tensor", lambda e, kc=kc, hB=hB, PPd=PPd, pvtB=pvtB,
                       st_=(kc == 0), sp_=(kc == nkc - 1): e.matmul(
                    pvtB, Vp[:, kc, 65 * hB:65 * hB + 65], PPd[:, 512:1024],
                    start=st_, stop=sp_),
                    inc=(sq if kc == nkc - 1 else None))
                if kc == nkc - 1:
                    PEND_TAIL.append((hp, pvset, pvtA, pvtB, sq.n))
                if PEND_TAIL and (kc == min(2, nkc - 1) or idx == len(chunk_list) - 1):
                    if PEND_TAIL[0][0] != hp or idx == len(chunk_list) - 1:
                        thp, tset, tA, tB, tpv = PEND_TAIL.pop(0)
                        emit_pair_tail(thp, tset, tA, tB, tpv)
            while PEND_TAIL:
                thp, tset, tA, tB, tpv = PEND_TAIL.pop(0)
                emit_pair_tail(thp, tset, tA, tB, tpv)
            v_attn = sv.n

            for tt in range(4):
                t = 4 * g + tt
                lxr = P.emit("sync", lambda e, s=s, t=t: e.dma_start(
                    out=x_t[:], in_=xs[s, 128 * t:128 * (t + 1), :]),
                    inc=ld, inc_by=16, wait=[(sv, sv.n), (sa, sa.n)])
                for o2 in range(2):
                    for hp in range(8):
                        P.emit("tensor", lambda e, o2=o2, hp=hp, tt=tt: e.matmul(
                            pA_q[:, 512 * o2:512 * (o2 + 1)],
                            attnT_g[:, hp, 128 * tt:128 * (tt + 1)],
                            WoT[:, hp, 512 * o2:512 * (o2 + 1)],
                            start=(hp == 0), stop=(hp == 7)),
                            inc=(sq if hp == 7 else None),
                            wait=([(sv, v_attn), (sa, sa.n), (sv, sv.n)] if (hp == 0 and o2 == 0) else []))
                mo = sq.n
                P.emit("vector", lambda e: e.tensor_tensor(out=zu[:], in0=pA_q[:, :], in1=bo_s[:], op=AO.add),
                       inc=sv, wait=[(sq, mo), (st, st.n)])
                vx1 = P.emit("vector", lambda e: e.tensor_tensor(out=zu[:], in0=zu[:], in1=x_t[:], op=AO.add),
                             inc=sv, wait=[(ld, lxr)])
                P.emit("gpsimd", lambda e, s=s, t=t: e.dma_start(
                    out=xmid[s, 128 * t:128 * (t + 1), :], in_=zu[:]),
                    inc=st, inc_by=16, wait=[(sv, vx1)])
                if "x1" in taps:
                    P.emit("gpsimd", lambda e, s=s, t=t: e.dma_start(
                        out=tap_t["x1"][s, 128 * t:128 * (t + 1), :], in_=zu[:]),
                        inc=st, inc_by=16, wait=[(sv, vx1)])

    for s in range(NSEQ):
        phase_a_seq(s)
    pa.close()
    watt.close()
    barrierA = [(sv, sv.n), (sq, sq.n), (st, st.n), (ld, ld.n), (sa, sa.n)]

    if stop_after == "phaseA":
        P.emit("gpsimd", lambda e: e.nop(), wait=barrierA)
        finalize(nc, P)
        return nc, stack

    pb = contextlib.ExitStack()
    qsb = lambda name, shape, dtype: pb.enter_context(nc.sbuf_tensor(name, list(shape), dtype))
    wgqT = qsb("wgqT", (128, 8, I_FFN), BF16)
    wuqT = qsb("wuqT", (128, 8, I_FFN), BF16)
    wdqT = qsb("wdqT", (128, 22, D), BF16)
    xb_t = qsb("xb_t", (128, D), F32)
    xb_t2 = qsb("xb_t2", (128, D), F32)
    h2_f = qsb("h2_f", (128, D), F32)
    mu_all = qsb("mu_all", (128, NSEQ * NT), F32)
    rstd_all = qsb("rstd_all", (128, NSEQ * NT), F32)
    h2qT = qsb("h2qT", (128, 8, 128), BF16)
    h1qT = qsb("h1qT", (128, 8, 128), BF16)
    gsil = qsb("gsil", (128, 1024), F32)
    h1_s = qsb("h1_s", (128, 1024), F32)
    ob_t = qsb("ob_t", (128, D), F32)
    ob_t2 = qsb("ob_t2", (128, D), F32)
    b_sq = qsb("b_sq", (128, 2 * D), F32)
    b_s1 = qsb("b_s1", (128, 2), F32); b_s2 = qsb("b_s2", (128, 2), F32)
    b_mu = qsb("b_mu", (128, 1), F32); b_rstd = qsb("b_rstd", (128, 1), F32)
    q_out = qsb("q_out", (128, D), BF16)
    q_ax2 = qsb("q_ax2", (128, D), F32)
    q_t2 = qsb("q_t2", (128, D), F32)
    q_y = qsb("q_y", (128, D), F32)
    q_y2 = qsb("q_y2", (128, D), F32)
    q_sg = qsb("q_sg", (128, D), BF16)
    q_m = qsb("q_m", (128, D), BF16)
    q_bm = qsb("q_bm", (128, D // 16), F32)
    q_sc = qsb("q_sc", (128, D // 16), F32)
    q_rs = qsb("q_rs", (128, D // 16), F32)
    q_s2 = qsb("q_s2", (128, D // 16), BF16)
    b_scr = (q_ax2, q_t2, q_y, q_y2, q_sg, q_m, q_bm, q_sc, q_rs, q_s2)

    pB_g = pb.enter_context(nc.psum_tensor("pB_g", [128, 1024], F32))
    pB_u = pb.enter_context(nc.psum_tensor("pB_u", [128, 1024], F32))
    pB_t = pb.enter_context(nc.psum_tensor("pB_t", [128, 512], BF16))
    pB_d = pb.enter_context(nc.psum_tensor("pB_d", [128, 1024], F32))

    for eng in ENGINES:
        P.emit(eng, lambda e: e.nop(), wait=barrierA)
    for dst, off, ch, nrows in ((wgqT, WGT_OFF, 8, I_FFN), (wuqT, WUT_OFF, 8, I_FFN), (wdqT, WDT_OFF, 22, D)):
        P.emit("sync", lambda e, dst=dst, off=off, ch=ch, nr=nrows: e.dma_start(
            out=dst[:], in_=wq_scr[off:off + 128 * ch * nr].rearrange("(p c n) -> p c n", p=128, c=ch)),
            inc=ld, inc_by=16)
    ld_wq = ld.n

    scr_guard = [None]
    OD_SLICES = [(0, 1024), (1024, 1024), (2048, 768)]

    def phase_b_prologue():
        NBT = 2
        for it in range(NSEQ * NT // NBT):
            ti0 = it * NBT
            s0, t0_ = ti0 // NT, ti0 % NT
            lx = P.emit("sync", lambda e, s0=s0, t0_=t0_: e.dma_start(
                out=b_sq[:].rearrange("p (b n) -> p b n", n=D),
                in_=xmid[s0, 128 * t0_:128 * (t0_ + NBT), :].rearrange("(b q) n -> q b n", b=NBT)),
                inc=ld, inc_by=16, wait=([(ld, ld_wq)] if it == 0 else []) + [(sv, sv.n), (sa, sa.n)])
            vmu = P.emit("vector", lambda e, ti0=ti0: e.reduce_sum(
                out=mu_all[:, ti0:ti0 + NBT], in_=b_sq[:].rearrange("p (b n) -> p b n", n=D), axis=AX.X),
                inc=sv, wait=[(ld, lx)])
            a0 = P.emit("scalar", lambda e: e.activation(out=b_sq[:], in_=b_sq[:], func=AF.Square),
                        inc=sa, wait=[(ld, lx), (sv, vmu)])
            P.emit("vector", lambda e: e.reduce_sum(
                out=b_s2[:, 0:NBT], in_=b_sq[:].rearrange("p (b n) -> p b n", n=D), axis=AX.X),
                inc=sv, wait=[(sa, a0), (sv, sv.n)])
            P.emit("vector", lambda e, ti0=ti0: e.tensor_scalar_mul(
                out=mu_all[:, ti0:ti0 + NBT], in0=mu_all[:, ti0:ti0 + NBT], scalar1=1.0 / D),
                inc=sv, wait=[(sv, sv.n)])
            P.emit("vector", lambda e: e.tensor_scalar_mul(out=b_s2[:, 0:NBT], in0=b_s2[:, 0:NBT], scalar1=1.0 / D),
                   inc=sv, wait=[(sv, sv.n)])
            P.emit("vector", lambda e, ti0=ti0: e.tensor_tensor(
                out=b_s1[:, 0:NBT], in0=mu_all[:, ti0:ti0 + NBT], in1=mu_all[:, ti0:ti0 + NBT], op=AO.mult),
                inc=sv, wait=[(sv, sv.n)])
            P.emit("vector", lambda e: e.tensor_tensor(out=b_s1[:, 0:NBT], in0=b_s1[:, 0:NBT], in1=b_s2[:, 0:NBT], op=AO.subtract),
                   inc=sv, wait=[(sv, sv.n)])
            P.emit("vector", lambda e: e.tensor_scalar(out=b_s1[:, 0:NBT], in0=b_s1[:, 0:NBT], scalar1=-1.0, scalar2=1e-5,
                                                       op0=AO.mult, op1=AO.add), inc=sv, wait=[(sv, sv.n)])
            vr3 = P.emit("vector", lambda e: e.reciprocal(out=b_s2[:, 0:NBT], in_=b_s1[:, 0:NBT]), inc=sv, wait=[(sv, sv.n)])
            P.emit("scalar", lambda e, ti0=ti0: e.activation(out=rstd_all[:, ti0:ti0 + NBT], in_=b_s2[:, 0:NBT], func=AF.Sqrt),
                   inc=sa, wait=[(sv, vr3)])

    VO_H = []
    SO_H = []

    def phase_b_tile(s, t):
        ti = s * NT + t
        xb = (xb_t, xb_t2)[ti % 2]
        w_lx = [(sv, VO_H[-2])] if len(VO_H) >= 2 else [(sv, sv.n), (sa, sa.n)]
        lx = P.emit("sync", lambda e, s=s, t=t, xb=xb: e.dma_start(
            out=xb[:], in_=xmid[s, 128 * t:128 * (t + 1), :]),
            inc=ld, inc_by=16, wait=w_lx)
        if "xb" in taps:
            P.emit("gpsimd", lambda e, s=s, t=t, xb=xb: e.dma_start(
                out=tap_t["xb"][s, 128 * t:128 * (t + 1), :], in_=xb[:]),
                inc=st, inc_by=16, wait=[(ld, lx)])
        ah = P.emit("vector", lambda e, ti=ti, xb=xb: e.tensor_scalar(
            out=h2_f[:], in0=xb[:], scalar1=mu_all[:, ti:ti + 1], scalar2=rstd_all[:, ti:ti + 1],
            op0=AO.subtract, op1=AO.mult),
            inc=sv, wait=[(ld, lx)])
        _, vq2p = emit_quant(b_scr, h2_f[:], D, q_out[:], h2_f,
                             wv=[(sv, ah)], wa=[(sv, ah)], w_final=[(sq, sq.n)],
                             scratch_guard=scr_guard[0], split=PHASEB_SPLIT, rscale_act=False)
        scr_guard[0] = vq2p
        if "h2q" in taps:
            vcp = P.emit("vector", lambda e: e.tensor_copy(out=q_t2[:], in_=q_out[:]), inc=sv, wait=[vq2p])
            P.emit("gpsimd", lambda e, s=s, t=t: e.dma_start(
                out=tap_t["h2q"][s, 128 * t:128 * (t + 1), :], in_=q_t2[:]),
                inc=st, inc_by=16, wait=[(sv, vcp)])
        for cq in range(2):
            for k in range(4):
                c = 4 * cq + k
                P.emit("tensor", lambda e, c=c, k=k: e.transpose(
                    pB_t[:, 128 * k:128 * (k + 1)], q_out[:, 128 * c:128 * (c + 1)], ident_b[:]),
                    inc=(sq if k == 3 else None),
                    wait=([(sv, sv.n), (sq, sq.n)] if k == 0 else []))
            tq = sq.n
            P.emit("vector", lambda e, cq=cq: e.tensor_copy(
                out=h2qT[:, 4 * cq:4 * cq + 4, :], in_=pB_t[:].rearrange("p (k n) -> p k n", n=128)),
                inc=sv, wait=[(sq, tq)])
        v_h2qT = sv.n

        first_dn = True
        def emit_gu(si, wv_extra):
            o0, ow = OD_SLICES[si]
            mw = [(sv, v_h2qT)] + wv_extra
            for pg, pt in ((pB_g, wgqT), (pB_u, wuqT)):
                for hh in range(0, ow, 512):
                    hw = min(512, ow - hh)
                    for c in range(8):
                        P.emit("tensor", lambda e, c=c, o0=o0, hh=hh, hw=hw, pg=pg, pt=pt: e.matmul(
                            pg[:, hh:hh + hw], h2qT[:, c, :], pt[:, c, o0 + hh:o0 + hh + hw],
                            start=(c == 0), stop=(c == 7)),
                            inc=(sq if c == 7 else None), wait=(mw if c == 0 else []))
                        mw = []
            return sq.n
        mgu = emit_gu(0, [(sa, sa.n), (sv, sv.n)])
        for si, (o0, ow) in enumerate(OD_SLICES):
            ag = P.emit("scalar", lambda e, ow=ow: e.activation(out=gsil[:, 0:ow], in_=pB_g[:, 0:ow], func=AF.Silu),
                        inc=sa, wait=[(sq, mgu)])
            vh1 = P.emit("vector", lambda e, ow=ow: e.tensor_tensor(
                out=h1_s[:, 0:ow], in0=gsil[:, 0:ow], in1=pB_u[:, 0:ow], op=AO.mult),
                inc=sv, wait=[(sa, ag)])
            if si + 1 < len(OD_SLICES):
                mgu = emit_gu(si + 1, [(sa, ag), (sv, vh1)])
            if "h1" in taps:
                P.emit("gpsimd", lambda e, s=s, t=t, o0=o0, ow=ow: e.dma_start(
                    out=tap_t["h1"][s, 128 * t:128 * (t + 1), o0:o0 + ow], in_=h1_s[:, 0:ow]),
                    inc=st, inc_by=16, wait=[(sv, vh1)])
            _, vq1p = emit_quant(b_scr, h1_s[:, 0:ow], ow, q_out[:, 0:ow], h1_s,
                                 wv=[(sv, vh1)], wa=[(sv, vh1)], w_final=[(sq, sq.n)],
                                 scratch_guard=scr_guard[0], split=PHASEB_SPLIT, rscale_act=False)
            scr_guard[0] = vq1p
            nch = ow // 128
            for cq in range((nch + 3) // 4):
                ks = [k for k in range(4) if 4 * cq + k < nch]
                for k in ks:
                    c = 4 * cq + k
                    P.emit("tensor", lambda e, c=c, k=k: e.transpose(
                        pB_t[:, 128 * k:128 * (k + 1)], q_out[:, 128 * c:128 * (c + 1)], ident_b[:]),
                        inc=(sq if k == ks[-1] else None),
                        wait=(([(sv, sv.n), (sq, sq.n)] + ([vq1p] if cq == 0 else [])) if k == 0 else []))
                tq1 = sq.n
                P.emit("vector", lambda e, cq=cq, nk=len(ks): e.tensor_copy(
                    out=h1qT[:, 4 * cq:4 * cq + nk, :], in_=pB_t[:, 0:128 * nk].rearrange("p (k n) -> p k n", n=128)),
                    inc=sv, wait=[(sq, tq1)])
            vev = sv.n
            for k in range(nch):
                cglob = o0 // 128 + k
                for o2 in range(2):
                    P.emit("tensor", lambda e, k=k, cglob=cglob, o2=o2,
                           st_=(first_dn and k == 0),
                           sp_=(si == len(OD_SLICES) - 1 and k == nch - 1): e.matmul(
                        pB_d[:, 512 * o2:512 * (o2 + 1)], h1qT[:, k, :],
                        wdqT[:, cglob, 512 * o2:512 * (o2 + 1)], start=st_, stop=sp_),
                        inc=(sq if (k == nch - 1 and o2 == 1) else None),
                        wait=([(sv, vev)] if (k == 0 and o2 == 0) else []))
            first_dn = False
        m_dn = sq.n
        ob = (ob_t, ob_t2)[ti % 2]
        w_vo = [(so, SO_H[-2])] if len(SO_H) >= 2 else [(so, so.n)]
        vo = P.emit("vector", lambda e, xb=(xb_t, xb_t2)[ti % 2], ob=ob: e.tensor_tensor(
            out=ob[:], in0=pB_d[:, :], in1=xb[:], op=AO.add),
            inc=sv, wait=[(sq, m_dn)] + w_vo)
        VO_H.append(vo)
        P.emit("gpsimd", lambda e, s=s, t=t, ob=ob: e.dma_start(
            out=out[s, 128 * t:128 * (t + 1), :], in_=ob[:]),
            inc=so, inc_by=16, wait=[(sv, vo)])
        SO_H.append(so.n)

    phase_b_prologue()
    for s in range(NSEQ):
        for t in range(NT):
            phase_b_tile(s, t)
    pb.close()

    P.emit("gpsimd", lambda e: e.nop(), wait=[(so, so.n), (st, st.n)])
    finalize(nc, P)
    return nc, stack


def finalize(nc, P):
    with nc.Block() as block:
        @block.sync
        def _(e):
            for fn in P.ops["sync"]:
                fn(e)

        @block.vector
        def _(e):
            for fn in P.ops["vector"]:
                fn(e)

        @block.scalar
        def _(e):
            for fn in P.ops["scalar"]:
                fn(e)

        @block.tensor
        def _(e):
            for fn in P.ops["tensor"]:
                fn(e)

        @block.gpsimd
        def _(e):
            for fn in P.ops["gpsimd"]:
                fn(e)


def make_in_maps(inputs, consts=None, n_cores=N_CORES):
    if consts is None:
        consts = host_consts()
    lay = host_inputs_layout(inputs)
    x = np.ascontiguousarray(np.asarray(inputs["L0_attn_x"], dtype=np.float32))
    base = {
        "Wq": np.ascontiguousarray(np.asarray(inputs["L0_attn_Wq"], dtype=np.float32)),
        "Wk": np.ascontiguousarray(np.asarray(inputs["L0_attn_Wk"], dtype=np.float32)),
        "Wv": np.ascontiguousarray(np.asarray(inputs["L0_attn_Wv"], dtype=np.float32)),
        "Wo": np.ascontiguousarray(np.asarray(inputs["L0_attn_Wo"], dtype=np.float32)),
        "wg": np.ascontiguousarray(np.asarray(inputs["L0_ffn_w_gate"], dtype=np.float32)),
        "wu": np.ascontiguousarray(np.asarray(inputs["L0_ffn_w_up"], dtype=np.float32)),
        "wd": np.ascontiguousarray(np.asarray(inputs["L0_ffn_w_down"], dtype=np.float32)),
        "c_cos": consts["c_cos"], "c_sinp": consts["c_sinp"],
        "c_ident": consts["c_ident"], "c_dmask": consts["c_dmask"],
        "gq": lay["gq"], "gqp": lay["gqp"], "bq": lay["bq"], "bqp": lay["bqp"],
        "gk": lay["gk"], "gkp": lay["gkp"], "bk": lay["bk"], "bkp": lay["bkp"],
        "bo_rep": lay["bo_rep"],
    }
    in_maps = []
    for c in range(n_cores):
        m = dict(base)
        m["xs"] = np.ascontiguousarray(x[NSEQ * c:NSEQ * (c + 1)])
        in_maps.append(m)
    return in_maps


def kernel(**inputs):
    nc, stack = build_program()
    in_maps = make_in_maps(inputs)
    res = run_bass_kernel_spmd(nc, in_maps, core_ids=list(range(N_CORES)))
    stack.close()
    return np.concatenate([np.asarray(r["out"]) for r in res.results], axis=0).astype(np.float32)


# version 27 (final)
# speedup vs baseline: 1.1811x; 1.0039x over previous
import math
import contextlib
import numpy as np
import ml_dtypes

import concourse.bass as bass
import concourse.mybir as mybir
from concourse.bass_utils import run_bass_kernel_spmd

F32 = mybir.dt.float32
BF16 = mybir.dt.bfloat16
AF = mybir.ActivationFunctionType
AO = mybir.AluOpType
AX = mybir.AxisListType

S = 2048
D = 1024
H = 16
HD = 64
NSEQ = 4
NT = S // 128
NG = 4
I_FFN = 2816
N_CORES = 8

ENGINES = ("sync", "vector", "scalar", "tensor", "gpsimd")


def host_consts():
    pos = np.arange(S, dtype=np.float32)[:, None]
    di = np.arange(0, HD, 2, dtype=np.float32) / HD
    theta = pos * np.exp(-di * math.log(10000.0))
    cos = np.repeat(np.cos(theta), 2, axis=1).astype(np.float32)
    sin = np.repeat(np.sin(theta), 2, axis=1).astype(np.float32)
    sinp = sin.copy()
    sinp[:, :32] *= -1.0
    dmask = np.zeros((4, 128, 512), dtype=np.float32)
    for j in range(4):
        p = np.arange(128)[:, None]
        f = np.arange(512)[None, :]
        dmask[j] = ((128 * j + p) <= f).astype(np.float32)
    return {
        "c_cos": cos, "c_sinp": sinp,
        "c_ident": np.eye(128, dtype=np.float32),
        "c_dmask": dmask.astype(ml_dtypes.bfloat16),
    }


def host_inputs_layout(inp):
    out = {}
    rep = lambda a: np.ascontiguousarray(
        np.broadcast_to(np.asarray(a, np.float32).reshape(1, -1), (128, np.asarray(a).size)))
    perm = np.empty(HD, dtype=np.int64)
    perm[:32] = 2 * np.arange(32) + 1
    perm[32:] = 2 * np.arange(32)
    for nm, g, b in (("q", inp["L0_attn_q_gamma"], inp["L0_attn_q_beta"]),
                     ("k", inp["L0_attn_k_gamma"], inp["L0_attn_k_beta"])):
        g = np.asarray(g, np.float32); b = np.asarray(b, np.float32)
        out[f"g{nm}"] = rep(g)
        out[f"g{nm}p"] = rep(g[perm])
        out[f"b{nm}"] = rep(b)
        out[f"b{nm}p"] = rep(b[perm])
    out["bo_rep"] = rep(inp["L0_attn_bo"])
    return out


class Sem:
    def __init__(self, stack, nc, name):
        self.h = stack.enter_context(nc.semaphore(name))
        self.n = 0


class Prog:
    def __init__(self, nc, stack):
        self.nc = nc
        self.stack = stack
        self.ops = {e: [] for e in ENGINES}

    def sem(self, name):
        return Sem(self.stack, self.nc, name)

    def emit(self, eng, fn, inc=None, inc_by=1, wait=()):
        waits = [(w[0], w[1]) if isinstance(w, tuple) else (w, w.n) for w in wait]
        if inc is not None:
            inc.n += inc_by
        post = inc.n if inc is not None else None

        def run(e, fn=fn, waits=waits, inc=inc, inc_by=inc_by):
            for s_, v_ in waits:
                if v_ > 0:
                    e.wait_ge(s_.h, v_)
            r = fn(e)
            if inc is not None:
                r.then_inc(inc.h, inc_by)
        self.ops[eng].append(run)
        return post


def build_program(taps=(), stop_after=None, PHASEB_SPLIT=False):
    nc = bass.Bass("TRN2", target_bir_lowering=False)
    stack = contextlib.ExitStack()

    def din(name, shape, dtype=F32):
        return nc.dram_tensor(name, list(shape), dtype, kind="ExternalInput").ap()

    xs = din("xs", (NSEQ, S, D))
    Wq = din("Wq", (D, D)); Wk = din("Wk", (D, D)); Wv = din("Wv", (D, D)); Wo = din("Wo", (D, D))
    wg = din("wg", (I_FFN, D)); wu = din("wu", (I_FFN, D)); wd = din("wd", (D, I_FFN))
    c_cos = din("c_cos", (S, HD)); c_sinp = din("c_sinp", (S, HD))
    c_ident = din("c_ident", (128, 128))
    c_dmask = din("c_dmask", (4, 128, 512), BF16)
    gq = din("gq", (128, HD)); gqp = din("gqp", (128, HD)); bq = din("bq", (128, HD)); bqp = din("bqp", (128, HD))
    gk = din("gk", (128, HD)); gkp = din("gkp", (128, HD)); bk = din("bk", (128, HD)); bkp = din("bkp", (128, HD))
    bo_rep = din("bo_rep", (128, D))

    out = nc.dram_tensor("out", [NSEQ, S, D], F32, kind="ExternalOutput").ap()
    xmid = nc.dram_tensor("xmid", [NSEQ, S, D], F32, kind="Internal").ap()
    rs_scr = nc.dram_tensor("rs_scr", [H, 512], F32, kind="Internal").ap()
    nm_scr = nc.dram_tensor("nm_scr", [H], F32, kind="Internal").ap()
    wq_scr = nc.dram_tensor("wq_scr", [2 * 128 * 8 * I_FFN + 128 * 22 * D], BF16, kind="Internal").ap()
    WGT_OFF = 0
    WUT_OFF = 128 * 8 * I_FFN
    WDT_OFF = 2 * 128 * 8 * I_FFN

    taps = set(taps)
    tap_t = {}
    for nm, shp in (("h", [NSEQ, S, D]), ("qr", [NSEQ, S, D]), ("kr", [NSEQ, S, D]),
                    ("attnT", [NSEQ, NG, 1024, 512]), ("x1", [NSEQ, S, D]),
                    ("h2q", [NSEQ, S, D]), ("h1", [NSEQ, S, I_FFN]), ("xb", [NSEQ, S, D])):
        if nm in taps:
            tap_t[nm] = nc.dram_tensor("tap_" + nm, shp, F32, kind="ExternalOutput").ap()

    P = Prog(nc, stack)
    ld = P.sem("ld")
    st = P.sem("st")
    sv = P.sem("sv")
    sa = P.sem("sa")
    sq = P.sem("sq")
    so = P.sem("so")
    sp = P.sem("sp")

    sb = lambda name, shape, dtype: stack.enter_context(nc.sbuf_tensor(name, list(shape), dtype))
    ident_f = sb("ident_f", (128, 128), F32)
    ident_b = sb("ident_b", (128, 128), BF16)
    dmask_b = sb("dmask_b", (128, 4 * 512), BF16)
    gq_s = sb("gq_s", (128, HD), F32); gqp_s = sb("gqp_s", (128, HD), F32)
    bq_s = sb("bq_s", (128, HD), F32); bqp_s = sb("bqp_s", (128, HD), F32)
    gk_s = sb("gk_s", (128, HD), F32); gkp_s = sb("gkp_s", (128, HD), F32)
    bk_s = sb("bk_s", (128, HD), F32); bkp_s = sb("bkp_s", (128, HD), F32)
    bo_s = sb("bo_s", (128, D), F32)
    negm_bc = sb("negm_bc", (128, H), F32)
    negmp = sb("negmp", (128, H // 2), F32)

    def ldc(dst, src):
        P.emit("sync", lambda e, d=dst, s=src: e.dma_start(out=d, in_=s), inc=ld, inc_by=16)
    ldc(ident_f[:], c_ident[:, :])
    ldc(dmask_b[:].rearrange("p (j n) -> p j n", n=512), c_dmask[:, :, :].transpose([1, 0, 2]))
    for d_, s_ in ((gq_s, gq), (gqp_s, gqp), (bq_s, bq), (bqp_s, bqp),
                   (gk_s, gk), (gkp_s, gkp), (bk_s, bk), (bkp_s, bkp), (bo_s, bo_rep)):
        ldc(d_[:], s_[:, :])
    ld_consts = ld.n
    v_identb = P.emit("vector", lambda e: e.tensor_copy(out=ident_b[:], in_=ident_f[:]),
                      inc=sv, wait=[(ld, ld_consts)])

    watt = contextlib.ExitStack()
    WqT = watt.enter_context(nc.sbuf_tensor("WqT", [128, 8, D], BF16))
    WkT = watt.enter_context(nc.sbuf_tensor("WkT", [128, 8, D], BF16))
    WvT = watt.enter_context(nc.sbuf_tensor("WvT", [128, 8, D], BF16))
    WoT = watt.enter_context(nc.sbuf_tensor("WoT", [128, 8, D], BF16))

    p0 = contextlib.ExitStack()
    wstage = p0.enter_context(nc.sbuf_tensor("wstage", [128, 8, D], F32))
    wcast = p0.enter_context(nc.sbuf_tensor("wcast", [128, 8, D], BF16))
    ptw = p0.enter_context(nc.psum_tensor("ptw", [128, 512], BF16))

    for wsrc, wdstT in ((Wq, WqT), (Wk, WkT), (Wv, WvT), (Wo, WoT)):
        l0 = P.emit("sync", lambda e, w=wsrc: e.dma_start(
            out=wstage[:], in_=w[:, :].rearrange("(r p) n -> p r n", p=128)),
            inc=ld, inc_by=16, wait=[(sv, sv.n)])
        cv = P.emit("vector", lambda e: e.tensor_copy(out=wcast[:], in_=wstage[:]),
                    inc=sv, wait=[(ld, l0), (sv, v_identb), (sq, sq.n)])
        for r in range(8):
            for cq in range(2):
                w_first = [(sv, cv), (sv, sv.n)]
                for k in range(4):
                    c = 4 * cq + k
                    P.emit("tensor", lambda e, r=r, c=c, k=k: e.transpose(
                        ptw[:, 128 * k:128 * (k + 1)],
                        wcast[:, r, 128 * c:128 * (c + 1)], ident_b[:]),
                        inc=(sq if k == 3 else None),
                        wait=(w_first if k == 0 else []))
                tq = sq.n
                P.emit("vector", lambda e, w=wdstT, r=r, cq=cq: e.tensor_copy(
                    out=w[:, 4 * cq:4 * cq + 4, 128 * r:128 * (r + 1)],
                    in_=ptw[:].rearrange("p (k n) -> p k n", n=128)),
                    inc=sv, wait=[(sq, tq)])

    wbar = p0.enter_context(nc.sbuf_tensor("wbar", [128, 8, H], F32))
    for wT in (WqT, WkT):
        vr = P.emit("vector", lambda e, wT=wT: e.tensor_reduce(
            out=wbar[:], in_=wT[:].rearrange("p c (h j) -> p c h j", j=64),
            axis=AX.X, op=AO.add), inc=sv, wait=[(sq, sq.n), (sv, sv.n)])
        P.emit("vector", lambda e: e.tensor_scalar_mul(out=wbar[:], in0=wbar[:], scalar1=1.0 / 64),
               inc=sv, wait=[(sv, vr)])
        P.emit("vector", lambda e, wT=wT: e.tensor_tensor(
            out=wT[:].rearrange("p c (h j) -> p c h j", j=64),
            in0=wT[:].rearrange("p c (h j) -> p c h j", j=64),
            in1=wbar[:].unsqueeze(3).broadcast_to([128, 8, H, 64]), op=AO.subtract),
            inc=sv, wait=[(sv, sv.n)])

    p0.close()
    fqs = contextlib.ExitStack()
    f_ins = [fqs.enter_context(nc.sbuf_tensor(f"f_in{pi}", [128, I_FFN], F32)) for pi in range(2)]
    f_qs = [fqs.enter_context(nc.sbuf_tensor(f"f_q{pi}", [128, I_FFN], BF16)) for pi in range(2)]
    f_ax2 = fqs.enter_context(nc.sbuf_tensor("f_ax2", [128, I_FFN], F32))
    f_t2 = fqs.enter_context(nc.sbuf_tensor("f_t2", [128, I_FFN], F32))
    f_y = fqs.enter_context(nc.sbuf_tensor("f_y", [128, I_FFN], F32))
    f_y2 = fqs.enter_context(nc.sbuf_tensor("f_y2", [128, I_FFN], F32))
    f_sg = fqs.enter_context(nc.sbuf_tensor("f_sg", [128, I_FFN], BF16))
    f_m = fqs.enter_context(nc.sbuf_tensor("f_m", [128, I_FFN], BF16))
    f_bm = fqs.enter_context(nc.sbuf_tensor("f_bm", [128, I_FFN // 16], F32))
    f_sc = fqs.enter_context(nc.sbuf_tensor("f_sc", [128, I_FFN // 16], F32))
    f_rs = fqs.enter_context(nc.sbuf_tensor("f_rs", [128, I_FFN // 16], F32))
    f_s2 = fqs.enter_context(nc.sbuf_tensor("f_s2", [128, I_FFN // 16], BF16))
    f_scr = (f_ax2, f_t2, f_y, f_y2, f_sg, f_m, f_bm, f_sc, f_rs, f_s2)
    f_ev = fqs.enter_context(nc.sbuf_tensor("f_ev", [128, 512], BF16))
    ptf = fqs.enter_context(nc.psum_tensor("ptf", [128, 512], BF16))

    def emit_staircase(eng, sem, t2_ap, y_ap, y2_ap, n, prot, wait_first=()):
        i1 = y2_ap.bitcast(mybir.dt.int32)
        pw = (lambda: [(sem, sem.n)]) if prot else (lambda: [])
        P.emit(eng, lambda e: e.tensor_copy(out=i1, in_=t2_ap), inc=sem, wait=pw() + list(wait_first))
        P.emit(eng, lambda e: e.tensor_scalar(out=y_ap, in0=i1, scalar1=4.0, scalar2=0.5,
                                              op0=AO.min, op1=AO.mult), inc=sem, wait=pw() or [(sem, sem.n)])
        P.emit(eng, lambda e: e.tensor_scalar(out=i1, in0=t2_ap, scalar1=0.5, scalar2=-2.0,
                                              op0=AO.mult, op1=AO.add), inc=sem, wait=pw() or [(sem, sem.n)])
        P.emit(eng, lambda e: e.tensor_scalar(out=y2_ap, in0=i1, scalar1=0.0, scalar2=2.0,
                                              op0=AO.max, op1=AO.min), inc=sem, wait=pw() or [(sem, sem.n)])
        P.emit(eng, lambda e: e.tensor_tensor(out=y_ap, in0=y_ap, in1=y2_ap, op=AO.add),
               inc=sem, wait=pw() or [(sem, sem.n)])
        P.emit(eng, lambda e: e.tensor_scalar(out=i1, in0=t2_ap, scalar1=0.5, scalar2=-4.5,
                                              op0=AO.mult, op1=AO.add), inc=sem, wait=pw() or [(sem, sem.n)])
        P.emit(eng, lambda e: e.tensor_scalar(out=y2_ap, in0=i1, scalar1=0.0, scalar2=1.0,
                                              op0=AO.max, op1=AO.min), inc=sem, wait=pw() or [(sem, sem.n)])
        P.emit(eng, lambda e: e.scalar_tensor_tensor(out=y_ap, in0=y2_ap, scalar=2.0, in1=y_ap,
                                                     op0=AO.mult, op1=AO.add), inc=sem, wait=pw() or [(sem, sem.n)])

    def emit_quant(scr, src_ap, n, dst_flat_bf, sign_src, wv, wa, w_final=(), split=True, scratch_guard=None, rscale_act=True):
        ax2, t2_, y_, y2_, sg_, m_, bm_, sc_, rs_, s2_ = scr
        nb = n // 16
        peng, psem = ("gpsimd", sp) if split else ("vector", sv)
        gw = ([scratch_guard] if scratch_guard else [(sp, sp.n)]) if split else ([scratch_guard] if scratch_guard else [])
        a0 = P.emit("scalar", lambda e: e.activation(out=ax2[:, :n], in_=src_ap, func=AF.Abs, scale=2.0),
                    inc=sa, wait=list(wa) + gw)
        P.emit("vector", lambda e: e.tensor_reduce(
            out=bm_[:, :nb], in_=src_ap.rearrange("p (b k) -> p b k", k=16),
            axis=AX.X, op=AO.max, apply_absolute_value=True), inc=sv, wait=list(wv) + gw)
        P.emit("vector", lambda e: e.tensor_scalar(out=sc_[:, :nb], in0=bm_[:, :nb],
                                                   scalar1=1.0 / 6.0, scalar2=1e-12,
                                                   op0=AO.mult, op1=AO.max), inc=sv, wait=[(sv, sv.n)] + gw)
        sc_done = sv.n
        if rscale_act:
            aL = P.emit("scalar", lambda e: e.activation(out=rs_[:, :nb], in_=sc_[:, :nb], func=AF.Ln),
                        inc=sa, wait=[(sv, sc_done)] + gw)
            aE = P.emit("scalar", lambda e: e.activation(out=rs_[:, :nb], in_=rs_[:, :nb], func=AF.Exp, scale=-1.0),
                        inc=sa, wait=[(sa, aL)])
            rs_ready = (sa, aE)
        else:
            vre = P.emit("vector", lambda e: e.reciprocal(out=rs_[:, :nb], in_=sc_[:, :nb]),
                         inc=sv, wait=[(sv, sc_done)] + gw)
            rs_ready = (sv, vre)
            aE = sa.n
        P.emit(peng, lambda e: e.tensor_scalar_mul(out=s2_[:, :nb], in0=sc_[:, :nb], scalar1=2.0),
               inc=psem, wait=[(sv, sc_done)] + ([(psem, psem.n)] if True else []))
        t2p = P.emit(peng, lambda e: e.tensor_tensor(
            out=t2_.rearrange("p (b k) -> p b k", k=16)[:, 0:nb, :],
            in0=ax2.rearrange("p (b k) -> p b k", k=16)[:, 0:nb, :],
            in1=rs_[:, 0:nb].unsqueeze(2).broadcast_to([128, nb, 16]), op=AO.mult),
            inc=psem, wait=[rs_ready] + ([(psem, psem.n)] if split else []))
        P.emit(peng, lambda e: e.tensor_scalar(out=sg_[:, :n], in0=sign_src[:, :n], scalar1=0.0, scalar2=0.5,
                                               op0=AO.is_gt, op1=AO.subtract),
               inc=psem, wait=(list(wa) if split else []))
        sgp = psem.n
        emit_staircase("vector", sv, t2_[:, :n], y_[:, :n], y2_[:, :n], n,
                       prot=(n < 512), wait_first=([(psem, t2p)] if split else []) + gw)
        y_done = sv.n
        P.emit(peng, lambda e: e.tensor_tensor(out=m_[:, :n], in0=y_[:, :n], in1=sg_[:, :n], op=AO.mult),
               inc=psem, wait=([(sv, y_done), (psem, sgp)] if split else ([(sv, sv.n)] if n < 512 else [])))
        vqp = P.emit(peng, lambda e: e.tensor_tensor(
            out=dst_flat_bf.rearrange("p (b k) -> p b k", k=16)[:, 0:nb, :],
            in0=m_.rearrange("p (b k) -> p b k", k=16)[:, 0:nb, :],
            in1=s2_[:, 0:nb].unsqueeze(2).broadcast_to([128, nb, 16]), op=AO.mult),
            inc=psem, wait=list(w_final) + ([(psem, psem.n)] if split else ([(sv, sv.n)] if n < 512 else [])))
        return (None, (psem, vqp))

    f_scr = (f_ax2, f_t2, f_y, f_y2, f_sg, f_m, f_bm, f_sc, f_rs, f_s2)

    F_GUARD = [(sq, sq.n), (sq, sq.n)]
    ABS_GUARD = [(sv, sv.n)]
    SCR_GUARD = [(sa, sa.n)]

    def quant_weight(wsrc, n_rows, n_cols, dst_off, tchunks, chain_idx):
        for r in range(n_rows // 128):
            pi = chain_idx[0] % 2
            chain_idx[0] += 1
            f_in = f_ins[pi]
            f_q = f_qs[pi]
            gwait = [F_GUARD[pi], (sv, sv.n if chain_idx[0] <= 2 else 0)] if chain_idx[0] <= 2 else [F_GUARD[pi]]
            l0 = P.emit("sync", lambda e, r=r, f_in=f_in: e.dma_start(
                out=f_in[:, :n_cols], in_=wsrc[128 * r:128 * (r + 1), :]),
                inc=ld, inc_by=16, wait=gwait)
            _, vqp = emit_quant(f_scr, f_in[:, :n_cols], n_cols, f_q[:, :n_cols], f_in,
                                wv=[(ld, l0)], wa=([(ld, l0)] + [ABS_GUARD[0]] if ABS_GUARD[0] else [(ld, l0)]),
                                split=False, scratch_guard=SCR_GUARD[0])
            ABS_GUARD[0] = (sv, sv.n)
            SCR_GUARD[0] = (sa, sa.n)
            ncol_ch = n_cols // 128
            for cq in range((ncol_ch + 3) // 4):
                ks = [k for k in range(4) if 4 * cq + k < ncol_ch]
                for k in ks:
                    c = 4 * cq + k
                    P.emit("tensor", lambda e, c=c, k=k, f_q=f_q: e.transpose(
                        ptf[:, 128 * k:128 * (k + 1)],
                        f_q[:, 128 * c:128 * (c + 1)], ident_b[:]),
                        inc=(sq if k == ks[-1] else None),
                        wait=(([(sv, sv.n), vqp]) if k == 0 else []))
                tq = sq.n
                ev = P.emit("vector", lambda e, nk=len(ks): e.tensor_copy(
                    out=f_ev[:, :128 * nk], in_=ptf[:, :128 * nk]),
                    inc=sv, wait=[(sq, tq), (st, st.n)])

                def store(e, cq=cq, nk=len(ks), r=r, dst_off=dst_off, tchunks=tchunks, nrows=n_rows):
                    dst = wq_scr[dst_off:dst_off + 128 * tchunks * nrows].rearrange(
                        "(p c n) -> p c n", p=128, c=tchunks)
                    return e.dma_start(out=dst[:, 4 * cq:4 * cq + nk, 128 * r:128 * (r + 1)],
                                       in_=f_ev[:, :128 * nk].rearrange("p (k n) -> p k n", n=128))
                P.emit("gpsimd", store, inc=st, inc_by=16, wait=[(sv, ev)])
            F_GUARD[pi] = (sq, sq.n)

    _ci = [0]
    quant_weight(wg, I_FFN, D, WGT_OFF, 8, _ci)
    quant_weight(wu, I_FFN, D, WUT_OFF, 8, _ci)
    quant_weight(wd, D, I_FFN, WDT_OFF, 22, _ci)
    fqs.close()
    barrier0 = [(sv, sv.n), (sq, sq.n), (st, st.n), (ld, ld.n), (sa, sa.n)]

    pa = contextlib.ExitStack()
    psb = lambda name, shape, dtype: pa.enter_context(nc.sbuf_tensor(name, list(shape), dtype))
    kT_all = psb("kT_all", (128, 8, S), BF16)
    Vp = psb("Vp", (128, NT, H * 65), BF16)
    qT_g = psb("qT_g", (128, 8, 512), BF16)
    hT_g = psb("hT_g", (128, 8, 512), BF16)
    attnT_g = psb("attnT_g", (128, 8, 512), BF16)
    x_t = psb("x_t", (128, D), F32)
    h_bf = psb("h_bf", (128, D), BF16)
    h_bf4 = [h_bf] + [psb(f"h_bf{i}", (128, D), BF16) for i in (1, 2)] + [h_bf]
    qkp_bf = psb("qkp_bf", (128, D), BF16)
    z2 = psb("z2", (128, D), F32)
    zu = psb("zu", (128, D), F32)
    zv = psb("zv", (128, D), F32)
    zqc0 = psb("zqc0", (128, D), BF16); zkc0 = psb("zkc0", (128, D), BF16)
    zqc1 = psb("zqc1", (128, D), BF16); zkc1 = psb("zkc1", (128, D), BF16)
    cos_t = psb("cos_t", (128, HD), F32)
    sinp_t = psb("sinp_t", (128, HD), F32)
    A_t = psb("A_t", (128, HD), F32); B_t = psb("B_t", (128, HD), F32)
    D_t = psb("D_t", (128, HD), F32); e1_t = psb("e1_t", (128, HD), F32)
    PP0 = psb("PP0", (128, 1024), BF16)
    PP1 = psb("PP1", (128, 1024), BF16)
    rb_s = psb("rb_s", (128, 512), F32)
    rs2_s = psb("rs2_s", (1, 1024), F32)
    rsA = rs2_s[:, 0:512]
    rsB = rs2_s[:, 512:1024]
    st16 = psb("st16", (128, 16), F32)
    st16b = psb("st16b", (128, 16), F32)
    mu_h = psb("mu_h", (128, 16), F32)
    rstd_h = psb("rstd_h", (128, 16), F32)
    qm2_a = psb("qm2_a", (128, 16), F32); qm2_b = psb("qm2_b", (128, 16), F32)
    km2_a = psb("km2_a", (128, 16), F32); km2_b = psb("km2_b", (128, 16), F32)
    s1_c = psb("s1_c", (128, 1), F32)
    s2_c = psb("s2_c", (128, 1), F32)
    mu_c = psb("mu_c", (128, 1), F32)
    rstd_c = psb("rstd_c", (128, 1), F32)
    nm16 = psb("nm16", (16, 1), F32)
    nm16b = psb("nm16b", (16, 1), F32)

    pA_q = pa.enter_context(nc.psum_tensor("pA_q", [128, 1024], F32))
    pA_k = pa.enter_context(nc.psum_tensor("pA_k", [128, 1024], F32))
    pA_v = pa.enter_context(nc.psum_tensor("pA_v", [128, 1024], F32))
    pA_t = pa.enter_context(nc.psum_tensor("pA_t", [128, 1024], BF16))
    pA_o = pa.enter_context(nc.psum_tensor("pA_o", [65, 512], F32))
    pA_s = pA_t[:].bitcast(F32)[0:16, 0:128]
    pA_oB = pA_t[:].bitcast(F32)[0:65, 0:512]

    for eng in ENGINES:
        P.emit(eng, lambda e: e.nop(), wait=barrier0)

    v_ones = P.emit("vector", lambda e: e.memset(
        Vp[:].rearrange("p t (h j) -> p t h j", j=65)[:, :, :, 64:65], 1.0), inc=sv)

    NM_WAIT = {}

    def emit_ln_stats(src_f32, n, l_wait):
        P.emit("vector", lambda e: e.reduce_sum(out=s1_c[:], in_=src_f32, axis=AX.X),
               inc=sv, wait=[(ld, l_wait)])
        a0 = P.emit("scalar", lambda e, n=n: e.activation(out=zv[:, :n], in_=src_f32, func=AF.Square),
                    inc=sa, wait=[(ld, l_wait), (sv, sv.n)])
        P.emit("vector", lambda e, n=n: e.reduce_sum(out=s2_c[:], in_=zv[:, :n], axis=AX.X),
               inc=sv, wait=[(sa, a0)])
        P.emit("vector", lambda e, n=n: e.tensor_scalar_mul(out=mu_c[:], in0=s1_c[:], scalar1=1.0 / n),
               inc=sv, wait=[(sv, sv.n)])
        P.emit("vector", lambda e, n=n: e.tensor_scalar_mul(out=s2_c[:], in0=s2_c[:], scalar1=1.0 / n),
               inc=sv, wait=[(sv, sv.n)])
        P.emit("vector", lambda e: e.scalar_tensor_tensor(
            out=s1_c[:], in0=mu_c[:], scalar=mu_c[:, 0:1], in1=s2_c[:],
            op0=AO.mult, op1=AO.subtract), inc=sv, wait=[(sv, sv.n)])
        P.emit("vector", lambda e: e.tensor_scalar(out=s1_c[:], in0=s1_c[:], scalar1=-1.0, scalar2=1e-5,
                                                   op0=AO.mult, op1=AO.add), inc=sv, wait=[(sv, sv.n)])
        vpe = sv.n
        aln = P.emit("scalar", lambda e: e.activation(out=s2_c[:], in_=s1_c[:], func=AF.Ln),
                     inc=sa, wait=[(sv, vpe)])
        va = P.emit("scalar", lambda e: e.activation(out=rstd_c[:], in_=s2_c[:], func=AF.Exp, scale=-0.5),
                    inc=sa, wait=[(sa, aln)])
        return va, sv.n

    def phase_a_seq(s):
        km_cnt = 0
        Z_GUARD = [None, None]
        RB_GUARD = [(sv, 0)]
        FD_DONE = set()
        STH = []

        def front_dve(t, slot):
            FD_DONE.add(t)
            xbuf = x_t if slot % 2 == 0 else zu
            l_x = P.emit("sync", lambda e, t=t, xbuf=xbuf: e.dma_start(
                out=xbuf[:], in_=xs[s, 128 * t:128 * (t + 1), :]),
                inc=ld, inc_by=16, wait=[(sa, sa.n), (sv, sv.n), (st, st.n)])
            va, vb = emit_ln_stats(xbuf[:], D, l_x)
            ah = P.emit("vector", lambda e, slot=slot, xbuf=xbuf: e.tensor_scalar(
                out=h_bf4[slot][:], in0=xbuf[:], scalar1=mu_c[:, 0:1], scalar2=rstd_c[:, 0:1],
                op0=AO.subtract, op1=AO.mult),
                inc=sv, wait=[(sa, va), (sv, sv.n), (sq, sq.n), (st, st.n)])
            if "h" in taps:
                P.emit("gpsimd", lambda e, s=s, t=t, slot=slot: e.dma_start(
                    out=tap_t["h"][s, 128 * t:128 * (t + 1), :], in_=h_bf4[slot][:]),
                    inc=st, inc_by=16, wait=[(sv, ah)])

        for g in range(NG):
            qm_cnt = 0

            def frontend(t, tt):
                if t not in FD_DONE:
                    front_dve(t, tt)
                for cq in range(2):
                    for k in range(4):
                        c = 4 * cq + k
                        P.emit("tensor", lambda e, c=c, k=k, tt=tt: e.transpose(
                            pA_t[:, 128 * k:128 * (k + 1)],
                            h_bf4[tt][:, 128 * c:128 * (c + 1)], ident_b[:]),
                            inc=(sq if k == 3 else None),
                            wait=([(sv, sv.n)] if k == 0 else []))
                    tq = sq.n
                    P.emit("vector", lambda e, cq=cq, tt=tt: e.tensor_copy(
                        out=hT_g[:, 4 * cq:4 * cq + 4, 128 * tt:128 * (tt + 1)],
                        in_=pA_t[:, 0:512].rearrange("p (k n) -> p k n", n=128)),
                        inc=sv, wait=[(sq, tq)])
                v_hT = sv.n
                first_w = [(sv, v_hT), (sa, sa.n), (sq, sq.n)]
                for (pdst, wT) in ((pA_q, WqT), (pA_k, WkT), (pA_v, WvT)):
                    for o2 in range(2):
                        for c in range(8):
                            P.emit("tensor", lambda e, pdst=pdst, wT=wT, o2=o2, c=c, tt=tt: e.matmul(
                                pdst[:, 512 * o2:512 * (o2 + 1)],
                                hT_g[:, c, 128 * tt:128 * (tt + 1)],
                                wT[:, c, 512 * o2:512 * (o2 + 1)],
                                start=(c == 0), stop=(c == 7)),
                                inc=(sq if c == 7 else None),
                                wait=(first_w if (c == 0 and first_w) else []))
                            first_w = []
                q_all = sq.n
                P.emit("vector", lambda e, t=t: e.tensor_copy(
                    out=Vp[:, t, :].rearrange("p (h j) -> p h j", j=65)[:, :, 0:64],
                    in_=pA_v[:].rearrange("p (h j) -> p h j", j=64)),
                    inc=sv, wait=[(sq, q_all), (sv, v_ones)])
                return q_all

            def emit_evacs(q_all, par, guard):
                zq_ = (zqc0, zqc1)[par]; zk_ = (zkc0, zkc1)[par]
                w0 = [(sq, q_all)] + ([guard] if guard else [])
                eq = P.emit("scalar", lambda e, zq_=zq_: e.activation(out=zq_[:], in_=pA_q[:, :], func=AF.Copy),
                            inc=sa, wait=w0)
                ek = P.emit("scalar", lambda e, zk_=zk_: e.activation(out=zk_[:], in_=pA_k[:, :], func=AF.Copy),
                            inc=sa)
                return ek

            def backend(t, tt, ek, par):
                nonlocal qm_cnt, km_cnt
                zqc = (zqc0, zqc1)[par]; zkc = (zkc0, zkc1)[par]
                P.emit("sync", lambda e, t=t: e.dma_start(
                    out=cos_t[:], in_=c_cos[128 * t:128 * (t + 1), :]), inc=ld, inc_by=16,
                    wait=[(sv, sv.n)])
                l_cs = P.emit("sync", lambda e, t=t: e.dma_start(
                    out=sinp_t[:], in_=c_sinp[128 * t:128 * (t + 1), :]), inc=ld, inc_by=16)
                for which, pcs, g_s_, gp_s_, b_s_, bp_s_, dst_bf in (
                        ("q", zqc, gq_s, gqp_s, bq_s, bqp_s, qkp_bf),
                        ("k", zkc, gk_s, gkp_s, bk_s, bkp_s, qkp_bf)):
                    pcv = pcs[:].rearrange("p (h j) -> p h j", j=64)
                    a1 = P.emit("scalar", lambda e, pcs=pcs: e.activation(
                        out=zu[:], in_=pcs[:], func=AF.Square), inc=sa, wait=[(sa, ek), (sv, sv.n), (st, st.n)])
                    P.emit("vector", lambda e: e.reduce_sum(
                        out=st16[:], in_=zu[:].rearrange("p (h j) -> p h j", j=64), axis=AX.X),
                        inc=sv, wait=[(sa, a1), (sv, sv.n)])
                    P.emit("vector", lambda e: e.tensor_scalar(out=st16[:], in0=st16[:], scalar1=1.0 / 64, scalar2=1e-5,
                                                               op0=AO.mult, op1=AO.add),
                           inc=sv, wait=[(sv, sv.n)])
                    aln2 = P.emit("scalar", lambda e: e.activation(out=st16b[:], in_=st16[:], func=AF.Ln),
                                  inc=sa, wait=[(sv, sv.n)])
                    a2 = P.emit("scalar", lambda e: e.activation(out=rstd_h[:], in_=st16b[:], func=AF.Exp, scale=-0.5),
                                inc=sa, wait=[(sa, aln2)])
                    P.emit("vector", lambda e, gg=g_s_: e.tensor_tensor(out=A_t[:], in0=cos_t[:], in1=gg[:], op=AO.mult),
                           inc=sv, wait=[(ld, l_cs)])
                    P.emit("vector", lambda e, gp=gp_s_: e.tensor_tensor(out=B_t[:], in0=sinp_t[:], in1=gp[:], op=AO.mult), inc=sv)
                    P.emit("vector", lambda e, bb=b_s_: e.tensor_tensor(out=e1_t[:], in0=cos_t[:], in1=bb[:], op=AO.mult), inc=sv)
                    P.emit("vector", lambda e, bp=bp_s_: e.tensor_tensor(out=D_t[:], in0=sinp_t[:], in1=bp[:], op=AO.mult), inc=sv)
                    P.emit("vector", lambda e: e.tensor_tensor(out=D_t[:], in0=D_t[:], in1=e1_t[:], op=AO.add),
                           inc=sv, wait=[(sv, sv.n)])
                    hb = lambda a: a.unsqueeze(2).broadcast_to([128, 16, 64])
                    jb = lambda a: a.unsqueeze(1).broadcast_to([128, 16, 64])
                    jb32 = lambda a: a.unsqueeze(1).broadcast_to([128, 16, 32])
                    z2v = z2[:].rearrange("p (h j) -> p h j", j=64)
                    zuv = zu[:].rearrange("p (h j) -> p h j", j=64)
                    zvv = zv[:].rearrange("p (h j) -> p h j", j=64)
                    P.emit("vector", lambda e, pcv=pcv, z2v=z2v: e.tensor_tensor(
                        out=z2v, in0=pcv, in1=hb(rstd_h[:]), op=AO.mult), inc=sv, wait=[(sa, a2), (sv, sv.n), (st, st.n)])
                    P.emit("vector", lambda e, z2v=z2v, zuv=zuv: e.tensor_tensor(
                        out=zuv, in0=z2v, in1=jb(A_t[:]), op=AO.mult), inc=sv)
                    P.emit("vector", lambda e, z2v=z2v, zvv=zvv: e.tensor_tensor(
                        out=zvv[:, :, 0:32], in0=z2v[:, :, 1::2], in1=jb32(B_t[:, 0:32]), op=AO.mult), inc=sv)
                    P.emit("vector", lambda e, z2v=z2v, zvv=zvv: e.tensor_tensor(
                        out=zvv[:, :, 32:64], in0=z2v[:, :, 0::2], in1=jb32(B_t[:, 32:64]), op=AO.mult), inc=sv)
                    P.emit("vector", lambda e: e.tensor_tensor(out=zu[:], in0=zu[:], in1=zv[:], op=AO.add), inc=sv)
                    vq = P.emit("vector", lambda e, dst=dst_bf, zuv=zuv: e.tensor_tensor(
                        out=dst[:].rearrange("p (h j) -> p h j", j=64), in0=zuv, in1=jb(D_t[:]), op=AO.add),
                        inc=sv, wait=[(sq, sq.n), (st, st.n), (sa, sa.n)])
                    a3 = P.emit("scalar", lambda e, dst=dst_bf: e.activation(
                        out=zu[:], in_=dst[:], func=AF.Square), inc=sa, wait=[(sv, vq)])
                    P.emit("vector", lambda e: e.reduce_sum(
                        out=st16[:], in_=zu[:].rearrange("p (h j) -> p h j", j=64), axis=AX.X),
                        inc=sv, wait=[(sa, a3), (sv, sv.n)])
                    if which == "q":
                        srcb, dstb = (qm2_a, qm2_b) if qm_cnt % 2 else (qm2_b, qm2_a)
                        if qm_cnt == 0:
                            P.emit("vector", lambda e, d=dstb: e.tensor_copy(out=d[:], in_=st16[:]),
                                   inc=sv, wait=[(sv, sv.n)])
                        else:
                            P.emit("vector", lambda e, s_=srcb, d=dstb: e.tensor_tensor(
                                out=d[:], in0=s_[:], in1=st16[:], op=AO.max), inc=sv, wait=[(sv, sv.n)])
                        qm_cnt += 1
                    else:
                        srcb, dstb = (km2_a, km2_b) if km_cnt % 2 else (km2_b, km2_a)
                        if km_cnt == 0:
                            P.emit("vector", lambda e, d=dstb: e.tensor_copy(out=d[:], in_=st16[:]),
                                   inc=sv, wait=[(sv, sv.n)])
                        else:
                            P.emit("vector", lambda e, s_=srcb, d=dstb: e.tensor_tensor(
                                out=d[:], in0=s_[:], in1=st16[:], op=AO.max), inc=sv, wait=[(sv, sv.n)])
                        km_cnt += 1
                    if (which == "q" and "qr" in taps) or (which == "k" and "kr" in taps):
                        nm = "qr" if which == "q" else "kr"
                        P.emit("gpsimd", lambda e, s=s, t=t, dst=dst_bf, nm=nm: e.dma_start(
                            out=tap_t[nm][s, 128 * t:128 * (t + 1), :], in_=dst[:]),
                            inc=st, inc_by=16, wait=[(sv, vq)])
                    tdst = qT_g if which == "q" else kT_all
                    toff = 128 * tt if which == "q" else 128 * t
                    for cq in range(2):
                        for k in range(4):
                            c = 4 * cq + k
                            P.emit("tensor", lambda e, c=c, k=k, dst=dst_bf: e.transpose(
                                pA_t[:, 128 * k:128 * (k + 1)],
                                dst[:, 128 * c:128 * (c + 1)], ident_b[:]),
                                inc=(sq if k == 3 else None),
                                wait=([(sv, sv.n)] if k == 0 else []))
                        tq2 = sq.n
                        P.emit("vector", lambda e, cq=cq, td=tdst, to=toff: e.tensor_copy(
                            out=td[:, 4 * cq:4 * cq + 4, to:to + 128],
                            in_=pA_t[:, 0:512].rearrange("p (k n) -> p k n", n=128)),
                            inc=sv, wait=[(sq, tq2)])

            pend = None
            for tt in range(4):
                t = 4 * g + tt
                q_all_t = frontend(t, tt)
                if pend is not None:
                    backend(pend[0], pend[1], pend[2], pend[3])
                    Z_GUARD[pend[3]] = (sv, sv.n)
                ek_t = emit_evacs(q_all_t, tt % 2, Z_GUARD[tt % 2])
                pend = (t, tt, ek_t, tt % 2)
            backend(pend[0], pend[1], pend[2], pend[3])
            Z_GUARD[pend[3]] = (sv, sv.n)

            qmb = qm2_b if ((qm_cnt - 1) % 2 == 1) else qm2_a
            kmb = km2_b if ((km_cnt - 1) % 2 == 1) else km2_a
            tqs = P.emit("tensor", lambda e, qmb=qmb: e.transpose(
                pA_s, qmb[:], ident_f[:]), inc=sq, wait=[(sv, sv.n)])
            P.emit("vector", lambda e: e.reduce_max(out=nm16[:], in_=pA_s, axis=AX.X),
                   inc=sv, wait=[(sq, tqs), (sv, sv.n)])
            tks = P.emit("tensor", lambda e, kmb=kmb: e.transpose(
                pA_s, kmb[:], ident_f[:]), inc=sq, wait=[(sv, sv.n)])
            P.emit("vector", lambda e: e.reduce_max(out=nm16b[:], in_=pA_s, axis=AX.X),
                   inc=sv, wait=[(sq, tks), (sv, sv.n)])
            vmm = P.emit("vector", lambda e: e.tensor_tensor(out=nm16[:], in0=nm16[:], in1=nm16b[:], op=AO.mult),
                         inc=sv, wait=[(sv, sv.n)])
            aM0 = P.emit("scalar", lambda e: e.activation(out=nm16b[:], in_=nm16[:], func=AF.Ln),
                         inc=sa, wait=[(sv, vmm)])
            aM = P.emit("scalar", lambda e: e.activation(out=nm16b[:], in_=nm16b[:], func=AF.Exp, scale=0.5),
                        inc=sa, wait=[(sa, aM0)])
            vM = P.emit("vector", lambda e: e.tensor_scalar_mul(out=nm16[:], in0=nm16b[:], scalar1=-0.125),
                        inc=sv, wait=[(sa, aM)])
            sM = P.emit("gpsimd", lambda e: e.dma_start(
                out=nm_scr[:].rearrange("(h o) -> h o", o=1), in_=nm16[:]),
                inc=st, inc_by=16, wait=[(sv, vM)])
            lM = P.emit("sync", lambda e: e.dma_start(
                out=negm_bc[:], in_=nm_scr[:].unsqueeze(0).broadcast_to([128, H])),
                inc=ld, inc_by=16, wait=[(st, sM), (sa, sa.n)])
            vnp = P.emit("vector", lambda e: e.tensor_tensor(
                out=negmp[:], in0=negm_bc[:].rearrange("p (h two) -> p h two", two=2)[:, :, 0],
                in1=negm_bc[:].rearrange("p (h two) -> p h two", two=2)[:, :, 1], op=AO.min),
                inc=sv, wait=[(ld, lM), (sv, sv.n)])
            NM_WAIT[(s, g)] = (lM, vnp)

            nkc = 4 * (g + 1)
            last_exp_parity = [0, 0]
            last_norm_pv = [sv.n, sv.n]
            chunk_list = [(hp, kc) for hp in range(8) for kc in range(nkc)]

            def emit_scores(idx, extra_wait=()):
                hp, kc = chunk_list[idx]
                buf = kc % 2
                sPair = (pA_q, pA_k)[buf]
                w0 = [(sa, last_exp_parity[buf])] + list(extra_wait)
                P.emit("tensor", lambda e, sPair=sPair, hp=hp, kc=kc: e.matmul(
                    sPair[:, 0:512], kT_all[0:64, hp, 128 * kc:128 * (kc + 1)],
                    qT_g[0:64, hp, :], start=True, stop=True, tile_position=(0, 0)),
                    inc=sq, wait=w0)
                P.emit("tensor", lambda e, sPair=sPair, hp=hp, kc=kc: e.matmul(
                    sPair[:, 512:1024], kT_all[64:128, hp, 128 * kc:128 * (kc + 1)],
                    qT_g[64:128, hp, :], start=True, stop=True, tile_position=(64, 0)),
                    inc=sq)
                return sq.n

            PEND_TAIL = []

            def emit_pair_tail(hp, pvset, pvtA, pvtB, pv_q):
                hA = 2 * hp
                if pvset == 0:
                    aR1 = P.emit("scalar", lambda e: e.activation(out=rs2_s[:], in_=pA_v[64:65, 0:1024], func=AF.Ln),
                                 inc=sa, wait=[(sq, pv_q), (st, st.n)])
                    aR4 = P.emit("scalar", lambda e: e.activation(out=rs2_s[:], in_=rs2_s[:], func=AF.Exp, scale=-1.0),
                                 inc=sa, wait=[(sa, aR1)])
                    s2_ = P.emit("gpsimd", lambda e, hA=hA: e.dma_start(
                        out=rs_scr[hA:hA + 2, :].rearrange("a n -> (a n)").unsqueeze(0), in_=rs2_s[:]),
                        inc=st, inc_by=16, wait=[(sa, aR4)])
                else:
                    aR1 = P.emit("scalar", lambda e, pvtA=pvtA: e.activation(out=rsA, in_=pvtA[64:65, :], func=AF.Ln),
                                 inc=sa, wait=[(sq, pv_q), (st, st.n)])
                    aR2 = P.emit("scalar", lambda e: e.activation(out=rsA, in_=rsA, func=AF.Exp, scale=-1.0),
                                 inc=sa, wait=[(sa, aR1)])
                    aR3 = P.emit("scalar", lambda e, pvtB=pvtB: e.activation(out=rsB, in_=pvtB[64:65, :], func=AF.Ln),
                                 inc=sa, wait=[(sa, aR2)])
                    aR4 = P.emit("scalar", lambda e: e.activation(out=rsB, in_=rsB, func=AF.Exp, scale=-1.0),
                                 inc=sa, wait=[(sa, aR3)])
                    P.emit("gpsimd", lambda e, hA=hA: e.dma_start(out=rs_scr[hA:hA + 1, :], in_=rsA),
                           inc=st, inc_by=16, wait=[(sa, aR4)])
                    s2_ = P.emit("gpsimd", lambda e, hA=hA: e.dma_start(out=rs_scr[hA + 1:hA + 2, :], in_=rsB),
                                 inc=st, inc_by=16)
                lrb = P.emit("sync", lambda e, hA=hA: e.dma_start(
                    out=rb_s[:], in_=rs_scr[hA:hA + 2, :].unsqueeze(1).broadcast_to([2, 64, 512])),
                    inc=ld, inc_by=16, wait=[(st, s2_), RB_GUARD[0]])
                P.emit("vector", lambda e, hp=hp, pvtA=pvtA: e.tensor_tensor(
                    out=attnT_g[0:64, hp, :], in0=pvtA[0:64, :], in1=rb_s[0:64, :], op=AO.mult),
                    inc=sv, wait=[(ld, lrb), (sq, sq.n)])
                vn = P.emit("vector", lambda e, hp=hp, pvtB=pvtB: e.tensor_tensor(
                    out=attnT_g[64:128, hp, :], in0=pvtB[0:64, :], in1=rb_s[64:128, :], op=AO.mult),
                    inc=sv)
                RB_GUARD[0] = (sv, vn)
                last_norm_pv[pvset] = vn
                if "attnT" in taps:
                    vcp = P.emit("vector", lambda e, hp=hp: e.tensor_copy(out=zu[:, 0:512], in_=attnT_g[:, hp, :]),
                                 inc=sv, wait=[(sv, vn)])
                    P.emit("gpsimd", lambda e, s=s, g=g, hp=hp: e.dma_start(
                        out=tap_t["attnT"][s, g, 128 * hp:128 * (hp + 1), :], in_=zu[:, 0:512]),
                        inc=st, inc_by=16, wait=[(sv, vcp)])

            FD_HOOKS = []
            if g + 1 < NG:
                L = len(chunk_list)
                for i in range(3):
                    FD_HOOKS.append((max(1, (L * (i + 1)) // 5), 4 * (g + 1) + i, i))

            mq_pend = emit_scores(0, extra_wait=[(sv, sv.n), (sq, sq.n), (sa, sa.n)])
            for idx, (hp, kc) in enumerate(chunk_list):
                while FD_HOOKS and idx == FD_HOOKS[0][0]:
                    _, t2, sl = FD_HOOKS.pop(0)
                    front_dve(t2, sl)
                hA, hB = 2 * hp, 2 * hp + 1
                pvset = hp % 2
                pvtA = pA_v[0:65, 0:512] if pvset == 0 else pA_o[0:65, 0:512]
                pvtB = pA_v[0:65, 512:1024] if pvset == 0 else pA_oB
                buf = kc % 2
                sPair = (pA_q, pA_k)[buf]
                PPd = (PP0, PP1)[buf]
                mq = mq_pend
                diag = (kc >= 4 * g)
                wexp = [(sq, mq), (ld, NM_WAIT[(s, g)][0]), (sv, NM_WAIT[(s, g)][1])]
                aB = P.emit("scalar", lambda e, sPair=sPair, hp=hp, dst=PPd: e.activation(
                    out=dst[:], in_=sPair[:, 0:1024], func=AF.Exp, scale=0.125, bias=negmp[:, hp:hp + 1]),
                    inc=sa, wait=wexp)
                aA = aB
                last_exp_parity[buf] = aB
                if idx + 1 < len(chunk_list):
                    mq_pend = emit_scores(idx + 1)
                pw = [(sa, aB)]
                if diag:
                    j = kc - 4 * g
                    vmB = P.emit("vector", lambda e, j=j, PPd=PPd: e.tensor_tensor(
                        out=PPd[:].rearrange("p (two n) -> p two n", n=512),
                        in0=PPd[:].rearrange("p (two n) -> p two n", n=512),
                        in1=dmask_b[:, 512 * j:512 * (j + 1)].unsqueeze(1).transpose([0, 2, 1]).broadcast_to([128, 2, 512])
                        if False else dmask_b[:, 512 * j:512 * (j + 1)].unsqueeze(1).broadcast_to([128, 2, 512]),
                        op=AO.mult),
                        inc=sv, wait=[(sa, aB)])
                    pw = [(sv, vmB)]
                if kc == 0:
                    pw = pw + [(sv, last_norm_pv[pvset])]
                P.emit("tensor", lambda e, kc=kc, hA=hA, PPd=PPd, pvtA=pvtA,
                       st_=(kc == 0), sp_=(kc == nkc - 1): e.matmul(
                    pvtA, Vp[:, kc, 65 * hA:65 * hA + 65], PPd[:, 0:512],
                    start=st_, stop=sp_),
                    inc=(sq if kc == nkc - 1 else None), wait=pw)
                P.emit("tensor", lambda e, kc=kc, hB=hB, PPd=PPd, pvtB=pvtB,
                       st_=(kc == 0), sp_=(kc == nkc - 1): e.matmul(
                    pvtB, Vp[:, kc, 65 * hB:65 * hB + 65], PPd[:, 512:1024],
                    start=st_, stop=sp_),
                    inc=(sq if kc == nkc - 1 else None))
                if kc == nkc - 1:
                    PEND_TAIL.append((hp, pvset, pvtA, pvtB, sq.n))
                if PEND_TAIL and (kc == min(2, nkc - 1) or idx == len(chunk_list) - 1):
                    if PEND_TAIL[0][0] != hp or idx == len(chunk_list) - 1:
                        thp, tset, tA, tB, tpv = PEND_TAIL.pop(0)
                        emit_pair_tail(thp, tset, tA, tB, tpv)
            while PEND_TAIL:
                thp, tset, tA, tB, tpv = PEND_TAIL.pop(0)
                emit_pair_tail(thp, tset, tA, tB, tpv)
            v_attn = sv.n

            for tt in range(4):
                t = 4 * g + tt
                lxr = P.emit("sync", lambda e, s=s, t=t: e.dma_start(
                    out=x_t[:], in_=xs[s, 128 * t:128 * (t + 1), :]),
                    inc=ld, inc_by=16, wait=[(sv, sv.n), (sa, sa.n)])
                for o2 in range(2):
                    for hp in range(8):
                        P.emit("tensor", lambda e, o2=o2, hp=hp, tt=tt: e.matmul(
                            pA_q[:, 512 * o2:512 * (o2 + 1)],
                            attnT_g[:, hp, 128 * tt:128 * (tt + 1)],
                            WoT[:, hp, 512 * o2:512 * (o2 + 1)],
                            start=(hp == 0), stop=(hp == 7)),
                            inc=(sq if hp == 7 else None),
                            wait=([(sv, v_attn), (sa, sa.n), (sv, sv.n)] if (hp == 0 and o2 == 0) else []))
                mo = sq.n
                ob = (zu, z2)[tt % 2]
                w_ev = [(st, STH[-2])] if len(STH) >= 2 else [(st, st.n)]
                P.emit("vector", lambda e, ob=ob: e.tensor_tensor(out=ob[:], in0=pA_q[:, :], in1=bo_s[:], op=AO.add),
                       inc=sv, wait=[(sq, mo)] + w_ev)
                vx1 = P.emit("vector", lambda e, ob=ob: e.tensor_tensor(out=ob[:], in0=ob[:], in1=x_t[:], op=AO.add),
                             inc=sv, wait=[(ld, lxr)])
                P.emit("gpsimd", lambda e, s=s, t=t, ob=ob: e.dma_start(
                    out=xmid[s, 128 * t:128 * (t + 1), :], in_=ob[:]),
                    inc=st, inc_by=16, wait=[(sv, vx1)])
                STH.append(st.n)
                if "x1" in taps:
                    P.emit("gpsimd", lambda e, s=s, t=t: e.dma_start(
                        out=tap_t["x1"][s, 128 * t:128 * (t + 1), :], in_=zu[:]),
                        inc=st, inc_by=16, wait=[(sv, vx1)])

    for s in range(NSEQ):
        phase_a_seq(s)
    pa.close()
    watt.close()
    barrierA = [(sv, sv.n), (sq, sq.n), (st, st.n), (ld, ld.n), (sa, sa.n)]

    if stop_after == "phaseA":
        P.emit("gpsimd", lambda e: e.nop(), wait=barrierA)
        finalize(nc, P)
        return nc, stack

    pb = contextlib.ExitStack()
    qsb = lambda name, shape, dtype: pb.enter_context(nc.sbuf_tensor(name, list(shape), dtype))
    wgqT = qsb("wgqT", (128, 8, I_FFN), BF16)
    wuqT = qsb("wuqT", (128, 8, I_FFN), BF16)
    wdqT = qsb("wdqT", (128, 22, D), BF16)
    xb_t = qsb("xb_t", (128, D), F32)
    xb_t2 = qsb("xb_t2", (128, D), F32)
    h2_f = qsb("h2_f", (128, D), F32)
    mu_all = qsb("mu_all", (128, NSEQ * NT), F32)
    rstd_all = qsb("rstd_all", (128, NSEQ * NT), F32)
    h2qT = qsb("h2qT", (128, 8, 128), BF16)
    h1qT = qsb("h1qT", (128, 8, 128), BF16)
    gsil = qsb("gsil", (128, 1024), F32)
    h1_s = qsb("h1_s", (128, 1024), F32)
    ob_t = qsb("ob_t", (128, D), F32)
    ob_t2 = qsb("ob_t2", (128, D), F32)
    b_sq = qsb("b_sq", (128, 2 * D), F32)
    b_s1 = qsb("b_s1", (128, 2), F32); b_s2 = qsb("b_s2", (128, 2), F32)
    b_mu = qsb("b_mu", (128, 1), F32); b_rstd = qsb("b_rstd", (128, 1), F32)
    q_out = qsb("q_out", (128, D), BF16)
    q_ax2 = qsb("q_ax2", (128, D), F32)
    q_t2 = qsb("q_t2", (128, D), F32)
    q_y = qsb("q_y", (128, D), F32)
    q_y2 = qsb("q_y2", (128, D), F32)
    q_sg = qsb("q_sg", (128, D), BF16)
    q_m = qsb("q_m", (128, D), BF16)
    q_bm = qsb("q_bm", (128, D // 16), F32)
    q_sc = qsb("q_sc", (128, D // 16), F32)
    q_rs = qsb("q_rs", (128, D // 16), F32)
    q_s2 = qsb("q_s2", (128, D // 16), BF16)
    b_scr = (q_ax2, q_t2, q_y, q_y2, q_sg, q_m, q_bm, q_sc, q_rs, q_s2)

    pB_g = pb.enter_context(nc.psum_tensor("pB_g", [128, 1024], F32))
    pB_u = pb.enter_context(nc.psum_tensor("pB_u", [128, 1024], F32))
    pB_t = pb.enter_context(nc.psum_tensor("pB_t", [128, 512], BF16))
    pB_d = pb.enter_context(nc.psum_tensor("pB_d", [128, 1024], F32))

    for eng in ENGINES:
        P.emit(eng, lambda e: e.nop(), wait=barrierA)
    for dst, off, ch, nrows in ((wgqT, WGT_OFF, 8, I_FFN), (wuqT, WUT_OFF, 8, I_FFN), (wdqT, WDT_OFF, 22, D)):
        P.emit("sync", lambda e, dst=dst, off=off, ch=ch, nr=nrows: e.dma_start(
            out=dst[:], in_=wq_scr[off:off + 128 * ch * nr].rearrange("(p c n) -> p c n", p=128, c=ch)),
            inc=ld, inc_by=16)
    ld_wq = ld.n

    scr_guard = [None]
    OD_SLICES = [(0, 1024), (1024, 1024), (2048, 768)]

    def phase_b_prologue():
        NBT = 2
        for it in range(NSEQ * NT // NBT):
            ti0 = it * NBT
            s0, t0_ = ti0 // NT, ti0 % NT
            lx = P.emit("sync", lambda e, s0=s0, t0_=t0_: e.dma_start(
                out=b_sq[:].rearrange("p (b n) -> p b n", n=D),
                in_=xmid[s0, 128 * t0_:128 * (t0_ + NBT), :].rearrange("(b q) n -> q b n", b=NBT)),
                inc=ld, inc_by=16, wait=([(ld, ld_wq)] if it == 0 else []) + [(sv, sv.n), (sa, sa.n)])
            vmu = P.emit("vector", lambda e, ti0=ti0: e.reduce_sum(
                out=mu_all[:, ti0:ti0 + NBT], in_=b_sq[:].rearrange("p (b n) -> p b n", n=D), axis=AX.X),
                inc=sv, wait=[(ld, lx)])
            a0 = P.emit("scalar", lambda e: e.activation(out=b_sq[:], in_=b_sq[:], func=AF.Square),
                        inc=sa, wait=[(ld, lx), (sv, vmu)])
            P.emit("vector", lambda e: e.reduce_sum(
                out=b_s2[:, 0:NBT], in_=b_sq[:].rearrange("p (b n) -> p b n", n=D), axis=AX.X),
                inc=sv, wait=[(sa, a0), (sv, sv.n)])
            P.emit("vector", lambda e, ti0=ti0: e.tensor_scalar_mul(
                out=mu_all[:, ti0:ti0 + NBT], in0=mu_all[:, ti0:ti0 + NBT], scalar1=1.0 / D),
                inc=sv, wait=[(sv, sv.n)])
            P.emit("vector", lambda e: e.tensor_scalar_mul(out=b_s2[:, 0:NBT], in0=b_s2[:, 0:NBT], scalar1=1.0 / D),
                   inc=sv, wait=[(sv, sv.n)])
            P.emit("vector", lambda e, ti0=ti0: e.tensor_tensor(
                out=b_s1[:, 0:NBT], in0=mu_all[:, ti0:ti0 + NBT], in1=mu_all[:, ti0:ti0 + NBT], op=AO.mult),
                inc=sv, wait=[(sv, sv.n)])
            P.emit("vector", lambda e: e.tensor_tensor(out=b_s1[:, 0:NBT], in0=b_s1[:, 0:NBT], in1=b_s2[:, 0:NBT], op=AO.subtract),
                   inc=sv, wait=[(sv, sv.n)])
            P.emit("vector", lambda e: e.tensor_scalar(out=b_s1[:, 0:NBT], in0=b_s1[:, 0:NBT], scalar1=-1.0, scalar2=1e-5,
                                                       op0=AO.mult, op1=AO.add), inc=sv, wait=[(sv, sv.n)])
            vr3 = P.emit("vector", lambda e: e.reciprocal(out=b_s2[:, 0:NBT], in_=b_s1[:, 0:NBT]), inc=sv, wait=[(sv, sv.n)])
            P.emit("scalar", lambda e, ti0=ti0: e.activation(out=rstd_all[:, ti0:ti0 + NBT], in_=b_s2[:, 0:NBT], func=AF.Sqrt),
                   inc=sa, wait=[(sv, vr3)])

    VO_H = []
    SO_H = []

    def phase_b_tile(s, t):
        ti = s * NT + t
        xb = (xb_t, xb_t2)[ti % 2]
        w_lx = [(sv, VO_H[-2])] if len(VO_H) >= 2 else [(sv, sv.n), (sa, sa.n)]
        lx = P.emit("sync", lambda e, s=s, t=t, xb=xb: e.dma_start(
            out=xb[:], in_=xmid[s, 128 * t:128 * (t + 1), :]),
            inc=ld, inc_by=16, wait=w_lx)
        if "xb" in taps:
            P.emit("gpsimd", lambda e, s=s, t=t, xb=xb: e.dma_start(
                out=tap_t["xb"][s, 128 * t:128 * (t + 1), :], in_=xb[:]),
                inc=st, inc_by=16, wait=[(ld, lx)])
        ah = P.emit("vector", lambda e, ti=ti, xb=xb: e.tensor_scalar(
            out=h2_f[:], in0=xb[:], scalar1=mu_all[:, ti:ti + 1], scalar2=rstd_all[:, ti:ti + 1],
            op0=AO.subtract, op1=AO.mult),
            inc=sv, wait=[(ld, lx)])
        _, vq2p = emit_quant(b_scr, h2_f[:], D, q_out[:], h2_f,
                             wv=[(sv, ah)], wa=[(sv, ah)], w_final=[(sq, sq.n)],
                             scratch_guard=scr_guard[0], split=PHASEB_SPLIT, rscale_act=False)
        scr_guard[0] = vq2p
        if "h2q" in taps:
            vcp = P.emit("vector", lambda e: e.tensor_copy(out=q_t2[:], in_=q_out[:]), inc=sv, wait=[vq2p])
            P.emit("gpsimd", lambda e, s=s, t=t: e.dma_start(
                out=tap_t["h2q"][s, 128 * t:128 * (t + 1), :], in_=q_t2[:]),
                inc=st, inc_by=16, wait=[(sv, vcp)])
        for cq in range(2):
            for k in range(4):
                c = 4 * cq + k
                P.emit("tensor", lambda e, c=c, k=k: e.transpose(
                    pB_t[:, 128 * k:128 * (k + 1)], q_out[:, 128 * c:128 * (c + 1)], ident_b[:]),
                    inc=(sq if k == 3 else None),
                    wait=([(sv, sv.n), (sq, sq.n)] if k == 0 else []))
            tq = sq.n
            P.emit("vector", lambda e, cq=cq: e.tensor_copy(
                out=h2qT[:, 4 * cq:4 * cq + 4, :], in_=pB_t[:].rearrange("p (k n) -> p k n", n=128)),
                inc=sv, wait=[(sq, tq)])
        v_h2qT = sv.n

        first_dn = True
        def emit_gu(si, wv_extra):
            o0, ow = OD_SLICES[si]
            mw = [(sv, v_h2qT)] + wv_extra
            for pg, pt in ((pB_g, wgqT), (pB_u, wuqT)):
                for hh in range(0, ow, 512):
                    hw = min(512, ow - hh)
                    for c in range(8):
                        P.emit("tensor", lambda e, c=c, o0=o0, hh=hh, hw=hw, pg=pg, pt=pt: e.matmul(
                            pg[:, hh:hh + hw], h2qT[:, c, :], pt[:, c, o0 + hh:o0 + hh + hw],
                            start=(c == 0), stop=(c == 7)),
                            inc=(sq if c == 7 else None), wait=(mw if c == 0 else []))
                        mw = []
            return sq.n
        mgu = emit_gu(0, [(sa, sa.n), (sv, sv.n)])
        for si, (o0, ow) in enumerate(OD_SLICES):
            ag = P.emit("scalar", lambda e, ow=ow: e.activation(out=gsil[:, 0:ow], in_=pB_g[:, 0:ow], func=AF.Silu),
                        inc=sa, wait=[(sq, mgu)])
            vh1 = P.emit("vector", lambda e, ow=ow: e.tensor_tensor(
                out=h1_s[:, 0:ow], in0=gsil[:, 0:ow], in1=pB_u[:, 0:ow], op=AO.mult),
                inc=sv, wait=[(sa, ag)])
            if si + 1 < len(OD_SLICES):
                mgu = emit_gu(si + 1, [(sa, ag), (sv, vh1)])
            if "h1" in taps:
                P.emit("gpsimd", lambda e, s=s, t=t, o0=o0, ow=ow: e.dma_start(
                    out=tap_t["h1"][s, 128 * t:128 * (t + 1), o0:o0 + ow], in_=h1_s[:, 0:ow]),
                    inc=st, inc_by=16, wait=[(sv, vh1)])
            _, vq1p = emit_quant(b_scr, h1_s[:, 0:ow], ow, q_out[:, 0:ow], h1_s,
                                 wv=[(sv, vh1)], wa=[(sv, vh1)], w_final=[(sq, sq.n)],
                                 scratch_guard=scr_guard[0], split=PHASEB_SPLIT, rscale_act=False)
            scr_guard[0] = vq1p
            nch = ow // 128
            for cq in range((nch + 3) // 4):
                ks = [k for k in range(4) if 4 * cq + k < nch]
                for k in ks:
                    c = 4 * cq + k
                    P.emit("tensor", lambda e, c=c, k=k: e.transpose(
                        pB_t[:, 128 * k:128 * (k + 1)], q_out[:, 128 * c:128 * (c + 1)], ident_b[:]),
                        inc=(sq if k == ks[-1] else None),
                        wait=(([(sv, sv.n), (sq, sq.n)] + ([vq1p] if cq == 0 else [])) if k == 0 else []))
                tq1 = sq.n
                P.emit("vector", lambda e, cq=cq, nk=len(ks): e.tensor_copy(
                    out=h1qT[:, 4 * cq:4 * cq + nk, :], in_=pB_t[:, 0:128 * nk].rearrange("p (k n) -> p k n", n=128)),
                    inc=sv, wait=[(sq, tq1)])
            vev = sv.n
            for k in range(nch):
                cglob = o0 // 128 + k
                for o2 in range(2):
                    P.emit("tensor", lambda e, k=k, cglob=cglob, o2=o2,
                           st_=(first_dn and k == 0),
                           sp_=(si == len(OD_SLICES) - 1 and k == nch - 1): e.matmul(
                        pB_d[:, 512 * o2:512 * (o2 + 1)], h1qT[:, k, :],
                        wdqT[:, cglob, 512 * o2:512 * (o2 + 1)], start=st_, stop=sp_),
                        inc=(sq if (k == nch - 1 and o2 == 1) else None),
                        wait=([(sv, vev)] if (k == 0 and o2 == 0) else []))
            first_dn = False
        m_dn = sq.n
        ob = (ob_t, ob_t2)[ti % 2]
        w_vo = [(so, SO_H[-2])] if len(SO_H) >= 2 else [(so, so.n)]
        vo = P.emit("vector", lambda e, xb=(xb_t, xb_t2)[ti % 2], ob=ob: e.tensor_tensor(
            out=ob[:], in0=pB_d[:, :], in1=xb[:], op=AO.add),
            inc=sv, wait=[(sq, m_dn)] + w_vo)
        VO_H.append(vo)
        P.emit("gpsimd", lambda e, s=s, t=t, ob=ob: e.dma_start(
            out=out[s, 128 * t:128 * (t + 1), :], in_=ob[:]),
            inc=so, inc_by=16, wait=[(sv, vo)])
        SO_H.append(so.n)

    phase_b_prologue()
    for s in range(NSEQ):
        for t in range(NT):
            phase_b_tile(s, t)
    pb.close()

    P.emit("gpsimd", lambda e: e.nop(), wait=[(so, so.n), (st, st.n)])
    finalize(nc, P)
    return nc, stack


def finalize(nc, P):
    with nc.Block() as block:
        @block.sync
        def _(e):
            for fn in P.ops["sync"]:
                fn(e)

        @block.vector
        def _(e):
            for fn in P.ops["vector"]:
                fn(e)

        @block.scalar
        def _(e):
            for fn in P.ops["scalar"]:
                fn(e)

        @block.tensor
        def _(e):
            for fn in P.ops["tensor"]:
                fn(e)

        @block.gpsimd
        def _(e):
            for fn in P.ops["gpsimd"]:
                fn(e)


def make_in_maps(inputs, consts=None, n_cores=N_CORES):
    if consts is None:
        consts = host_consts()
    lay = host_inputs_layout(inputs)
    x = np.ascontiguousarray(np.asarray(inputs["L0_attn_x"], dtype=np.float32))
    base = {
        "Wq": np.ascontiguousarray(np.asarray(inputs["L0_attn_Wq"], dtype=np.float32)),
        "Wk": np.ascontiguousarray(np.asarray(inputs["L0_attn_Wk"], dtype=np.float32)),
        "Wv": np.ascontiguousarray(np.asarray(inputs["L0_attn_Wv"], dtype=np.float32)),
        "Wo": np.ascontiguousarray(np.asarray(inputs["L0_attn_Wo"], dtype=np.float32)),
        "wg": np.ascontiguousarray(np.asarray(inputs["L0_ffn_w_gate"], dtype=np.float32)),
        "wu": np.ascontiguousarray(np.asarray(inputs["L0_ffn_w_up"], dtype=np.float32)),
        "wd": np.ascontiguousarray(np.asarray(inputs["L0_ffn_w_down"], dtype=np.float32)),
        "c_cos": consts["c_cos"], "c_sinp": consts["c_sinp"],
        "c_ident": consts["c_ident"], "c_dmask": consts["c_dmask"],
        "gq": lay["gq"], "gqp": lay["gqp"], "bq": lay["bq"], "bqp": lay["bqp"],
        "gk": lay["gk"], "gkp": lay["gkp"], "bk": lay["bk"], "bkp": lay["bkp"],
        "bo_rep": lay["bo_rep"],
    }
    in_maps = []
    for c in range(n_cores):
        m = dict(base)
        m["xs"] = np.ascontiguousarray(x[NSEQ * c:NSEQ * (c + 1)])
        in_maps.append(m)
    return in_maps


def kernel(**inputs):
    nc, stack = build_program()
    in_maps = make_in_maps(inputs)
    res = run_bass_kernel_spmd(nc, in_maps, core_ids=list(range(N_CORES)))
    stack.close()
    return np.concatenate([np.asarray(r["out"]) for r in res.results], axis=0).astype(np.float32)


# version 28
# speedup vs baseline: 1.1849x; 1.0032x over previous
import math
import contextlib
import numpy as np
import ml_dtypes

import concourse.bass as bass
import concourse.mybir as mybir
from concourse.bass_utils import run_bass_kernel_spmd

F32 = mybir.dt.float32
BF16 = mybir.dt.bfloat16
AF = mybir.ActivationFunctionType
AO = mybir.AluOpType
AX = mybir.AxisListType

S = 2048
D = 1024
H = 16
HD = 64
NSEQ = 4
NT = S // 128
NG = 4
I_FFN = 2816
N_CORES = 8

ENGINES = ("sync", "vector", "scalar", "tensor", "gpsimd")


def host_consts():
    pos = np.arange(S, dtype=np.float32)[:, None]
    di = np.arange(0, HD, 2, dtype=np.float32) / HD
    theta = pos * np.exp(-di * math.log(10000.0))
    cos = np.repeat(np.cos(theta), 2, axis=1).astype(np.float32)
    sin = np.repeat(np.sin(theta), 2, axis=1).astype(np.float32)
    sinp = sin.copy()
    sinp[:, :32] *= -1.0
    dmask = np.zeros((4, 128, 512), dtype=np.float32)
    for j in range(4):
        p = np.arange(128)[:, None]
        f = np.arange(512)[None, :]
        dmask[j] = ((128 * j + p) <= f).astype(np.float32)
    return {
        "c_cos": cos, "c_sinp": sinp,
        "c_ident": np.eye(128, dtype=np.float32),
        "c_dmask": dmask.astype(ml_dtypes.bfloat16),
    }


def host_inputs_layout(inp):
    out = {}
    rep = lambda a: np.ascontiguousarray(
        np.broadcast_to(np.asarray(a, np.float32).reshape(1, -1), (128, np.asarray(a).size)))
    perm = np.empty(HD, dtype=np.int64)
    perm[:32] = 2 * np.arange(32) + 1
    perm[32:] = 2 * np.arange(32)
    for nm, g, b in (("q", inp["L0_attn_q_gamma"], inp["L0_attn_q_beta"]),
                     ("k", inp["L0_attn_k_gamma"], inp["L0_attn_k_beta"])):
        g = np.asarray(g, np.float32); b = np.asarray(b, np.float32)
        out[f"g{nm}"] = rep(g)
        out[f"g{nm}p"] = rep(g[perm])
        out[f"b{nm}"] = rep(b)
        out[f"b{nm}p"] = rep(b[perm])
    out["bo_rep"] = rep(inp["L0_attn_bo"])
    return out


class Sem:
    def __init__(self, stack, nc, name):
        self.h = stack.enter_context(nc.semaphore(name))
        self.n = 0


class Prog:
    def __init__(self, nc, stack):
        self.nc = nc
        self.stack = stack
        self.ops = {e: [] for e in ENGINES}

    def sem(self, name):
        return Sem(self.stack, self.nc, name)

    def emit(self, eng, fn, inc=None, inc_by=1, wait=()):
        waits = [(w[0], w[1]) if isinstance(w, tuple) else (w, w.n) for w in wait]
        if inc is not None:
            inc.n += inc_by
        post = inc.n if inc is not None else None

        def run(e, fn=fn, waits=waits, inc=inc, inc_by=inc_by):
            for s_, v_ in waits:
                if v_ > 0:
                    e.wait_ge(s_.h, v_)
            r = fn(e)
            if inc is not None:
                r.then_inc(inc.h, inc_by)
        self.ops[eng].append(run)
        return post


def build_program(taps=(), stop_after=None, PHASEB_SPLIT=False):
    nc = bass.Bass("TRN2", target_bir_lowering=False)
    stack = contextlib.ExitStack()

    def din(name, shape, dtype=F32):
        return nc.dram_tensor(name, list(shape), dtype, kind="ExternalInput").ap()

    xs = din("xs", (NSEQ, S, D))
    Wq = din("Wq", (D, D)); Wk = din("Wk", (D, D)); Wv = din("Wv", (D, D)); Wo = din("Wo", (D, D))
    wg = din("wg", (I_FFN, D)); wu = din("wu", (I_FFN, D)); wd = din("wd", (D, I_FFN))
    c_cos = din("c_cos", (S, HD)); c_sinp = din("c_sinp", (S, HD))
    c_ident = din("c_ident", (128, 128))
    c_dmask = din("c_dmask", (4, 128, 512), BF16)
    gq = din("gq", (128, HD)); gqp = din("gqp", (128, HD)); bq = din("bq", (128, HD)); bqp = din("bqp", (128, HD))
    gk = din("gk", (128, HD)); gkp = din("gkp", (128, HD)); bk = din("bk", (128, HD)); bkp = din("bkp", (128, HD))
    bo_rep = din("bo_rep", (128, D))

    out = nc.dram_tensor("out", [NSEQ, S, D], F32, kind="ExternalOutput").ap()
    xmid = nc.dram_tensor("xmid", [NSEQ, S, D], F32, kind="Internal").ap()
    rs_scr = nc.dram_tensor("rs_scr", [H, 512], F32, kind="Internal").ap()
    nm_scr = nc.dram_tensor("nm_scr", [H], F32, kind="Internal").ap()
    wq_scr = nc.dram_tensor("wq_scr", [2 * 128 * 8 * I_FFN + 128 * 22 * D], BF16, kind="Internal").ap()
    WGT_OFF = 0
    WUT_OFF = 128 * 8 * I_FFN
    WDT_OFF = 2 * 128 * 8 * I_FFN

    taps = set(taps)
    tap_t = {}
    for nm, shp in (("h", [NSEQ, S, D]), ("qr", [NSEQ, S, D]), ("kr", [NSEQ, S, D]),
                    ("attnT", [NSEQ, NG, 1024, 512]), ("x1", [NSEQ, S, D]),
                    ("h2q", [NSEQ, S, D]), ("h1", [NSEQ, S, I_FFN]), ("xb", [NSEQ, S, D])):
        if nm in taps:
            tap_t[nm] = nc.dram_tensor("tap_" + nm, shp, F32, kind="ExternalOutput").ap()

    P = Prog(nc, stack)
    ld = P.sem("ld")
    st = P.sem("st")
    sv = P.sem("sv")
    sa = P.sem("sa")
    sq = P.sem("sq")
    so = P.sem("so")
    sp = P.sem("sp")

    sb = lambda name, shape, dtype: stack.enter_context(nc.sbuf_tensor(name, list(shape), dtype))
    ident_f = sb("ident_f", (128, 128), F32)
    ident_b = sb("ident_b", (128, 128), BF16)
    dmask_b = sb("dmask_b", (128, 4 * 512), BF16)
    gq_s = sb("gq_s", (128, HD), F32); gqp_s = sb("gqp_s", (128, HD), F32)
    bq_s = sb("bq_s", (128, HD), F32); bqp_s = sb("bqp_s", (128, HD), F32)
    gk_s = sb("gk_s", (128, HD), F32); gkp_s = sb("gkp_s", (128, HD), F32)
    bk_s = sb("bk_s", (128, HD), F32); bkp_s = sb("bkp_s", (128, HD), F32)
    bo_s = sb("bo_s", (128, D), F32)
    negm_bc = sb("negm_bc", (128, H), F32)
    negmp = sb("negmp", (128, H // 2), F32)

    def ldc(dst, src):
        P.emit("sync", lambda e, d=dst, s=src: e.dma_start(out=d, in_=s), inc=ld, inc_by=16)
    ldc(ident_f[:], c_ident[:, :])
    ldc(dmask_b[:].rearrange("p (j n) -> p j n", n=512), c_dmask[:, :, :].transpose([1, 0, 2]))
    for d_, s_ in ((gq_s, gq), (gqp_s, gqp), (bq_s, bq), (bqp_s, bqp),
                   (gk_s, gk), (gkp_s, gkp), (bk_s, bk), (bkp_s, bkp), (bo_s, bo_rep)):
        ldc(d_[:], s_[:, :])
    ld_consts = ld.n
    v_identb = P.emit("vector", lambda e: e.tensor_copy(out=ident_b[:], in_=ident_f[:]),
                      inc=sv, wait=[(ld, ld_consts)])

    watt = contextlib.ExitStack()
    WqT = watt.enter_context(nc.sbuf_tensor("WqT", [128, 8, D], BF16))
    WkT = watt.enter_context(nc.sbuf_tensor("WkT", [128, 8, D], BF16))
    WvT = watt.enter_context(nc.sbuf_tensor("WvT", [128, 8, D], BF16))
    WoT = watt.enter_context(nc.sbuf_tensor("WoT", [128, 8, D], BF16))

    p0 = contextlib.ExitStack()
    wstage = p0.enter_context(nc.sbuf_tensor("wstage", [128, 8, D], F32))
    wcast = p0.enter_context(nc.sbuf_tensor("wcast", [128, 8, D], BF16))
    ptw = p0.enter_context(nc.psum_tensor("ptw", [128, 512], BF16))

    for wsrc, wdstT in ((Wq, WqT), (Wk, WkT), (Wv, WvT), (Wo, WoT)):
        l0 = P.emit("sync", lambda e, w=wsrc: e.dma_start(
            out=wstage[:], in_=w[:, :].rearrange("(r p) n -> p r n", p=128)),
            inc=ld, inc_by=16, wait=[(sv, sv.n)])
        cv = P.emit("vector", lambda e: e.tensor_copy(out=wcast[:], in_=wstage[:]),
                    inc=sv, wait=[(ld, l0), (sv, v_identb), (sq, sq.n)])
        for r in range(8):
            for cq in range(2):
                w_first = [(sv, cv), (sv, sv.n)]
                for k in range(4):
                    c = 4 * cq + k
                    P.emit("tensor", lambda e, r=r, c=c, k=k: e.transpose(
                        ptw[:, 128 * k:128 * (k + 1)],
                        wcast[:, r, 128 * c:128 * (c + 1)], ident_b[:]),
                        inc=(sq if k == 3 else None),
                        wait=(w_first if k == 0 else []))
                tq = sq.n
                P.emit("vector", lambda e, w=wdstT, r=r, cq=cq: e.tensor_copy(
                    out=w[:, 4 * cq:4 * cq + 4, 128 * r:128 * (r + 1)],
                    in_=ptw[:].rearrange("p (k n) -> p k n", n=128)),
                    inc=sv, wait=[(sq, tq)])

    wbar = p0.enter_context(nc.sbuf_tensor("wbar", [128, 8, H], F32))
    for wT in (WqT, WkT):
        vr = P.emit("vector", lambda e, wT=wT: e.tensor_reduce(
            out=wbar[:], in_=wT[:].rearrange("p c (h j) -> p c h j", j=64),
            axis=AX.X, op=AO.add), inc=sv, wait=[(sq, sq.n), (sv, sv.n)])
        P.emit("vector", lambda e: e.tensor_scalar_mul(out=wbar[:], in0=wbar[:], scalar1=1.0 / 64),
               inc=sv, wait=[(sv, vr)])
        P.emit("vector", lambda e, wT=wT: e.tensor_tensor(
            out=wT[:].rearrange("p c (h j) -> p c h j", j=64),
            in0=wT[:].rearrange("p c (h j) -> p c h j", j=64),
            in1=wbar[:].unsqueeze(3).broadcast_to([128, 8, H, 64]), op=AO.subtract),
            inc=sv, wait=[(sv, sv.n)])

    p0.close()
    fqs = contextlib.ExitStack()
    f_ins = [fqs.enter_context(nc.sbuf_tensor(f"f_in{pi}", [128, I_FFN], F32)) for pi in range(2)]
    f_qs = [fqs.enter_context(nc.sbuf_tensor(f"f_q{pi}", [128, I_FFN], BF16)) for pi in range(2)]
    f_ax2 = fqs.enter_context(nc.sbuf_tensor("f_ax2", [128, I_FFN], F32))
    f_t2 = fqs.enter_context(nc.sbuf_tensor("f_t2", [128, I_FFN], F32))
    f_y = fqs.enter_context(nc.sbuf_tensor("f_y", [128, I_FFN], F32))
    f_y2 = fqs.enter_context(nc.sbuf_tensor("f_y2", [128, I_FFN], F32))
    f_sg = fqs.enter_context(nc.sbuf_tensor("f_sg", [128, I_FFN], BF16))
    f_m = fqs.enter_context(nc.sbuf_tensor("f_m", [128, I_FFN], BF16))
    f_bm = fqs.enter_context(nc.sbuf_tensor("f_bm", [128, I_FFN // 16], F32))
    f_sc = fqs.enter_context(nc.sbuf_tensor("f_sc", [128, I_FFN // 16], F32))
    f_rs = fqs.enter_context(nc.sbuf_tensor("f_rs", [128, I_FFN // 16], F32))
    f_s2 = fqs.enter_context(nc.sbuf_tensor("f_s2", [128, I_FFN // 16], BF16))
    f_scr = (f_ax2, f_t2, f_y, f_y2, f_sg, f_m, f_bm, f_sc, f_rs, f_s2)
    f_ev = fqs.enter_context(nc.sbuf_tensor("f_ev", [128, 512], BF16))
    ptf = fqs.enter_context(nc.psum_tensor("ptf", [128, 512], BF16))

    def emit_staircase(eng, sem, t2_ap, y_ap, y2_ap, n, prot, wait_first=()):
        i1 = y2_ap.bitcast(mybir.dt.int32)
        pw = (lambda: [(sem, sem.n)]) if prot else (lambda: [])
        P.emit(eng, lambda e: e.tensor_copy(out=i1, in_=t2_ap), inc=sem, wait=pw() + list(wait_first))
        P.emit(eng, lambda e: e.tensor_scalar(out=y_ap, in0=i1, scalar1=4.0, scalar2=0.5,
                                              op0=AO.min, op1=AO.mult), inc=sem, wait=pw() or [(sem, sem.n)])
        P.emit(eng, lambda e: e.tensor_scalar(out=i1, in0=t2_ap, scalar1=0.5, scalar2=-2.0,
                                              op0=AO.mult, op1=AO.add), inc=sem, wait=pw() or [(sem, sem.n)])
        P.emit(eng, lambda e: e.tensor_scalar(out=y2_ap, in0=i1, scalar1=0.0, scalar2=2.0,
                                              op0=AO.max, op1=AO.min), inc=sem, wait=pw() or [(sem, sem.n)])
        P.emit(eng, lambda e: e.tensor_tensor(out=y_ap, in0=y_ap, in1=y2_ap, op=AO.add),
               inc=sem, wait=pw() or [(sem, sem.n)])
        P.emit(eng, lambda e: e.tensor_scalar(out=i1, in0=t2_ap, scalar1=0.5, scalar2=-4.5,
                                              op0=AO.mult, op1=AO.add), inc=sem, wait=pw() or [(sem, sem.n)])
        P.emit(eng, lambda e: e.tensor_scalar(out=y2_ap, in0=i1, scalar1=0.0, scalar2=1.0,
                                              op0=AO.max, op1=AO.min), inc=sem, wait=pw() or [(sem, sem.n)])
        P.emit(eng, lambda e: e.scalar_tensor_tensor(out=y_ap, in0=y2_ap, scalar=2.0, in1=y_ap,
                                                     op0=AO.mult, op1=AO.add), inc=sem, wait=pw() or [(sem, sem.n)])

    def emit_quant(scr, src_ap, n, dst_flat_bf, sign_src, wv, wa, w_final=(), split=True, scratch_guard=None, rscale_act=True):
        ax2, t2_, y_, y2_, sg_, m_, bm_, sc_, rs_, s2_ = scr
        nb = n // 16
        peng, psem = ("gpsimd", sp) if split else ("vector", sv)
        gw = ([scratch_guard] if scratch_guard else [(sp, sp.n)]) if split else ([scratch_guard] if scratch_guard else [])
        a0 = P.emit("scalar", lambda e: e.activation(out=ax2[:, :n], in_=src_ap, func=AF.Abs, scale=2.0),
                    inc=sa, wait=list(wa) + gw)
        P.emit("vector", lambda e: e.tensor_reduce(
            out=bm_[:, :nb], in_=src_ap.rearrange("p (b k) -> p b k", k=16),
            axis=AX.X, op=AO.max, apply_absolute_value=True), inc=sv, wait=list(wv) + gw)
        P.emit("vector", lambda e: e.tensor_scalar(out=sc_[:, :nb], in0=bm_[:, :nb],
                                                   scalar1=1.0 / 6.0, scalar2=1e-12,
                                                   op0=AO.mult, op1=AO.max), inc=sv, wait=[(sv, sv.n)] + gw)
        sc_done = sv.n
        if rscale_act:
            aL = P.emit("scalar", lambda e: e.activation(out=rs_[:, :nb], in_=sc_[:, :nb], func=AF.Ln),
                        inc=sa, wait=[(sv, sc_done)] + gw)
            aE = P.emit("scalar", lambda e: e.activation(out=rs_[:, :nb], in_=rs_[:, :nb], func=AF.Exp, scale=-1.0),
                        inc=sa, wait=[(sa, aL)])
            rs_ready = (sa, aE)
        else:
            vre = P.emit("vector", lambda e: e.reciprocal(out=rs_[:, :nb], in_=sc_[:, :nb]),
                         inc=sv, wait=[(sv, sc_done)] + gw)
            rs_ready = (sv, vre)
            aE = sa.n
        P.emit(peng, lambda e: e.tensor_scalar_mul(out=s2_[:, :nb], in0=sc_[:, :nb], scalar1=2.0),
               inc=psem, wait=[(sv, sc_done)] + ([(psem, psem.n)] if True else []))
        t2p = P.emit(peng, lambda e: e.tensor_tensor(
            out=t2_.rearrange("p (b k) -> p b k", k=16)[:, 0:nb, :],
            in0=ax2.rearrange("p (b k) -> p b k", k=16)[:, 0:nb, :],
            in1=rs_[:, 0:nb].unsqueeze(2).broadcast_to([128, nb, 16]), op=AO.mult),
            inc=psem, wait=[rs_ready] + ([(psem, psem.n)] if split else []))
        P.emit(peng, lambda e: e.tensor_scalar(out=sg_[:, :n], in0=sign_src[:, :n], scalar1=0.0, scalar2=0.5,
                                               op0=AO.is_gt, op1=AO.subtract),
               inc=psem, wait=(list(wa) if split else []))
        sgp = psem.n
        emit_staircase("vector", sv, t2_[:, :n], y_[:, :n], y2_[:, :n], n,
                       prot=(n < 512), wait_first=([(psem, t2p)] if split else []) + gw)
        y_done = sv.n
        P.emit(peng, lambda e: e.tensor_tensor(out=m_[:, :n], in0=y_[:, :n], in1=sg_[:, :n], op=AO.mult),
               inc=psem, wait=([(sv, y_done), (psem, sgp)] if split else ([(sv, sv.n)] if n < 512 else [])))
        vqp = P.emit(peng, lambda e: e.tensor_tensor(
            out=dst_flat_bf.rearrange("p (b k) -> p b k", k=16)[:, 0:nb, :],
            in0=m_.rearrange("p (b k) -> p b k", k=16)[:, 0:nb, :],
            in1=s2_[:, 0:nb].unsqueeze(2).broadcast_to([128, nb, 16]), op=AO.mult),
            inc=psem, wait=list(w_final) + ([(psem, psem.n)] if split else ([(sv, sv.n)] if n < 512 else [])))
        return (None, (psem, vqp))

    f_scr = (f_ax2, f_t2, f_y, f_y2, f_sg, f_m, f_bm, f_sc, f_rs, f_s2)

    F_GUARD = [(sq, sq.n), (sq, sq.n)]
    ABS_GUARD = [(sv, sv.n)]
    SCR_GUARD = [(sa, sa.n)]

    def quant_weight(wsrc, n_rows, n_cols, dst_off, tchunks, chain_idx):
        for r in range(n_rows // 128):
            pi = chain_idx[0] % 2
            chain_idx[0] += 1
            f_in = f_ins[pi]
            f_q = f_qs[pi]
            gwait = [F_GUARD[pi], (sv, sv.n if chain_idx[0] <= 2 else 0)] if chain_idx[0] <= 2 else [F_GUARD[pi]]
            l0 = P.emit("sync", lambda e, r=r, f_in=f_in: e.dma_start(
                out=f_in[:, :n_cols], in_=wsrc[128 * r:128 * (r + 1), :]),
                inc=ld, inc_by=16, wait=gwait)
            _, vqp = emit_quant(f_scr, f_in[:, :n_cols], n_cols, f_q[:, :n_cols], f_in,
                                wv=[(ld, l0)], wa=([(ld, l0)] + [ABS_GUARD[0]] if ABS_GUARD[0] else [(ld, l0)]),
                                split=False, scratch_guard=SCR_GUARD[0])
            ABS_GUARD[0] = (sv, sv.n)
            SCR_GUARD[0] = (sa, sa.n)
            ncol_ch = n_cols // 128
            for cq in range((ncol_ch + 3) // 4):
                ks = [k for k in range(4) if 4 * cq + k < ncol_ch]
                for k in ks:
                    c = 4 * cq + k
                    P.emit("tensor", lambda e, c=c, k=k, f_q=f_q: e.transpose(
                        ptf[:, 128 * k:128 * (k + 1)],
                        f_q[:, 128 * c:128 * (c + 1)], ident_b[:]),
                        inc=(sq if k == ks[-1] else None),
                        wait=(([(sv, sv.n), vqp]) if k == 0 else []))
                tq = sq.n
                ev = P.emit("vector", lambda e, nk=len(ks): e.tensor_copy(
                    out=f_ev[:, :128 * nk], in_=ptf[:, :128 * nk]),
                    inc=sv, wait=[(sq, tq), (st, st.n)])

                def store(e, cq=cq, nk=len(ks), r=r, dst_off=dst_off, tchunks=tchunks, nrows=n_rows):
                    dst = wq_scr[dst_off:dst_off + 128 * tchunks * nrows].rearrange(
                        "(p c n) -> p c n", p=128, c=tchunks)
                    return e.dma_start(out=dst[:, 4 * cq:4 * cq + nk, 128 * r:128 * (r + 1)],
                                       in_=f_ev[:, :128 * nk].rearrange("p (k n) -> p k n", n=128))
                P.emit("gpsimd", store, inc=st, inc_by=16, wait=[(sv, ev)])
            F_GUARD[pi] = (sq, sq.n)

    _ci = [0]
    quant_weight(wg, I_FFN, D, WGT_OFF, 8, _ci)
    quant_weight(wu, I_FFN, D, WUT_OFF, 8, _ci)
    quant_weight(wd, D, I_FFN, WDT_OFF, 22, _ci)
    fqs.close()
    barrier0 = [(sv, sv.n), (sq, sq.n), (st, st.n), (ld, ld.n), (sa, sa.n)]

    pa = contextlib.ExitStack()
    psb = lambda name, shape, dtype: pa.enter_context(nc.sbuf_tensor(name, list(shape), dtype))
    kT_all = psb("kT_all", (128, 8, S), BF16)
    Vp = psb("Vp", (128, NT, H * 65), BF16)
    qT_g = psb("qT_g", (128, 8, 512), BF16)
    hT_g = psb("hT_g", (128, 8, 512), BF16)
    attnT_g = psb("attnT_g", (128, 8, 512), BF16)
    x_t = psb("x_t", (128, D), F32)
    h_bf = psb("h_bf", (128, D), BF16)
    h_bf4 = [h_bf] + [psb(f"h_bf{i}", (128, D), BF16) for i in (1, 2)] + [h_bf]
    qkp_bf = psb("qkp_bf", (128, D), BF16)
    z2 = psb("z2", (128, D), F32)
    zu = psb("zu", (128, D), F32)
    zv = psb("zv", (128, D), F32)
    zqc0 = psb("zqc0", (128, D), BF16); zkc0 = psb("zkc0", (128, D), BF16)
    zqc1 = psb("zqc1", (128, D), BF16); zkc1 = psb("zkc1", (128, D), BF16)
    cos_t = psb("cos_t", (128, HD), F32)
    sinp_t = psb("sinp_t", (128, HD), F32)
    A_t = psb("A_t", (128, HD), F32); B_t = psb("B_t", (128, HD), F32)
    D_t = psb("D_t", (128, HD), F32); e1_t = psb("e1_t", (128, HD), F32)
    PP0 = psb("PP0", (128, 1024), BF16)
    PP1 = psb("PP1", (128, 1024), BF16)
    rb_s = psb("rb_s", (128, 512), F32)
    rs2_s = psb("rs2_s", (1, 1024), F32)
    rsA = rs2_s[:, 0:512]
    rsB = rs2_s[:, 512:1024]
    st16 = psb("st16", (128, 16), F32)
    st16b = psb("st16b", (128, 16), F32)
    mu_h = psb("mu_h", (128, 16), F32)
    rstd_h = psb("rstd_h", (128, 16), F32)
    qm2_a = psb("qm2_a", (128, 16), F32); qm2_b = psb("qm2_b", (128, 16), F32)
    km2_a = psb("km2_a", (128, 16), F32); km2_b = psb("km2_b", (128, 16), F32)
    s1_c = psb("s1_c", (128, 1), F32)
    s2_c = psb("s2_c", (128, 1), F32)
    mu_c = psb("mu_c", (128, 1), F32)
    rstd_c = psb("rstd_c", (128, 1), F32)
    nm16 = psb("nm16", (16, 1), F32)
    nm16b = psb("nm16b", (16, 1), F32)

    pA_q = pa.enter_context(nc.psum_tensor("pA_q", [128, 1024], F32))
    pA_k = pa.enter_context(nc.psum_tensor("pA_k", [128, 1024], F32))
    pA_v = pa.enter_context(nc.psum_tensor("pA_v", [128, 1024], F32))
    pA_t = pa.enter_context(nc.psum_tensor("pA_t", [128, 1024], BF16))
    pA_o = pa.enter_context(nc.psum_tensor("pA_o", [65, 512], F32))
    pA_s = pA_t[:].bitcast(F32)[0:16, 0:128]
    pA_oB = pA_t[:].bitcast(F32)[0:65, 0:512]

    for eng in ENGINES:
        P.emit(eng, lambda e: e.nop(), wait=barrier0)

    v_ones = P.emit("vector", lambda e: e.memset(
        Vp[:].rearrange("p t (h j) -> p t h j", j=65)[:, :, :, 64:65], 1.0), inc=sv)

    NM_WAIT = {}

    def emit_ln_stats(src_f32, n, l_wait):
        P.emit("vector", lambda e: e.reduce_sum(out=s1_c[:], in_=src_f32, axis=AX.X),
               inc=sv, wait=[(ld, l_wait)])
        a0 = P.emit("scalar", lambda e, n=n: e.activation(out=zv[:, :n], in_=src_f32, func=AF.Square),
                    inc=sa, wait=[(ld, l_wait), (sv, sv.n)])
        P.emit("vector", lambda e, n=n: e.reduce_sum(out=s2_c[:], in_=zv[:, :n], axis=AX.X),
               inc=sv, wait=[(sa, a0)])
        P.emit("vector", lambda e, n=n: e.tensor_scalar_mul(out=mu_c[:], in0=s1_c[:], scalar1=1.0 / n),
               inc=sv, wait=[(sv, sv.n)])
        P.emit("vector", lambda e, n=n: e.tensor_scalar_mul(out=s2_c[:], in0=s2_c[:], scalar1=1.0 / n),
               inc=sv, wait=[(sv, sv.n)])
        P.emit("vector", lambda e: e.scalar_tensor_tensor(
            out=s1_c[:], in0=mu_c[:], scalar=mu_c[:, 0:1], in1=s2_c[:],
            op0=AO.mult, op1=AO.subtract), inc=sv, wait=[(sv, sv.n)])
        P.emit("vector", lambda e: e.tensor_scalar(out=s1_c[:], in0=s1_c[:], scalar1=-1.0, scalar2=1e-5,
                                                   op0=AO.mult, op1=AO.add), inc=sv, wait=[(sv, sv.n)])
        vpe = sv.n
        aln = P.emit("scalar", lambda e: e.activation(out=s2_c[:], in_=s1_c[:], func=AF.Ln),
                     inc=sa, wait=[(sv, vpe)])
        va = P.emit("scalar", lambda e: e.activation(out=rstd_c[:], in_=s2_c[:], func=AF.Exp, scale=-0.5),
                    inc=sa, wait=[(sa, aln)])
        return va, sv.n

    def phase_a_seq(s):
        km_cnt = 0
        Z_GUARD = [None, None]
        RB_GUARD = [(sv, 0)]
        FD_DONE = set()
        STH = []

        def front_dve(t, slot):
            FD_DONE.add(t)
            xbuf = x_t if slot % 2 == 0 else zu
            l_x = P.emit("sync", lambda e, t=t, xbuf=xbuf: e.dma_start(
                out=xbuf[:], in_=xs[s, 128 * t:128 * (t + 1), :]),
                inc=ld, inc_by=16, wait=[(sa, sa.n), (sv, sv.n), (st, st.n)])
            va, vb = emit_ln_stats(xbuf[:], D, l_x)
            ah = P.emit("vector", lambda e, slot=slot, xbuf=xbuf: e.tensor_scalar(
                out=h_bf4[slot][:], in0=xbuf[:], scalar1=mu_c[:, 0:1], scalar2=rstd_c[:, 0:1],
                op0=AO.subtract, op1=AO.mult),
                inc=sv, wait=[(sa, va), (sv, sv.n), (sq, sq.n), (st, st.n)])
            if "h" in taps:
                P.emit("gpsimd", lambda e, s=s, t=t, slot=slot: e.dma_start(
                    out=tap_t["h"][s, 128 * t:128 * (t + 1), :], in_=h_bf4[slot][:]),
                    inc=st, inc_by=16, wait=[(sv, ah)])

        for g in range(NG):
            qm_cnt = 0

            def frontend(t, tt):
                if t not in FD_DONE:
                    front_dve(t, tt)
                for cq in range(2):
                    for k in range(4):
                        c = 4 * cq + k
                        P.emit("tensor", lambda e, c=c, k=k, tt=tt: e.transpose(
                            pA_t[:, 128 * k:128 * (k + 1)],
                            h_bf4[tt][:, 128 * c:128 * (c + 1)], ident_b[:]),
                            inc=(sq if k == 3 else None),
                            wait=([(sv, sv.n)] if k == 0 else []))
                    tq = sq.n
                    P.emit("vector", lambda e, cq=cq, tt=tt: e.tensor_copy(
                        out=hT_g[:, 4 * cq:4 * cq + 4, 128 * tt:128 * (tt + 1)],
                        in_=pA_t[:, 0:512].rearrange("p (k n) -> p k n", n=128)),
                        inc=sv, wait=[(sq, tq)])
                v_hT = sv.n
                first_w = [(sv, v_hT), (sa, sa.n), (sq, sq.n)]
                for (pdst, wT) in ((pA_q, WqT), (pA_k, WkT), (pA_v, WvT)):
                    for o2 in range(2):
                        for c in range(8):
                            P.emit("tensor", lambda e, pdst=pdst, wT=wT, o2=o2, c=c, tt=tt: e.matmul(
                                pdst[:, 512 * o2:512 * (o2 + 1)],
                                hT_g[:, c, 128 * tt:128 * (tt + 1)],
                                wT[:, c, 512 * o2:512 * (o2 + 1)],
                                start=(c == 0), stop=(c == 7)),
                                inc=(sq if c == 7 else None),
                                wait=(first_w if (c == 0 and first_w) else []))
                            first_w = []
                q_all = sq.n
                P.emit("vector", lambda e, t=t: e.tensor_copy(
                    out=Vp[:, t, :].rearrange("p (h j) -> p h j", j=65)[:, :, 0:64],
                    in_=pA_v[:].rearrange("p (h j) -> p h j", j=64)),
                    inc=sv, wait=[(sq, q_all), (sv, v_ones)])
                return q_all

            def emit_evacs(q_all, par, guard):
                zq_ = (zqc0, zqc1)[par]; zk_ = (zkc0, zkc1)[par]
                w0 = [(sq, q_all)] + ([guard] if guard else [])
                eq = P.emit("scalar", lambda e, zq_=zq_: e.activation(out=zq_[:], in_=pA_q[:, :], func=AF.Copy),
                            inc=sa, wait=w0)
                ek = P.emit("scalar", lambda e, zk_=zk_: e.activation(out=zk_[:], in_=pA_k[:, :], func=AF.Copy),
                            inc=sa)
                return ek

            def backend(t, tt, ek, par):
                nonlocal qm_cnt, km_cnt
                zqc = (zqc0, zqc1)[par]; zkc = (zkc0, zkc1)[par]
                P.emit("sync", lambda e, t=t: e.dma_start(
                    out=cos_t[:], in_=c_cos[128 * t:128 * (t + 1), :]), inc=ld, inc_by=16,
                    wait=[(sv, sv.n)])
                l_cs = P.emit("sync", lambda e, t=t: e.dma_start(
                    out=sinp_t[:], in_=c_sinp[128 * t:128 * (t + 1), :]), inc=ld, inc_by=16)
                for which, pcs, g_s_, gp_s_, b_s_, bp_s_, dst_bf in (
                        ("q", zqc, gq_s, gqp_s, bq_s, bqp_s, qkp_bf),
                        ("k", zkc, gk_s, gkp_s, bk_s, bkp_s, qkp_bf)):
                    pcv = pcs[:].rearrange("p (h j) -> p h j", j=64)
                    a1 = P.emit("scalar", lambda e, pcs=pcs: e.activation(
                        out=zu[:], in_=pcs[:], func=AF.Square), inc=sa, wait=[(sa, ek), (sv, sv.n), (st, st.n)])
                    P.emit("vector", lambda e: e.reduce_sum(
                        out=st16[:], in_=zu[:].rearrange("p (h j) -> p h j", j=64), axis=AX.X),
                        inc=sv, wait=[(sa, a1), (sv, sv.n)])
                    P.emit("vector", lambda e: e.tensor_scalar(out=st16[:], in0=st16[:], scalar1=1.0 / 64, scalar2=1e-5,
                                                               op0=AO.mult, op1=AO.add),
                           inc=sv, wait=[(sv, sv.n)])
                    aln2 = P.emit("scalar", lambda e: e.activation(out=st16b[:], in_=st16[:], func=AF.Ln),
                                  inc=sa, wait=[(sv, sv.n)])
                    a2 = P.emit("scalar", lambda e: e.activation(out=rstd_h[:], in_=st16b[:], func=AF.Exp, scale=-0.5),
                                inc=sa, wait=[(sa, aln2)])
                    P.emit("vector", lambda e, gg=g_s_: e.tensor_tensor(out=A_t[:], in0=cos_t[:], in1=gg[:], op=AO.mult),
                           inc=sv, wait=[(ld, l_cs)])
                    P.emit("vector", lambda e, gp=gp_s_: e.tensor_tensor(out=B_t[:], in0=sinp_t[:], in1=gp[:], op=AO.mult), inc=sv)
                    P.emit("vector", lambda e, bb=b_s_: e.tensor_tensor(out=e1_t[:], in0=cos_t[:], in1=bb[:], op=AO.mult), inc=sv)
                    P.emit("vector", lambda e, bp=bp_s_: e.tensor_tensor(out=D_t[:], in0=sinp_t[:], in1=bp[:], op=AO.mult), inc=sv)
                    P.emit("vector", lambda e: e.tensor_tensor(out=D_t[:], in0=D_t[:], in1=e1_t[:], op=AO.add),
                           inc=sv, wait=[(sv, sv.n)])
                    hb = lambda a: a.unsqueeze(2).broadcast_to([128, 16, 64])
                    jb = lambda a: a.unsqueeze(1).broadcast_to([128, 16, 64])
                    jb32 = lambda a: a.unsqueeze(1).broadcast_to([128, 16, 32])
                    z2v = z2[:].rearrange("p (h j) -> p h j", j=64)
                    zuv = zu[:].rearrange("p (h j) -> p h j", j=64)
                    zvv = zv[:].rearrange("p (h j) -> p h j", j=64)
                    P.emit("vector", lambda e, pcv=pcv, z2v=z2v: e.tensor_tensor(
                        out=z2v, in0=pcv, in1=hb(rstd_h[:]), op=AO.mult), inc=sv, wait=[(sa, a2), (sv, sv.n), (st, st.n)])
                    P.emit("vector", lambda e, z2v=z2v, zuv=zuv: e.tensor_tensor(
                        out=zuv, in0=z2v, in1=jb(A_t[:]), op=AO.mult), inc=sv)
                    P.emit("vector", lambda e, z2v=z2v, zvv=zvv: e.tensor_tensor(
                        out=zvv[:, :, 0:32], in0=z2v[:, :, 1::2], in1=jb32(B_t[:, 0:32]), op=AO.mult), inc=sv)
                    P.emit("vector", lambda e, z2v=z2v, zvv=zvv: e.tensor_tensor(
                        out=zvv[:, :, 32:64], in0=z2v[:, :, 0::2], in1=jb32(B_t[:, 32:64]), op=AO.mult), inc=sv)
                    P.emit("vector", lambda e: e.tensor_tensor(out=zu[:], in0=zu[:], in1=zv[:], op=AO.add), inc=sv)
                    vq = P.emit("vector", lambda e, dst=dst_bf, zuv=zuv: e.tensor_tensor(
                        out=dst[:].rearrange("p (h j) -> p h j", j=64), in0=zuv, in1=jb(D_t[:]), op=AO.add),
                        inc=sv, wait=[(sq, sq.n), (st, st.n), (sa, sa.n)])
                    a3 = P.emit("scalar", lambda e, dst=dst_bf: e.activation(
                        out=zu[:], in_=dst[:], func=AF.Square), inc=sa, wait=[(sv, vq)])
                    P.emit("vector", lambda e: e.reduce_sum(
                        out=st16[:], in_=zu[:].rearrange("p (h j) -> p h j", j=64), axis=AX.X),
                        inc=sv, wait=[(sa, a3), (sv, sv.n)])
                    if which == "q":
                        srcb, dstb = (qm2_a, qm2_b) if qm_cnt % 2 else (qm2_b, qm2_a)
                        if qm_cnt == 0:
                            P.emit("vector", lambda e, d=dstb: e.tensor_copy(out=d[:], in_=st16[:]),
                                   inc=sv, wait=[(sv, sv.n)])
                        else:
                            P.emit("vector", lambda e, s_=srcb, d=dstb: e.tensor_tensor(
                                out=d[:], in0=s_[:], in1=st16[:], op=AO.max), inc=sv, wait=[(sv, sv.n)])
                        qm_cnt += 1
                    else:
                        srcb, dstb = (km2_a, km2_b) if km_cnt % 2 else (km2_b, km2_a)
                        if km_cnt == 0:
                            P.emit("vector", lambda e, d=dstb: e.tensor_copy(out=d[:], in_=st16[:]),
                                   inc=sv, wait=[(sv, sv.n)])
                        else:
                            P.emit("vector", lambda e, s_=srcb, d=dstb: e.tensor_tensor(
                                out=d[:], in0=s_[:], in1=st16[:], op=AO.max), inc=sv, wait=[(sv, sv.n)])
                        km_cnt += 1
                    if (which == "q" and "qr" in taps) or (which == "k" and "kr" in taps):
                        nm = "qr" if which == "q" else "kr"
                        P.emit("gpsimd", lambda e, s=s, t=t, dst=dst_bf, nm=nm: e.dma_start(
                            out=tap_t[nm][s, 128 * t:128 * (t + 1), :], in_=dst[:]),
                            inc=st, inc_by=16, wait=[(sv, vq)])
                    tdst = qT_g if which == "q" else kT_all
                    toff = 128 * tt if which == "q" else 128 * t
                    for cq in range(2):
                        for k in range(4):
                            c = 4 * cq + k
                            P.emit("tensor", lambda e, c=c, k=k, dst=dst_bf: e.transpose(
                                pA_t[:, 128 * k:128 * (k + 1)],
                                dst[:, 128 * c:128 * (c + 1)], ident_b[:]),
                                inc=(sq if k == 3 else None),
                                wait=([(sv, sv.n)] if k == 0 else []))
                        tq2 = sq.n
                        P.emit("vector", lambda e, cq=cq, td=tdst, to=toff: e.tensor_copy(
                            out=td[:, 4 * cq:4 * cq + 4, to:to + 128],
                            in_=pA_t[:, 0:512].rearrange("p (k n) -> p k n", n=128)),
                            inc=sv, wait=[(sq, tq2)])

            pend = None
            for tt in range(4):
                t = 4 * g + tt
                q_all_t = frontend(t, tt)
                if pend is not None:
                    backend(pend[0], pend[1], pend[2], pend[3])
                    Z_GUARD[pend[3]] = (sv, sv.n)
                ek_t = emit_evacs(q_all_t, tt % 2, Z_GUARD[tt % 2])
                pend = (t, tt, ek_t, tt % 2)
            backend(pend[0], pend[1], pend[2], pend[3])
            Z_GUARD[pend[3]] = (sv, sv.n)

            qmb = qm2_b if ((qm_cnt - 1) % 2 == 1) else qm2_a
            kmb = km2_b if ((km_cnt - 1) % 2 == 1) else km2_a
            tqs = P.emit("tensor", lambda e, qmb=qmb: e.transpose(
                pA_s, qmb[:], ident_f[:]), inc=sq, wait=[(sv, sv.n)])
            P.emit("vector", lambda e: e.reduce_max(out=nm16[:], in_=pA_s, axis=AX.X),
                   inc=sv, wait=[(sq, tqs), (sv, sv.n)])
            tks = P.emit("tensor", lambda e, kmb=kmb: e.transpose(
                pA_s, kmb[:], ident_f[:]), inc=sq, wait=[(sv, sv.n)])
            P.emit("vector", lambda e: e.reduce_max(out=nm16b[:], in_=pA_s, axis=AX.X),
                   inc=sv, wait=[(sq, tks), (sv, sv.n)])
            vmm = P.emit("vector", lambda e: e.tensor_tensor(out=nm16[:], in0=nm16[:], in1=nm16b[:], op=AO.mult),
                         inc=sv, wait=[(sv, sv.n)])
            aM0 = P.emit("scalar", lambda e: e.activation(out=nm16b[:], in_=nm16[:], func=AF.Ln),
                         inc=sa, wait=[(sv, vmm)])
            aM = P.emit("scalar", lambda e: e.activation(out=nm16b[:], in_=nm16b[:], func=AF.Exp, scale=0.5),
                        inc=sa, wait=[(sa, aM0)])
            vM = P.emit("vector", lambda e: e.tensor_scalar_mul(out=nm16[:], in0=nm16b[:], scalar1=-0.125),
                        inc=sv, wait=[(sa, aM)])
            sM = P.emit("gpsimd", lambda e: e.dma_start(
                out=nm_scr[:].rearrange("(h o) -> h o", o=1), in_=nm16[:]),
                inc=st, inc_by=16, wait=[(sv, vM)])
            lM = P.emit("sync", lambda e: e.dma_start(
                out=negm_bc[:], in_=nm_scr[:].unsqueeze(0).broadcast_to([128, H])),
                inc=ld, inc_by=16, wait=[(st, sM), (sa, sa.n)])
            vnp = P.emit("vector", lambda e: e.tensor_tensor(
                out=negmp[:], in0=negm_bc[:].rearrange("p (h two) -> p h two", two=2)[:, :, 0],
                in1=negm_bc[:].rearrange("p (h two) -> p h two", two=2)[:, :, 1], op=AO.min),
                inc=sv, wait=[(ld, lM), (sv, sv.n)])
            NM_WAIT[(s, g)] = (lM, vnp)

            nkc = 4 * (g + 1)
            last_exp_parity = [0, 0]
            last_norm_pv = [sv.n, sv.n]
            chunk_list = [(hp, kc) for hp in range(8) for kc in range(nkc)]

            def emit_scores(idx, extra_wait=()):
                hp, kc = chunk_list[idx]
                buf = kc % 2
                sPair = (pA_q, pA_k)[buf]
                w0 = [(sa, last_exp_parity[buf])] + list(extra_wait)
                P.emit("tensor", lambda e, sPair=sPair, hp=hp, kc=kc: e.matmul(
                    sPair[:, 0:512], kT_all[0:64, hp, 128 * kc:128 * (kc + 1)],
                    qT_g[0:64, hp, :], start=True, stop=True, tile_position=(0, 0)),
                    inc=sq, wait=w0)
                P.emit("tensor", lambda e, sPair=sPair, hp=hp, kc=kc: e.matmul(
                    sPair[:, 512:1024], kT_all[64:128, hp, 128 * kc:128 * (kc + 1)],
                    qT_g[64:128, hp, :], start=True, stop=True, tile_position=(64, 0)),
                    inc=sq)
                return sq.n

            PEND_TAIL = []

            def emit_pair_tail(hp, pvset, pvtA, pvtB, pv_q):
                hA = 2 * hp
                if pvset == 0:
                    aR1 = P.emit("scalar", lambda e: e.activation(out=rs2_s[:], in_=pA_v[64:65, 0:1024], func=AF.Ln),
                                 inc=sa, wait=[(sq, pv_q), (st, st.n)])
                    aR4 = P.emit("scalar", lambda e: e.activation(out=rs2_s[:], in_=rs2_s[:], func=AF.Exp, scale=-1.0),
                                 inc=sa, wait=[(sa, aR1)])
                    s2_ = P.emit("gpsimd", lambda e, hA=hA: e.dma_start(
                        out=rs_scr[hA:hA + 2, :].rearrange("a n -> (a n)").unsqueeze(0), in_=rs2_s[:]),
                        inc=st, inc_by=16, wait=[(sa, aR4)])
                else:
                    aR1 = P.emit("scalar", lambda e, pvtA=pvtA: e.activation(out=rsA, in_=pvtA[64:65, :], func=AF.Ln),
                                 inc=sa, wait=[(sq, pv_q), (st, st.n)])
                    aR2 = P.emit("scalar", lambda e: e.activation(out=rsA, in_=rsA, func=AF.Exp, scale=-1.0),
                                 inc=sa, wait=[(sa, aR1)])
                    aR3 = P.emit("scalar", lambda e, pvtB=pvtB: e.activation(out=rsB, in_=pvtB[64:65, :], func=AF.Ln),
                                 inc=sa, wait=[(sa, aR2)])
                    aR4 = P.emit("scalar", lambda e: e.activation(out=rsB, in_=rsB, func=AF.Exp, scale=-1.0),
                                 inc=sa, wait=[(sa, aR3)])
                    P.emit("gpsimd", lambda e, hA=hA: e.dma_start(out=rs_scr[hA:hA + 1, :], in_=rsA),
                           inc=st, inc_by=16, wait=[(sa, aR4)])
                    s2_ = P.emit("gpsimd", lambda e, hA=hA: e.dma_start(out=rs_scr[hA + 1:hA + 2, :], in_=rsB),
                                 inc=st, inc_by=16)
                lrb = P.emit("sync", lambda e, hA=hA: e.dma_start(
                    out=rb_s[:], in_=rs_scr[hA:hA + 2, :].unsqueeze(1).broadcast_to([2, 64, 512])),
                    inc=ld, inc_by=16, wait=[(st, s2_), RB_GUARD[0]])
                P.emit("vector", lambda e, hp=hp, pvtA=pvtA: e.tensor_tensor(
                    out=attnT_g[0:64, hp, :], in0=pvtA[0:64, :], in1=rb_s[0:64, :], op=AO.mult),
                    inc=sv, wait=[(ld, lrb), (sq, sq.n)])
                vn = P.emit("vector", lambda e, hp=hp, pvtB=pvtB: e.tensor_tensor(
                    out=attnT_g[64:128, hp, :], in0=pvtB[0:64, :], in1=rb_s[64:128, :], op=AO.mult),
                    inc=sv)
                RB_GUARD[0] = (sv, vn)
                last_norm_pv[pvset] = vn
                if "attnT" in taps:
                    vcp = P.emit("vector", lambda e, hp=hp: e.tensor_copy(out=zu[:, 0:512], in_=attnT_g[:, hp, :]),
                                 inc=sv, wait=[(sv, vn)])
                    P.emit("gpsimd", lambda e, s=s, g=g, hp=hp: e.dma_start(
                        out=tap_t["attnT"][s, g, 128 * hp:128 * (hp + 1), :], in_=zu[:, 0:512]),
                        inc=st, inc_by=16, wait=[(sv, vcp)])

            FD_HOOKS = []
            if g + 1 < NG:
                L = len(chunk_list)
                for i in range(3):
                    FD_HOOKS.append((max(1, (L * (i + 1)) // 5), 4 * (g + 1) + i, i))

            mq_pend = emit_scores(0, extra_wait=[(sv, sv.n), (sq, sq.n), (sa, sa.n)])
            for idx, (hp, kc) in enumerate(chunk_list):
                while FD_HOOKS and idx == FD_HOOKS[0][0]:
                    _, t2, sl = FD_HOOKS.pop(0)
                    front_dve(t2, sl)
                hA, hB = 2 * hp, 2 * hp + 1
                pvset = hp % 2
                pvtA = pA_v[0:65, 0:512] if pvset == 0 else pA_o[0:65, 0:512]
                pvtB = pA_v[0:65, 512:1024] if pvset == 0 else pA_oB
                buf = kc % 2
                sPair = (pA_q, pA_k)[buf]
                PPd = (PP0, PP1)[buf]
                mq = mq_pend
                diag = (kc >= 4 * g)
                wexp = [(sq, mq), (ld, NM_WAIT[(s, g)][0]), (sv, NM_WAIT[(s, g)][1])]
                aB = P.emit("scalar", lambda e, sPair=sPair, hp=hp, dst=PPd: e.activation(
                    out=dst[:], in_=sPair[:, 0:1024], func=AF.Exp, scale=0.125, bias=negmp[:, hp:hp + 1]),
                    inc=sa, wait=wexp)
                aA = aB
                last_exp_parity[buf] = aB
                if idx + 1 < len(chunk_list):
                    mq_pend = emit_scores(idx + 1)
                pw = [(sa, aB)]
                if diag:
                    j = kc - 4 * g
                    vmB = P.emit("vector", lambda e, j=j, PPd=PPd: e.tensor_tensor(
                        out=PPd[:].rearrange("p (two n) -> p two n", n=512),
                        in0=PPd[:].rearrange("p (two n) -> p two n", n=512),
                        in1=dmask_b[:, 512 * j:512 * (j + 1)].unsqueeze(1).transpose([0, 2, 1]).broadcast_to([128, 2, 512])
                        if False else dmask_b[:, 512 * j:512 * (j + 1)].unsqueeze(1).broadcast_to([128, 2, 512]),
                        op=AO.mult),
                        inc=sv, wait=[(sa, aB)])
                    pw = [(sv, vmB)]
                if kc == 0:
                    pw = pw + [(sv, last_norm_pv[pvset])]
                P.emit("tensor", lambda e, kc=kc, hA=hA, PPd=PPd, pvtA=pvtA,
                       st_=(kc == 0), sp_=(kc == nkc - 1): e.matmul(
                    pvtA, Vp[:, kc, 65 * hA:65 * hA + 65], PPd[:, 0:512],
                    start=st_, stop=sp_),
                    inc=(sq if kc == nkc - 1 else None), wait=pw)
                P.emit("tensor", lambda e, kc=kc, hB=hB, PPd=PPd, pvtB=pvtB,
                       st_=(kc == 0), sp_=(kc == nkc - 1): e.matmul(
                    pvtB, Vp[:, kc, 65 * hB:65 * hB + 65], PPd[:, 512:1024],
                    start=st_, stop=sp_),
                    inc=(sq if kc == nkc - 1 else None))
                if kc == nkc - 1:
                    PEND_TAIL.append((hp, pvset, pvtA, pvtB, sq.n))
                if PEND_TAIL and (kc == min(2, nkc - 1) or idx == len(chunk_list) - 1):
                    if PEND_TAIL[0][0] != hp or idx == len(chunk_list) - 1:
                        thp, tset, tA, tB, tpv = PEND_TAIL.pop(0)
                        emit_pair_tail(thp, tset, tA, tB, tpv)
            while PEND_TAIL:
                thp, tset, tA, tB, tpv = PEND_TAIL.pop(0)
                emit_pair_tail(thp, tset, tA, tB, tpv)
            v_attn = sv.n

            for tt in range(4):
                t = 4 * g + tt
                lxr = P.emit("sync", lambda e, s=s, t=t: e.dma_start(
                    out=x_t[:], in_=xs[s, 128 * t:128 * (t + 1), :]),
                    inc=ld, inc_by=16, wait=[(sv, sv.n), (sa, sa.n)])
                for o2 in range(2):
                    for hp in range(8):
                        P.emit("tensor", lambda e, o2=o2, hp=hp, tt=tt: e.matmul(
                            pA_q[:, 512 * o2:512 * (o2 + 1)],
                            attnT_g[:, hp, 128 * tt:128 * (tt + 1)],
                            WoT[:, hp, 512 * o2:512 * (o2 + 1)],
                            start=(hp == 0), stop=(hp == 7)),
                            inc=(sq if hp == 7 else None),
                            wait=([(sv, v_attn), (sa, sa.n), (sv, sv.n)] if (hp == 0 and o2 == 0) else []))
                mo = sq.n
                ob = (zu, z2)[tt % 2]
                w_ev = [(st, STH[-2])] if len(STH) >= 2 else [(st, st.n)]
                P.emit("vector", lambda e, ob=ob: e.tensor_tensor(out=ob[:], in0=pA_q[:, :], in1=bo_s[:], op=AO.add),
                       inc=sv, wait=[(sq, mo)] + w_ev)
                vx1 = P.emit("vector", lambda e, ob=ob: e.tensor_tensor(out=ob[:], in0=ob[:], in1=x_t[:], op=AO.add),
                             inc=sv, wait=[(ld, lxr)])
                P.emit("gpsimd", lambda e, s=s, t=t, ob=ob: e.dma_start(
                    out=xmid[s, 128 * t:128 * (t + 1), :], in_=ob[:]),
                    inc=st, inc_by=16, wait=[(sv, vx1)])
                STH.append(st.n)
                if "x1" in taps:
                    P.emit("gpsimd", lambda e, s=s, t=t: e.dma_start(
                        out=tap_t["x1"][s, 128 * t:128 * (t + 1), :], in_=zu[:]),
                        inc=st, inc_by=16, wait=[(sv, vx1)])

    for s in range(NSEQ):
        phase_a_seq(s)
    pa.close()
    watt.close()
    barrierA = [(sv, sv.n), (sq, sq.n), (st, st.n), (ld, ld.n), (sa, sa.n)]

    if stop_after == "phaseA":
        P.emit("gpsimd", lambda e: e.nop(), wait=barrierA)
        finalize(nc, P)
        return nc, stack

    pb = contextlib.ExitStack()
    qsb = lambda name, shape, dtype: pb.enter_context(nc.sbuf_tensor(name, list(shape), dtype))
    wgqT = qsb("wgqT", (128, 8, I_FFN), BF16)
    wuqT = qsb("wuqT", (128, 8, I_FFN), BF16)
    wdqT = qsb("wdqT", (128, 22, D), BF16)
    xb_t = qsb("xb_t", (128, D), F32)
    xb_t2 = qsb("xb_t2", (128, D), F32)
    h2_f = qsb("h2_f", (128, D), F32)
    mu_all = qsb("mu_all", (128, NSEQ * NT), F32)
    rstd_all = qsb("rstd_all", (128, NSEQ * NT), F32)
    h2qT = qsb("h2qT", (128, 8, 128), BF16)
    h1qT = qsb("h1qT", (128, 8, 128), BF16)
    gsil = qsb("gsil", (128, 1024), F32)
    h1_s = qsb("h1_s", (128, 1024), F32)
    ob_t = qsb("ob_t", (128, D), F32)
    ob_t2 = qsb("ob_t2", (128, D), F32)
    b_sq = qsb("b_sq", (128, 2 * D), F32)
    b_s1 = qsb("b_s1", (128, 2), F32); b_s2 = qsb("b_s2", (128, 2), F32)
    b_mu = qsb("b_mu", (128, 1), F32); b_rstd = qsb("b_rstd", (128, 1), F32)
    q_out = qsb("q_out", (128, D), BF16)
    q_ax2 = qsb("q_ax2", (128, D), F32)
    q_t2 = qsb("q_t2", (128, D), F32)
    q_y = qsb("q_y", (128, D), F32)
    q_y2 = qsb("q_y2", (128, D), F32)
    q_sg = qsb("q_sg", (128, D), BF16)
    q_m = qsb("q_m", (128, D), BF16)
    q_bm = qsb("q_bm", (128, D // 16), F32)
    q_sc = qsb("q_sc", (128, D // 16), F32)
    q_rs = qsb("q_rs", (128, D // 16), F32)
    q_s2 = qsb("q_s2", (128, D // 16), BF16)
    b_scr = (q_ax2, q_t2, q_y, q_y2, q_sg, q_m, q_bm, q_sc, q_rs, q_s2)

    pB_g = pb.enter_context(nc.psum_tensor("pB_g", [128, 1024], F32))
    pB_u = pb.enter_context(nc.psum_tensor("pB_u", [128, 1024], F32))
    pB_t = pb.enter_context(nc.psum_tensor("pB_t", [128, 512], BF16))
    pB_d = pb.enter_context(nc.psum_tensor("pB_d", [128, 1024], F32))

    for eng in ENGINES:
        P.emit(eng, lambda e: e.nop(), wait=barrierA)
    for dst, off, ch, nrows in ((wgqT, WGT_OFF, 8, I_FFN), (wuqT, WUT_OFF, 8, I_FFN), (wdqT, WDT_OFF, 22, D)):
        P.emit("sync", lambda e, dst=dst, off=off, ch=ch, nr=nrows: e.dma_start(
            out=dst[:], in_=wq_scr[off:off + 128 * ch * nr].rearrange("(p c n) -> p c n", p=128, c=ch)),
            inc=ld, inc_by=16)
    ld_wq = ld.n

    scr_guard = [None]
    OD_SLICES = [(0, 1024), (1024, 1024), (2048, 768)]

    def phase_b_prologue():
        NBT = 2
        for it in range(NSEQ * NT // NBT):
            ti0 = it * NBT
            s0, t0_ = ti0 // NT, ti0 % NT
            lx = P.emit("sync", lambda e, s0=s0, t0_=t0_: e.dma_start(
                out=b_sq[:].rearrange("p (b n) -> p b n", n=D),
                in_=xmid[s0, 128 * t0_:128 * (t0_ + NBT), :].rearrange("(b q) n -> q b n", b=NBT)),
                inc=ld, inc_by=16, wait=([(ld, ld_wq)] if it == 0 else []) + [(sv, sv.n), (sa, sa.n)])
            vmu = P.emit("vector", lambda e, ti0=ti0: e.reduce_sum(
                out=mu_all[:, ti0:ti0 + NBT], in_=b_sq[:].rearrange("p (b n) -> p b n", n=D), axis=AX.X),
                inc=sv, wait=[(ld, lx)])
            a0 = P.emit("scalar", lambda e: e.activation(out=b_sq[:], in_=b_sq[:], func=AF.Square),
                        inc=sa, wait=[(ld, lx), (sv, vmu)])
            P.emit("vector", lambda e: e.reduce_sum(
                out=b_s2[:, 0:NBT], in_=b_sq[:].rearrange("p (b n) -> p b n", n=D), axis=AX.X),
                inc=sv, wait=[(sa, a0), (sv, sv.n)])
            P.emit("vector", lambda e, ti0=ti0: e.tensor_scalar_mul(
                out=mu_all[:, ti0:ti0 + NBT], in0=mu_all[:, ti0:ti0 + NBT], scalar1=1.0 / D),
                inc=sv, wait=[(sv, sv.n)])
            P.emit("vector", lambda e: e.tensor_scalar_mul(out=b_s2[:, 0:NBT], in0=b_s2[:, 0:NBT], scalar1=1.0 / D),
                   inc=sv, wait=[(sv, sv.n)])
            P.emit("vector", lambda e, ti0=ti0: e.tensor_tensor(
                out=b_s1[:, 0:NBT], in0=mu_all[:, ti0:ti0 + NBT], in1=mu_all[:, ti0:ti0 + NBT], op=AO.mult),
                inc=sv, wait=[(sv, sv.n)])
            P.emit("vector", lambda e: e.tensor_tensor(out=b_s1[:, 0:NBT], in0=b_s1[:, 0:NBT], in1=b_s2[:, 0:NBT], op=AO.subtract),
                   inc=sv, wait=[(sv, sv.n)])
            P.emit("vector", lambda e: e.tensor_scalar(out=b_s1[:, 0:NBT], in0=b_s1[:, 0:NBT], scalar1=-1.0, scalar2=1e-5,
                                                       op0=AO.mult, op1=AO.add), inc=sv, wait=[(sv, sv.n)])
            vr3 = P.emit("vector", lambda e: e.reciprocal(out=b_s2[:, 0:NBT], in_=b_s1[:, 0:NBT]), inc=sv, wait=[(sv, sv.n)])
            P.emit("scalar", lambda e, ti0=ti0: e.activation(out=rstd_all[:, ti0:ti0 + NBT], in_=b_s2[:, 0:NBT], func=AF.Sqrt),
                   inc=sa, wait=[(sv, vr3)])

    VO_H = []
    SO_H = []
    QG_X = [(sq, 0)]

    def phase_b_tile(s, t):
        ti = s * NT + t
        xb = (xb_t, xb_t2)[ti % 2]
        w_lx = [(sv, VO_H[-2])] if len(VO_H) >= 2 else [(sv, sv.n), (sa, sa.n)]
        lx = P.emit("sync", lambda e, s=s, t=t, xb=xb: e.dma_start(
            out=xb[:], in_=xmid[s, 128 * t:128 * (t + 1), :]),
            inc=ld, inc_by=16, wait=w_lx)
        if "xb" in taps:
            P.emit("gpsimd", lambda e, s=s, t=t, xb=xb: e.dma_start(
                out=tap_t["xb"][s, 128 * t:128 * (t + 1), :], in_=xb[:]),
                inc=st, inc_by=16, wait=[(ld, lx)])
        ah = P.emit("vector", lambda e, ti=ti, xb=xb: e.tensor_scalar(
            out=h2_f[:], in0=xb[:], scalar1=mu_all[:, ti:ti + 1], scalar2=rstd_all[:, ti:ti + 1],
            op0=AO.subtract, op1=AO.mult),
            inc=sv, wait=[(ld, lx)])
        _, vq2p = emit_quant(b_scr, h2_f[:], D, q_out[:], h2_f,
                             wv=[(sv, ah)], wa=[(sv, ah)], w_final=[QG_X[0]],
                             scratch_guard=scr_guard[0], split=PHASEB_SPLIT, rscale_act=False)
        scr_guard[0] = vq2p
        if "h2q" in taps:
            vcp = P.emit("vector", lambda e: e.tensor_copy(out=q_t2[:], in_=q_out[:]), inc=sv, wait=[vq2p])
            P.emit("gpsimd", lambda e, s=s, t=t: e.dma_start(
                out=tap_t["h2q"][s, 128 * t:128 * (t + 1), :], in_=q_t2[:]),
                inc=st, inc_by=16, wait=[(sv, vcp)])
        for cq in range(2):
            for k in range(4):
                c = 4 * cq + k
                P.emit("tensor", lambda e, c=c, k=k: e.transpose(
                    pB_t[:, 128 * k:128 * (k + 1)], q_out[:, 128 * c:128 * (c + 1)], ident_b[:]),
                    inc=(sq if k == 3 else None),
                    wait=([(sv, sv.n), (sq, sq.n)] if k == 0 else []))
            tq = sq.n
            P.emit("vector", lambda e, cq=cq: e.tensor_copy(
                out=h2qT[:, 4 * cq:4 * cq + 4, :], in_=pB_t[:].rearrange("p (k n) -> p k n", n=128)),
                inc=sv, wait=[(sq, tq)])
        v_h2qT = sv.n
        QG_X[0] = (sq, sq.n)

        first_dn = True
        def emit_gu(si, wv_extra):
            o0, ow = OD_SLICES[si]
            mw = [(sv, v_h2qT)] + wv_extra
            for pg, pt in ((pB_g, wgqT), (pB_u, wuqT)):
                for hh in range(0, ow, 512):
                    hw = min(512, ow - hh)
                    for c in range(8):
                        P.emit("tensor", lambda e, c=c, o0=o0, hh=hh, hw=hw, pg=pg, pt=pt: e.matmul(
                            pg[:, hh:hh + hw], h2qT[:, c, :], pt[:, c, o0 + hh:o0 + hh + hw],
                            start=(c == 0), stop=(c == 7)),
                            inc=(sq if c == 7 else None), wait=(mw if c == 0 else []))
                        mw = []
            return sq.n
        mgu = emit_gu(0, [(sa, sa.n), (sv, sv.n)])
        for si, (o0, ow) in enumerate(OD_SLICES):
            ag = P.emit("scalar", lambda e, ow=ow: e.activation(out=gsil[:, 0:ow], in_=pB_g[:, 0:ow], func=AF.Silu),
                        inc=sa, wait=[(sq, mgu)])
            vh1 = P.emit("vector", lambda e, ow=ow: e.tensor_tensor(
                out=h1_s[:, 0:ow], in0=gsil[:, 0:ow], in1=pB_u[:, 0:ow], op=AO.mult),
                inc=sv, wait=[(sa, ag)])
            if si + 1 < len(OD_SLICES):
                mgu = emit_gu(si + 1, [(sa, ag), (sv, vh1)])
            if "h1" in taps:
                P.emit("gpsimd", lambda e, s=s, t=t, o0=o0, ow=ow: e.dma_start(
                    out=tap_t["h1"][s, 128 * t:128 * (t + 1), o0:o0 + ow], in_=h1_s[:, 0:ow]),
                    inc=st, inc_by=16, wait=[(sv, vh1)])
            _, vq1p = emit_quant(b_scr, h1_s[:, 0:ow], ow, q_out[:, 0:ow], h1_s,
                                 wv=[(sv, vh1)], wa=[(sv, vh1)], w_final=[QG_X[0]],
                                 scratch_guard=scr_guard[0], split=PHASEB_SPLIT, rscale_act=False)
            scr_guard[0] = vq1p
            nch = ow // 128
            for cq in range((nch + 3) // 4):
                ks = [k for k in range(4) if 4 * cq + k < nch]
                for k in ks:
                    c = 4 * cq + k
                    P.emit("tensor", lambda e, c=c, k=k: e.transpose(
                        pB_t[:, 128 * k:128 * (k + 1)], q_out[:, 128 * c:128 * (c + 1)], ident_b[:]),
                        inc=(sq if k == ks[-1] else None),
                        wait=(([(sv, sv.n), (sq, sq.n)] + ([vq1p] if cq == 0 else [])) if k == 0 else []))
                tq1 = sq.n
                P.emit("vector", lambda e, cq=cq, nk=len(ks): e.tensor_copy(
                    out=h1qT[:, 4 * cq:4 * cq + nk, :], in_=pB_t[:, 0:128 * nk].rearrange("p (k n) -> p k n", n=128)),
                    inc=sv, wait=[(sq, tq1)])
            vev = sv.n
            QG_X[0] = (sq, sq.n)
            for k in range(nch):
                cglob = o0 // 128 + k
                for o2 in range(2):
                    P.emit("tensor", lambda e, k=k, cglob=cglob, o2=o2,
                           st_=(first_dn and k == 0),
                           sp_=(si == len(OD_SLICES) - 1 and k == nch - 1): e.matmul(
                        pB_d[:, 512 * o2:512 * (o2 + 1)], h1qT[:, k, :],
                        wdqT[:, cglob, 512 * o2:512 * (o2 + 1)], start=st_, stop=sp_),
                        inc=(sq if (k == nch - 1 and o2 == 1) else None),
                        wait=([(sv, vev)] if (k == 0 and o2 == 0) else []))
            first_dn = False
        m_dn = sq.n
        ob = (ob_t, ob_t2)[ti % 2]
        w_vo = [(so, SO_H[-2])] if len(SO_H) >= 2 else [(so, so.n)]
        vo = P.emit("vector", lambda e, xb=(xb_t, xb_t2)[ti % 2], ob=ob: e.tensor_tensor(
            out=ob[:], in0=pB_d[:, :], in1=xb[:], op=AO.add),
            inc=sv, wait=[(sq, m_dn)] + w_vo)
        VO_H.append(vo)
        P.emit("gpsimd", lambda e, s=s, t=t, ob=ob: e.dma_start(
            out=out[s, 128 * t:128 * (t + 1), :], in_=ob[:]),
            inc=so, inc_by=16, wait=[(sv, vo)])
        SO_H.append(so.n)

    phase_b_prologue()
    for s in range(NSEQ):
        for t in range(NT):
            phase_b_tile(s, t)
    pb.close()

    P.emit("gpsimd", lambda e: e.nop(), wait=[(so, so.n), (st, st.n)])
    finalize(nc, P)
    return nc, stack


def finalize(nc, P):
    with nc.Block() as block:
        @block.sync
        def _(e):
            for fn in P.ops["sync"]:
                fn(e)

        @block.vector
        def _(e):
            for fn in P.ops["vector"]:
                fn(e)

        @block.scalar
        def _(e):
            for fn in P.ops["scalar"]:
                fn(e)

        @block.tensor
        def _(e):
            for fn in P.ops["tensor"]:
                fn(e)

        @block.gpsimd
        def _(e):
            for fn in P.ops["gpsimd"]:
                fn(e)


def make_in_maps(inputs, consts=None, n_cores=N_CORES):
    if consts is None:
        consts = host_consts()
    lay = host_inputs_layout(inputs)
    x = np.ascontiguousarray(np.asarray(inputs["L0_attn_x"], dtype=np.float32))
    base = {
        "Wq": np.ascontiguousarray(np.asarray(inputs["L0_attn_Wq"], dtype=np.float32)),
        "Wk": np.ascontiguousarray(np.asarray(inputs["L0_attn_Wk"], dtype=np.float32)),
        "Wv": np.ascontiguousarray(np.asarray(inputs["L0_attn_Wv"], dtype=np.float32)),
        "Wo": np.ascontiguousarray(np.asarray(inputs["L0_attn_Wo"], dtype=np.float32)),
        "wg": np.ascontiguousarray(np.asarray(inputs["L0_ffn_w_gate"], dtype=np.float32)),
        "wu": np.ascontiguousarray(np.asarray(inputs["L0_ffn_w_up"], dtype=np.float32)),
        "wd": np.ascontiguousarray(np.asarray(inputs["L0_ffn_w_down"], dtype=np.float32)),
        "c_cos": consts["c_cos"], "c_sinp": consts["c_sinp"],
        "c_ident": consts["c_ident"], "c_dmask": consts["c_dmask"],
        "gq": lay["gq"], "gqp": lay["gqp"], "bq": lay["bq"], "bqp": lay["bqp"],
        "gk": lay["gk"], "gkp": lay["gkp"], "bk": lay["bk"], "bkp": lay["bkp"],
        "bo_rep": lay["bo_rep"],
    }
    in_maps = []
    for c in range(n_cores):
        m = dict(base)
        m["xs"] = np.ascontiguousarray(x[NSEQ * c:NSEQ * (c + 1)])
        in_maps.append(m)
    return in_maps


def kernel(**inputs):
    nc, stack = build_program()
    in_maps = make_in_maps(inputs)
    res = run_bass_kernel_spmd(nc, in_maps, core_ids=list(range(N_CORES)))
    stack.close()
    return np.concatenate([np.asarray(r["out"]) for r in res.results], axis=0).astype(np.float32)
